# version 20
# baseline (speedup 1.0000x reference)
import numpy as np
import concourse.bass as bass
import concourse.mybir as mybir
from concourse.bass_utils import run_bass_kernel_spmd
from concourse.bass import broadcast_tensor_aps

F32 = mybir.dt.float32
BF16 = mybir.dt.bfloat16
AF = mybir.ActivationFunctionType
ALU = mybir.AluOpType
AX = mybir.AxisListType

D = 2048
T = 2048
NH = 16
NKV = 4
KV = 64
LAT = 1024
HF = 8192
GROUP = 64
LN_EPS = 1e-5
RMS_EPS = float(np.finfo(np.float32).eps)
QTHR = 0.5 + 2.0 ** -10
NCORES = 8


def q_chunk(h):
    return (h % 4) + 4 * (h // 8)


def q_slot(h):
    return (h // 4) % 2


class Buf:
    __slots__ = ("lw", "rd", "name", "dsem")

    def __init__(self, name=""):
        self.lw = None
        self.rd = []
        self.name = name
        self.dsem = None


class Arena:

    def __init__(self, tensor, nbytes):
        self.t = tensor
        self.free = [(0, nbytes)]
        self.live = {}
        self.ghost = []

    def alloc(self, name, nbytes, dtype, width=None, side="low"):
        nbytes = (nbytes + 63) & ~63
        idxs = range(len(self.free)) if side == "low" else \
            range(len(self.free) - 1, -1, -1)
        for i in idxs:
            off, sz = self.free[i]
            if sz >= nbytes:
                if sz == nbytes:
                    self.free.pop(i)
                elif side == "low":
                    self.free[i] = (off + nbytes, sz - nbytes)
                else:
                    self.free[i] = (off, sz - nbytes)
                    off = off + sz - nbytes
                buf = Buf(name)
                keep = []
                for (go, gs, gb) in self.ghost:
                    if go < off + nbytes and off < go + gs:
                        if gb.lw is not None:
                            buf.rd.append(gb.lw)
                        buf.rd.extend(gb.rd)
                    else:
                        keep.append((go, gs, gb))
                self.ghost = keep
                self.live[off] = (nbytes, buf)
                ap = self.t[:, off:off + nbytes].bitcast(dtype)
                if width is not None:
                    ap = ap[:, 0:width]
                return ap, buf, off
        raise MemoryError(f"SBUF arena: cannot alloc {nbytes} for {name}; "
                          f"free={self.free}")

    def release(self, off):
        sz, buf = self.live.pop(off)
        self.ghost.append((off, sz, buf))
        self.free.append((off, sz))
        self.free.sort()
        out = []
        for seg in self.free:
            if out and out[-1][0] + out[-1][1] == seg[0]:
                out[-1] = (out[-1][0], out[-1][1] + seg[1])
            else:
                out.append(list(seg) if False else (seg[0], seg[1]))
        merged = []
        for seg in out:
            if merged and merged[-1][0] + merged[-1][1] == seg[0]:
                merged[-1] = (merged[-1][0], merged[-1][1] + seg[1])
            else:
                merged.append(seg)
        self.free = merged


class Prog:
    def __init__(self, nc):
        self.nc = nc
        self.eng = {
            "pe": nc.tensor, "sc": nc.scalar, "ve": nc.vector,
            "gp": nc.gpsimd, "sp": nc.sync,
        }
        self.sem = {}
        self.cnt = {}
        self.waited = {e: {} for e in self.eng}
        for e in ("pe", "sc", "ve", "gp"):
            self.new_sem(e)

    def new_sem(self, name):
        h = self.nc.alloc_semaphore(name)
        self.sem[name] = h
        self.cnt[name] = 0
        return name

    def make_dma_pool(self, n):
        self.dma_pool = [self.new_sem(f"dpool{i}") for i in range(n)]
        self.dma_pool_next = 0

    def clear_sems_and_barrier(self):
        for h in self.sem.values():
            self.nc.gpsimd.sem_clear(h)
        self.nc.all_engine_barrier()

    def wait(self, ename, tok):
        if tok is None:
            return
        s, v = tok
        w = self.waited[ename]
        if w.get(s, 0) >= v:
            return
        self.eng[ename].wait_ge(self.sem[s], v)
        w[s] = v

    def emit(self, ename, fn, reads=(), writes=(), sem=None, dinc=1):
        for b in reads:
            if b.lw is not None:
                self.wait(ename, b.lw)
        for b in writes:
            if b.lw is not None and b.lw[0] != ename:
                self.wait(ename, b.lw)
            for t in b.rd:
                if t[0] != ename:
                    self.wait(ename, t)
        ins = fn()
        sname = sem if sem is not None else ename
        self.cnt[sname] += dinc
        ins.then_inc(self.sem[sname], dinc)
        tok = (sname, self.cnt[sname])
        for b in writes:
            b.lw = tok
            b.rd = []
        for b in reads:
            b.rd.append(tok)
        return tok

    def dma(self, qname, out, in_, reads=(), writes=(), sem=None, **kw):
        assert len(writes) == 1
        w = writes[0]
        if w.dsem is None:
            assert self.dma_pool_next < len(self.dma_pool), "dma sem pool exhausted"
            w.dsem = self.dma_pool[self.dma_pool_next]
            self.dma_pool_next += 1
        return self.emit(
            qname, lambda: self.eng[qname].dma_start(out=out, in_=in_, **kw),
            reads=reads, writes=writes, sem=w.dsem, dinc=16)


def ceil_div(a, b):
    return -(-a // b)


def build_program(debug=False):
    nc = bass.Bass("TRN2", target_bir_lowering=False)
    p = Prog(nc)

    x_d = nc.dram_tensor("x", [T, D], F32, kind="ExternalInput").ap()
    wq_d = nc.dram_tensor("Wq", [D, D], F32, kind="ExternalInput").ap()
    bq_d = nc.dram_tensor("bq", [D], F32, kind="ExternalInput").ap()
    wl_d = nc.dram_tensor("Wl", [D, LAT], F32, kind="ExternalInput").ap()
    bl_d = nc.dram_tensor("bl", [LAT], F32, kind="ExternalInput").ap()
    wk_d = nc.dram_tensor("Wk", [LAT, NKV * KV], F32, kind="ExternalInput").ap()
    bk_d = nc.dram_tensor("bk", [NKV * KV], F32, kind="ExternalInput").ap()
    wv_d = nc.dram_tensor("Wv", [LAT, NKV * KV], F32, kind="ExternalInput").ap()
    bv_d = nc.dram_tensor("bv", [NKV * KV], F32, kind="ExternalInput").ap()
    wq2kv_d = nc.dram_tensor("Wq2kv", [128, KV], F32, kind="ExternalInput").ap()
    wkv2h_d = nc.dram_tensor("Wkv2h", [KV, 128], F32, kind="ExternalInput").ap()
    bkv2h_d = nc.dram_tensor("bkv2h", [128], F32, kind="ExternalInput").ap()
    wo_d = nc.dram_tensor("Wo", [D, D], F32, kind="ExternalInput").ap()
    bo_d = nc.dram_tensor("bo", [D], F32, kind="ExternalInput").ap()
    wgu_sh_d = nc.dram_tensor("Wgu_sh", [2 * HF // 8, D], F32, kind="ExternalInput").ap()
    wp_sh_d = nc.dram_tensor("Wp_sh", [D, HF // 8], F32, kind="ExternalInput").ap()
    out_d = nc.dram_tensor("out", [T, D], F32, kind="ExternalOutput").ap()

    wqp_blk = nc.dram_tensor("wqp_blk", [8, 128, 16, 128], BF16, kind="Internal").ap()
    wq_bf = nc.dram_tensor("wq_bf", [D, D], BF16, kind="Internal").ap()
    wl_blk = nc.dram_tensor("wl_blk", [8, 128, 16, 128], BF16, kind="Internal").ap()
    wk_blk = nc.dram_tensor("wk_blk", [2, 128, 8, 128], BF16, kind="Internal").ap()
    wv_nat = nc.dram_tensor("wv_nat", [LAT, NKV * KV], BF16, kind="Internal").ap()
    wpr_nat = nc.dram_tensor("wpr_nat", [LAT, D], BF16, kind="Internal").ap()
    wgu_q = nc.dram_tensor("wgu_q", [2 * HF, D], BF16, kind="Internal", addr_space="Shared").ap()
    wguq_sh = nc.dram_tensor("wguq_sh", [2 * HF // 8, D], BF16, kind="Internal").ap()
    wpqt_sh = nc.dram_tensor("wpqt_sh", [HF // 8, D], BF16, kind="Internal").ap()
    wpt_full = nc.dram_tensor("wpt_full", [HF, D], BF16, kind="Internal",
                              addr_space="Shared").ap()
    s2_dram = nc.dram_tensor("s2_dram", [T, D], F32, kind="Internal").ap()
    ln3t_dram = nc.dram_tensor("ln3t_dram", [D, T], BF16, kind="Internal").ap()
    hact_dram = nc.dram_tensor("hact_dram", [HF, T], BF16, kind="Internal").ap()
    ftok_dram = nc.dram_tensor("ftok_dram", [T, D], F32, kind="Internal").ap()

    dbg = {}
    if debug:
        for name, shape, dt in [
            ("dbg_ln1t", [D, T], BF16), ("dbg_q2t", [LAT, T], BF16),
            ("dbg_latt", [LAT, T], BF16), ("dbg_kt", [256, T], BF16),
            ("dbg_v", [T, NKV * KV], BF16), ("dbg_ctxt", [LAT, T], BF16),
            ("dbg_s2", [T, D], F32), ("dbg_ln3t", [D, T], BF16),
            ("dbg_wguq", [2 * HF, D], BF16), ("dbg_wpq", [D, HF], BF16),
            ("dbg_hact", [HF, T], BF16), ("dbg_ffn", [T, D], F32),
        ]:
            dbg[name] = nc.dram_tensor(name, shape, dt, kind="ExternalOutput").ap()

    from contextlib import ExitStack
    es = ExitStack()
    with es:
        SBN = 211968
        SBt = es.enter_context(nc.sbuf_tensor("SB", [128, SBN], mybir.dt.uint8))
        PSt = es.enter_context(nc.psum_tensor("PS", [128, 4096], F32))
        A = Arena(SBt, SBN)

        def psf(b0, nb=1):
            return PSt[:, b0 * 512:(b0 + nb) * 512]

        def psb(b0, nb=1):
            return PSt[:, b0 * 512:(b0 + nb) * 512].bitcast(BF16)

        PSB = [Buf(f"psum{i}") for i in range(8)]
        p.make_dma_pool(80)
        p.clear_sems_and_barrier()


        def sba(name, nelem, dtype, side="low"):
            sz = nelem * (2 if dtype == BF16 else 4)
            ap, buf, off = A.alloc(name, sz, dtype, width=nelem, side=side)
            return ap, buf, off

        id_bf, B_id, _ = sba("id_bf", 128, BF16)
        id_f32, B_idf, _ = sba("id_f32", 128, F32)
        mask_t, B_mask, _ = sba("mask_t", 128, F32)
        ones_row_f, B_ones, _ = sba("ones_row_f", 128, F32)
        bias_pack, B_bias, _ = sba("bias_pack", 64, F32)
        bv_bc, B_bvbc, o_bvbc = sba("bv_bc", 256, F32)
        bpp_bc, B_bppbc, o_bppbc = sba("bpp_bc", D, F32)
        borow_t, B_borow, o_bo = sba("bo_row", D, F32)
        bpprow_t, B_bpprow, o_bpp = sba("bpp_row", D, F32)
        bvrow_t, B_bvrow, o_bv = sba("bv_row", 256, F32)
        bo_row = borow_t[0:1, :]
        bpp_row = bpprow_t[0:1, :]
        bv_row = bvrow_t[0:1, :]
        wq2kv_bf, B_w2kv, _ = sba("wq2kv_bf", KV, BF16)
        wq2kv_f32, B_w2kvf, _ = sba("wq2kv_f32", KV, F32)
        wkv2h_f, B_wkvf, _ = sba("wkv2h_f", 128, BF16)
        wkv2h_t, B_wkvt, _ = sba("wkv2h_t", KV, BF16)
        bkv_bf, B_bkvbf, _ = sba("bkv_bf", 1, BF16)
        lnstat, B_lnstat, _ = sba("lnstat", 8, F32)
        eps_col, B_eps, _ = sba("eps_col", 1, F32)
        lnstat_, B_lnstat_ = [], []
        for i in range(2):
            a, b, o = sba(f"lnstat{i}", 32, F32)
            lnstat_.append(a); B_lnstat_.append(b)
        trev_, B_trev, o_trev = [], [], []
        for i in range(2):
            a, b, o = sba(f"trev{i}", 512, BF16)
            trev_.append(a); B_trev.append(b); o_trev.append(o)
        NQSLOT = 2
        QW = 1024
        qrow_, B_qrow, o_qrow = [], [], []
        qmisc = {}
        for i in range(NQSLOT):
            a, b, o = sba(f"qrow{i}", QW, BF16)
            qrow_.append(a); B_qrow.append(b); o_qrow.append(o)
        for nm, ne, dt in [("qsc", 16, F32), ("qscb", 16, BF16), ("qthr", 16, F32),
                           ("qaw", QW, BF16), ("qmask", QW, BF16), ("qsgn", QW, BF16),
                           ("qwh", QW, BF16)]:
            qmisc[nm] = []
            for i in range(NQSLOT):
                a, b, o = sba(f"{nm}{i}", ne, dt)
                qmisc[nm].append((a, b, o))
        qtr_, B_qtr = [], []
        for i in range(2):
            a, b, o = sba(f"qtr{i}", 1024, BF16)
            qtr_.append(a); B_qtr.append(b)
        smax, B_smax, _ = sba("smax", 4, F32)

        def make_identity(ident, bbuf):
            p.emit("gp", lambda: nc.gpsimd.memset(ident, 0.0), writes=[bbuf])
            p.emit("gp", lambda: nc.gpsimd.affine_select(
                out=ident, in_=ident, compare_op=ALU.not_equal, fill=1.0,
                base=0, pattern=[[-1, 128]], channel_multiplier=1),
                reads=[bbuf], writes=[bbuf])

        make_identity(id_bf, B_id)
        make_identity(id_f32, B_idf)
        p.emit("gp", lambda: nc.gpsimd.memset(mask_t, 0.0), writes=[B_mask])
        p.emit("gp", lambda: nc.gpsimd.affine_select(
            out=mask_t, in_=mask_t, compare_op=ALU.is_ge, fill=-30000.0,
            base=0, pattern=[[-1, 128]], channel_multiplier=1),
            reads=[B_mask], writes=[B_mask])
        p.emit("gp", lambda: nc.gpsimd.memset(ones_row_f, 1.0), writes=[B_ones])
        p.emit("gp", lambda: nc.gpsimd.memset(eps_col, LN_EPS), writes=[B_eps])

        with nc.allow_non_contiguous_dma("tiny bias column loads"):
            p.dma("sp", bias_pack[:, 0:16], bq_d.rearrange("(n p) -> p n", p=128),
                  writes=[B_bias], sem=None)
            p.dma("sp", bias_pack[:, 16:24], bl_d.rearrange("(n p) -> p n", p=128),
                  writes=[B_bias], sem=None)
            p.dma("sp", bias_pack[:, 24:26], bk_d.rearrange("(n p) -> p n", p=128),
                  writes=[B_bias], sem=None)
            p.dma("sp", bias_pack[:, 34:35], bkv2h_d.rearrange("(n p) -> p n", p=128),
                  writes=[B_bias], sem=None)
        p.dma("sp", bv_row, bv_d.rearrange("(o n) -> o n", o=1),
              writes=[B_bvrow], sem=None)
        p.dma("sp", bo_row, bo_d.rearrange("(o n) -> o n", o=1),
              writes=[B_borow], sem=None)
        p.dma("gp", wq2kv_bf, wq2kv_d, writes=[B_w2kv], sem=None)
        p.dma("sp", wq2kv_f32, wq2kv_d, writes=[B_w2kvf], sem=None)
        p.dma("gp", wkv2h_f[0:64, :], wkv2h_d, writes=[B_wkvf], sem=None)

        p.emit("pe", lambda: nc.tensor.matmul(
            psf(0)[:, 0:256], ones_row_f[0:1, :], bv_row,
            start=True, stop=True), reads=[B_ones, B_bvrow], writes=[PSB[0]])
        p.emit("ve", lambda: nc.vector.tensor_copy(out=bv_bc, in_=psf(0)[:, 0:256]),
               reads=[PSB[0]], writes=[B_bvbc])

        p.emit("pe", lambda: nc.tensor.transpose(
            psb(1)[:, 0:64], wkv2h_f[0:64, :], id_bf[0:64, 0:64]),
            reads=[B_wkvf, B_id], writes=[PSB[1]])
        p.emit("sc", lambda: nc.scalar.copy(out=wkv2h_t, in_=psb(1)[:, 0:64]),
               reads=[PSB[1]], writes=[B_wkvt])
        p.emit("ve", lambda: nc.vector.tensor_copy(
            out=bkv_bf, in_=bias_pack[:, 34:35]), reads=[B_bias], writes=[B_bkvbf])

        for h in range(NH):
            p.emit("pe", lambda h=h: nc.tensor.matmul(
                psf(2)[0:64, h:h + 1],
                wq2kv_f32, bias_pack[:, h:h + 1],
                start=True, stop=True), reads=[B_w2kvf, B_bias], writes=[PSB[2]])
        p.emit("ve", lambda: nc.vector.tensor_copy(
            out=bias_pack[0:64, 40:56], in_=psf(2)[0:64, 0:16]),
            reads=[PSB[2]], writes=[B_bias])
        for h in range(NH):
            c, sl = q_chunk(h), q_slot(h)
            p.emit("ve", lambda h=h, c=c, sl=sl: nc.vector.tensor_copy(
                out=bias_pack[sl * 64:sl * 64 + 64, 26 + c:27 + c],
                in_=bias_pack[0:64, 40 + h:41 + h]),
                reads=[B_bias], writes=[B_bias])

        wrow0, B_wr0, o_wr0 = sba("wrow0", D, BF16)
        wrow1, B_wr1, o_wr1 = sba("wrow1", D, BF16)
        wrow = [wrow0, wrow1]
        B_wrow = [B_wr0, B_wr1]
        B_wq_bf = Buf("wq_bf_dram")
        B_wl_blk = Buf("wl_blk_dram")
        B_wk_blk = Buf("wk_blk_dram")
        B_wv_nat = Buf("wv_nat_dram")
        B_wpr = Buf("wpr_dram")

        for k in range(16):
            s = k % 2
            p.dma("gp", wrow[s], wq_d[k * 128:(k + 1) * 128, :],
                  writes=[B_wrow[s]], sem=None)
            p.dma("sp", wq_bf[k * 128:(k + 1) * 128, :], wrow[s],
                  reads=[B_wrow[s]], writes=[B_wq_bf], sem=None)
        for k in range(16):
            s = k % 2
            p.dma("gp", wrow[s][:, 0:LAT], wl_d[k * 128:(k + 1) * 128, :],
                  writes=[B_wrow[s]], sem=None)
            p.dma("sp", wl_blk[:, :, k, :].rearrange("n p f -> p n f"),
                  wrow[s][:, 0:LAT].rearrange("p (n f) -> p n f", f=128),
                  reads=[B_wrow[s]], writes=[B_wl_blk], sem=None)
        for k in range(8):
            s = k % 2
            p.dma("gp", wrow[s][:, 0:256], wk_d[k * 128:(k + 1) * 128, :],
                  writes=[B_wrow[s]], sem=None)
            p.dma("sp", wk_blk[:, :, k, :].rearrange("n p f -> p n f"),
                  wrow[s][:, 0:256].rearrange("p (n f) -> p n f", f=128),
                  reads=[B_wrow[s]], writes=[B_wk_blk], sem=None)
        for k in range(8):
            s = k % 2
            p.dma("gp", wrow[s][:, 0:256], wv_d[k * 128:(k + 1) * 128, :],
                  writes=[B_wrow[s]], sem=None)
            p.dma("sp", wv_nat[k * 128:(k + 1) * 128, :], wrow[s][:, 0:256],
                  reads=[B_wrow[s]], writes=[B_wv_nat], sem=None)

        wpr_ev, B_wprev, o_wprev = sba("wpr_ev", D, BF16)
        p.emit("ve", lambda: nc.vector.tensor_copy(
            out=bkv_bf, in_=bias_pack[:, 34:35]), reads=[B_bias], writes=[B_bkvbf])
        p.emit("ve", lambda: nc.vector.tensor_copy(
            out=bpp_row, in_=bo_row), reads=[B_borow], writes=[B_bpprow])

        def prep_wo_step(h):
            s = h % 2
            p.dma("gp", wrow[s], wo_d[h * 128:(h + 1) * 128, :],
                  writes=[B_wrow[s]], sem=None)
            for n in range(4):
                p.emit("pe", lambda s=s, n=n: nc.tensor.matmul(
                    psf(n)[0:64, :], wkv2h_t, wrow[s][:, n * 512:(n + 1) * 512],
                    start=True, stop=True),
                    reads=[B_wrow[s], B_wkvt], writes=[PSB[n]])
                p.emit("pe", lambda s=s, n=n: nc.tensor.matmul(
                    psf(7)[0:1, :], bkv_bf, wrow[s][:, n * 512:(n + 1) * 512],
                    start=True, stop=True),
                    reads=[B_wrow[s], B_bkvbf], writes=[PSB[7]])
                p.emit("ve", lambda n=n: nc.vector.tensor_add(
                    out=bpp_row[:, n * 512:(n + 1) * 512],
                    in0=bpp_row[:, n * 512:(n + 1) * 512], in1=psf(7)[0:1, :]),
                    reads=[PSB[7], B_bpprow], writes=[B_bpprow])
            p.emit("sc", lambda: nc.scalar.copy(
                out=wpr_ev[0:64, :], in_=PSt[0:64, 0:2048]),
                reads=[PSB[0], PSB[1], PSB[2], PSB[3]], writes=[B_wprev])
            p.dma("sp", wpr_nat[h * 64:(h + 1) * 64, :], wpr_ev[0:64, :],
                  reads=[B_wprev], writes=[B_wpr], sem=None)

        def prep_bpp_bcast():
            for n in range(4):
                p.emit("pe", lambda n=n: nc.tensor.matmul(
                    psf(n)[:, :], ones_row_f[0:1, :],
                    bpp_row[:, n * 512:(n + 1) * 512],
                    start=True, stop=True), reads=[B_ones, B_bpprow],
                    writes=[PSB[n]])
                p.emit("ve", lambda n=n: nc.vector.tensor_copy(
                    out=bpp_bc[:, n * 512:(n + 1) * 512], in_=psf(n)[:, :]),
                    reads=[PSB[n]], writes=[B_bppbc])

        wqth0, B_wqth0, o_wqth0 = sba("wqth0", D, BF16)
        wqth1, B_wqth1, o_wqth1 = sba("wqth1", D, BF16)
        wqth = [wqth0, wqth1]
        B_wqth = [B_wqth0, B_wqth1]
        qp_ev, B_qpev, o_qpev = sba("qp_ev", 16 * 128, BF16)
        B_wqp_blk = Buf("wqp_blk_dram")
        def prep_wqp_step(c):
            for sl in range(2):
                h = (c % 4) + 4 * sl + 8 * (c // 4)
                s = (2 * c + sl) % 2
                p.dma("sp", wqth[s], wq_bf[:, h * 128:(h + 1) * 128],
                      reads=[B_wq_bf], writes=[B_wqth[s]], sem=None, transpose=True)
                for k in range(16):
                    bk = k % 2
                    p.emit("pe", lambda s=s, k=k, bk=bk: nc.tensor.matmul(
                        psf(bk)[:, 0:64],
                        wqth[s][:, k * 128:(k + 1) * 128],
                        wq2kv_bf, start=True, stop=True),
                        reads=[B_wqth[s], B_w2kv], writes=[PSB[bk]])
                    p.emit("sc", lambda k=k, sl=sl, bk=bk: nc.scalar.copy(
                        out=qp_ev[:, k * 128 + sl * 64:k * 128 + sl * 64 + 64],
                        in_=psf(bk)[:, 0:64]),
                        reads=[PSB[bk]], writes=[B_qpev])
            p.dma("sp", wqp_blk[c], qp_ev.rearrange("p (k f) -> p k f", f=128),
                  reads=[B_qpev], writes=[B_wqp_blk], sem=None)

        prep_steps = ([(lambda h=h: prep_wo_step(h)) for h in range(NH)]
                      + [prep_bpp_bcast]
                      + [(lambda c=c: prep_wqp_step(c)) for c in range(8)])
        pi_state = {"i": 0}

        def emit_prep(n):
            for _ in range(n):
                if pi_state["i"] >= len(prep_steps):
                    return
                prep_steps[pi_state["i"]]()
                pi_state["i"] += 1



        B_wgu_q = Buf("wgu_q_dram")
        B_wp_q = Buf("wp_q_dram")

        def quant_load(src_ap, s):
            p.dma("gp", qrow_[s], src_ap, writes=[B_qrow[s]], sem=None)

        def quant_tile(src_ap, dst_ap, dst_buf, s, transpose_to=None):
            qr, Bqr = qrow_[s], B_qrow[s]
            qsc, Bqsc = qmisc["qsc"][s][0], qmisc["qsc"][s][1]
            qscb, Bqscb = qmisc["qscb"][s][0], qmisc["qscb"][s][1]
            qthr, Bqthr = qmisc["qthr"][s][0], qmisc["qthr"][s][1]
            qaw, Bqaw = qmisc["qaw"][s][0], qmisc["qaw"][s][1]
            qmask, Bqmask = qmisc["qmask"][s][0], qmisc["qmask"][s][1]
            qsgn, Bqsgn = qmisc["qsgn"][s][0], qmisc["qsgn"][s][1]
            qwh, Bqwh = qmisc["qwh"][s][0], qmisc["qwh"][s][1]
            p.emit("ve", lambda: nc.vector.tensor_reduce(
                out=qsc, in_=qr.rearrange("p (g w) -> p g w", w=GROUP),
                axis=AX.X, op=ALU.add, apply_absolute_value=True),
                reads=[Bqr], writes=[Bqsc])
            p.emit("ve", lambda: nc.vector.tensor_scalar(
                out=qscb, in0=qsc, scalar1=1.0 / GROUP, scalar2=1e-8,
                op0=ALU.mult, op1=ALU.max), reads=[Bqsc], writes=[Bqscb])
            p.emit("ve", lambda: nc.vector.tensor_scalar(
                out=qthr, in0=qscb, scalar1=QTHR, scalar2=None, op0=ALU.mult),
                reads=[Bqscb], writes=[Bqthr])
            p.emit("sc", lambda: nc.scalar.activation(out=qaw, in_=qr, func=AF.Abs),
                   reads=[Bqr], writes=[Bqaw])
            p.emit("sc", lambda: nc.scalar.activation(out=qsgn, in_=qr, func=AF.Sign),
                   reads=[Bqr], writes=[Bqsgn])
            a3, t3 = broadcast_tensor_aps(
                qaw.rearrange("p (g w) -> p g w", w=GROUP),
                qthr.rearrange("p (g w) -> p g w", w=1))
            p.emit("ve", lambda: nc.vector.tensor_tensor(
                out=qmask.rearrange("p (g w) -> p g w", w=GROUP),
                in0=a3, in1=t3, op=ALU.is_gt),
                reads=[Bqaw, Bqthr], writes=[Bqmask])
            m3, s3 = broadcast_tensor_aps(
                qmask.rearrange("p (g w) -> p g w", w=GROUP),
                qscb.rearrange("p (g w) -> p g w", w=1))
            p.emit("gp", lambda: nc.gpsimd.tensor_tensor(
                out=qmask.rearrange("p (g w) -> p g w", w=GROUP),
                in0=m3, in1=s3, op=ALU.mult),
                reads=[Bqmask, Bqscb], writes=[Bqmask])
            p.emit("ve", lambda: nc.vector.tensor_mul(
                out=qwh, in0=qmask, in1=qsgn), reads=[Bqmask, Bqsgn], writes=[Bqwh])
            p.emit("ve", lambda: nc.vector.tensor_sub(
                out=qwh, in0=qwh, in1=qr), reads=[Bqwh, Bqr], writes=[Bqwh])
            p.emit("ve", lambda: nc.vector.tensor_add(
                out=qwh, in0=qwh, in1=qr), reads=[Bqwh, Bqr], writes=[Bqwh])
            if transpose_to is None:
                p.dma("sp", dst_ap, qwh, reads=[Bqwh], writes=[dst_buf], sem=None)
            else:
                for hc in range(QW // 128):
                    p.emit("pe", lambda hc=hc: nc.tensor.transpose(
                        psb(7)[:, hc * 128:(hc + 1) * 128],
                        qwh[:, hc * 128:(hc + 1) * 128], id_bf),
                        reads=[Bqwh, B_id], writes=[PSB[7]])
                p.emit("ve", lambda s=s: nc.vector.tensor_copy(
                    out=qtr_[s], in_=psb(7)[:, 0:QW]),
                    reads=[PSB[7]], writes=[B_qtr[s]])
                p.dma("sp", transpose_to,
                      qtr_[s].rearrange("p (c f) -> p c f", f=128),
                      reads=[B_qtr[s]], writes=[dst_buf], sem=None)

        B_wguq_sh = Buf("wguq_sh_dram")
        B_wpq_sh = Buf("wpq_sh_dram")
        quant_steps = []
        for r in range(2 * HF // 8 // 128):
            for c in range(D // QW):
                quant_steps.append((wgu_sh_d[r * 128:(r + 1) * 128, c * QW:(c + 1) * QW],
                                    wguq_sh[r * 128:(r + 1) * 128, c * QW:(c + 1) * QW],
                                    B_wguq_sh))
        for r in range(D // 128):
            dst3 = wpqt_sh.rearrange("(c p) d -> p c d", p=128)[
                :, :, r * 128:(r + 1) * 128]
            quant_steps.append((wp_sh_d[r * 128:(r + 1) * 128, :], None,
                                B_wpq_sh, dst3))
        qi_state = {"i": 0, "loaded": 0}
        QLOOK = NQSLOT - 1

        def emit_quant(n):
            for _ in range(n):
                i = qi_state["i"]
                if i >= len(quant_steps):
                    return
                while (qi_state["loaded"] < len(quant_steps)
                       and qi_state["loaded"] <= i + QLOOK):
                    j = qi_state["loaded"]
                    quant_load(quant_steps[j][0], j % NQSLOT)
                    qi_state["loaded"] += 1
                step = quant_steps[i]
                if len(step) == 3:
                    quant_tile(step[0], step[1], step[2], i % NQSLOT)
                else:
                    quant_tile(step[0], None, step[2], i % NQSLOT,
                               transpose_to=step[3])
                qi_state["i"] += 1


        def ln_token_major(x_buf, x_ap, out_ap, out_buf, eps, sl=0):
            lnstat = lnstat_[sl]
            B_lnstat = B_lnstat_[sl]
            for c4 in range(4):
                p.emit("ve", lambda c4=c4: nc.vector.bn_stats(
                    out=lnstat[:, 8 + c4 * 6:14 + c4 * 6],
                    in_=x_ap[:, c4 * 512:(c4 + 1) * 512]),
                    reads=[x_buf], writes=[B_lnstat])
            p.emit("ve", lambda: nc.vector.bn_aggr(
                out=lnstat[:, 0:2], in_=lnstat[:, 8:32]),
                reads=[B_lnstat], writes=[B_lnstat])
            p.emit("sc", lambda: nc.scalar.activation(
                out=lnstat[:, 7:8], in_=lnstat[:, 1:2], func=AF.Ln, bias=eps_col),
                reads=[B_lnstat, B_eps], writes=[B_lnstat])
            p.emit("sc", lambda: nc.scalar.activation(
                out=lnstat[:, 4:5], in_=lnstat[:, 7:8], func=AF.Exp, scale=-0.5),
                reads=[B_lnstat], writes=[B_lnstat])
            p.emit("ve", lambda: nc.vector.tensor_scalar(
                out=lnstat[:, 5:6], in0=lnstat[:, 0:1], scalar1=lnstat[:, 4:5],
                scalar2=-1.0, op0=ALU.mult, op1=ALU.mult),
                reads=[B_lnstat], writes=[B_lnstat])
            p.emit("sc", lambda: nc.scalar.activation(
                out=out_ap, in_=x_ap, func=AF.Identity,
                bias=lnstat[:, 5:6], scale=lnstat[:, 4:5]),
                reads=[x_buf, B_lnstat], writes=[out_buf])

        ln1t, B_ln1t, o_ln1t = sba("ln1t", 16 * T, BF16, side="high")
        ln1t3 = ln1t.rearrange("p (c t) -> p c t", t=T)
        xrow_, B_xrow, o_xrow = [], [], []
        for i in range(3):
            a, b, o = sba(f"xrow{i}", D, F32)
            xrow_.append(a); B_xrow.append(b); o_xrow.append(o)
        lntok_, B_lntok, o_lntok = [], [], []
        for i in range(2):
            a, b, o = sba(f"lntok{i}", D, BF16)
            lntok_.append(a); B_lntok.append(b); o_lntok.append(o)

        def transpose_to_feat(src_ap, src_buf, dst3, dst_buf, tc, banks=(5, 6)):
            for cg in range(4):
                bk = banks[cg % 2]
                for c4 in range(4):
                    c = cg * 4 + c4
                    p.emit("pe", lambda bk=bk, c4=c4, c=c: nc.tensor.transpose(
                        psb(bk)[:, c4 * 128:(c4 + 1) * 128],
                        src_ap[:, c * 128:(c + 1) * 128], id_bf),
                        reads=[src_buf, B_id], writes=[PSB[bk]])
                out3 = dst3[:, cg * 4:(cg + 1) * 4, tc * 128:(tc + 1) * 128]
                p.emit("ve", lambda bk=bk, out3=out3: nc.vector.tensor_copy(
                    out=out3, in_=psb(bk)[:, 0:512].rearrange("p (c t) -> p c t", t=128)),
                    reads=[PSB[bk]], writes=[dst_buf])

        def xload(tc):
            p.dma("sp", xrow_[tc % 3], x_d[tc * 128:(tc + 1) * 128, :],
                  writes=[B_xrow[tc % 3]], sem=None)
        xload(0)
        xload(1)
        for tc in range(16):
            s = tc % 3
            sp2 = tc % 2
            if tc + 2 < 16:
                xload(tc + 2)
            ln_token_major(B_xrow[s], xrow_[s], lntok_[sp2], B_lntok[sp2],
                           LN_EPS, sl=sp2)
            transpose_to_feat(lntok_[sp2], B_lntok[sp2], ln1t3, B_ln1t, tc)
            emit_quant(1)
            emit_prep(2)

        if debug:
            p.dma("sp", dbg["dbg_ln1t"].rearrange("(c p) t -> p (c t)", p=128),
                  ln1t, reads=[B_ln1t], writes=[Buf("dbg")], sem=None)

        emit_prep(len(prep_steps))
        for o in [o_wqth0, o_wqth1, o_qpev, o_wprev, o_wr0, o_wr1,
                  o_bo, o_bpp, o_bv]:
            A.release(o)
        for o in o_xrow + o_lntok:
            A.release(o)
        q2t, B_q2t, o_q2t = sba("q2t", 8 * T, BF16, side="high")
        latt, B_latt, o_latt = sba("latt", 8 * T, BF16, side="high")
        kt, B_kt, o_kt = sba("kt", 2 * T, BF16, side="high")
        vtok, B_vtok, o_vtok = sba("vtok", 16 * 256, BF16, side="high")
        q2t3 = q2t.rearrange("p (c t) -> p c t", t=T)
        latt3 = latt.rearrange("p (c t) -> p c t", t=T)
        kt3 = kt.rearrange("p (c t) -> p c t", t=T)
        wblk_, B_wblk, o_wblk = [], [], []
        for i in range(3):
            a, b, o = sba(f"wblk{i}", 16 * 128, BF16)
            wblk_.append(a); B_wblk.append(b); o_wblk.append(o)

        def proj_feat(n_groups, n_k, load_fn, load_buf, rhs3, rhs_buf, out_sb,
                      out_buf, bias_col_fn, act, qsteps=0):
            def wload(n):
                s = n % 3
                p.dma("sp", wblk_[s][:, 0:n_k * 128].rearrange(
                    "p (k f) -> p k f", f=128), load_fn(n),
                    reads=[load_buf], writes=[B_wblk[s]], sem=None)
            wload(0)
            if n_groups > 1:
                wload(1)
            for n in range(n_groups):
                s = n % 3
                if n + 2 < n_groups:
                    wload(n + 2)
                for tq in range(4):
                    bk = tq % 4
                    for k in range(n_k):
                        p.emit("pe", lambda s=s, bk=bk, k=k, tq=tq: nc.tensor.matmul(
                            psf(bk), wblk_[s][:, k * 128:(k + 1) * 128],
                            rhs3[:, k, tq * 512:(tq + 1) * 512],
                            start=(k == 0), stop=(k == n_k - 1)),
                            reads=[B_wblk[s], rhs_buf], writes=[PSB[bk]])
                    p.emit("sc", lambda bk=bk, n=n, tq=tq: nc.scalar.activation(
                        out=out_sb[:, n * T + tq * 512:n * T + (tq + 1) * 512],
                        in_=psf(bk), func=act, bias=bias_col_fn(n)),
                        reads=[PSB[bk], B_bias], writes=[out_buf])
                emit_quant(qsteps)

        proj_feat(8, 16, lambda n: wqp_blk[n], B_wqp_blk, ln1t3, B_ln1t,
                  q2t, B_q2t, lambda n: bias_pack[:, 26 + n:27 + n], AF.Identity,
                  qsteps=1)
        proj_feat(8, 16, lambda n: wl_blk[n, :, :, :], B_wl_blk, ln1t3, B_ln1t,
                  latt, B_latt, lambda n: bias_pack[:, 16 + n:17 + n], AF.Silu,
                  qsteps=1)
        proj_feat(2, 8, lambda n: wk_blk[n, :, :, :], B_wk_blk, latt3, B_latt,
                  kt, B_kt, lambda n: bias_pack[:, 24 + n:25 + n], AF.Identity,
                  qsteps=1)

        p.dma("sp", wblk_[0][:, 0:8 * 256].rearrange("p (k f) -> p k f", f=256),
              wv_nat.rearrange("(k p) f -> p k f", p=128),
              reads=[B_wv_nat], writes=[B_wblk[0]], sem=None)
        wv3 = wblk_[0][:, 0:8 * 256].rearrange("p (k f) -> p k f", f=256)
        for tv in range(16):
            bk = 2 + tv % 2
            for k in range(8):
                p.emit("pe", lambda bk=bk, k=k, tv=tv: nc.tensor.matmul(
                    psf(bk)[:, 0:256], latt3[:, k, tv * 128:(tv + 1) * 128],
                    wv3[:, k, :], start=(k == 0), stop=(k == 7)),
                    reads=[B_latt, B_wblk[0]], writes=[PSB[bk]])
            p.emit("ve", lambda bk=bk, tv=tv: nc.vector.tensor_add(
                out=vtok[:, tv * 256:(tv + 1) * 256], in0=psf(bk)[:, 0:256],
                in1=bv_bc), reads=[PSB[bk], B_bvbc], writes=[B_vtok])
            emit_quant(1)

        if debug:
            p.dma("sp", dbg["dbg_q2t"].rearrange("(c p) t -> p (c t)", p=128),
                  q2t, reads=[B_q2t], writes=[Buf("dbg")], sem=None)
            p.dma("sp", dbg["dbg_latt"].rearrange("(c p) t -> p (c t)", p=128),
                  latt, reads=[B_latt], writes=[Buf("dbg")], sem=None)
            p.dma("sp", dbg["dbg_kt"].rearrange("(c p) t -> p (c t)", p=128),
                  kt, reads=[B_kt], writes=[Buf("dbg")], sem=None)
            p.dma("sp", dbg["dbg_v"].rearrange("(c p) f -> p (c f)", p=128),
                  vtok, reads=[B_vtok], writes=[Buf("dbg")], sem=None)

        A.release(o_ln1t)
        A.release(o_latt)
        A.release(o_bvbc)
        ctxt, B_ctxt, o_ctxt = sba("ctxt", 8 * T, BF16, side="high")
        ctxt3 = ctxt.rearrange("p (c t) -> p c t", t=T)
        NW = 4
        sc_, B_scsb, o_scsb = [], [], []
        pr_, B_prsb, o_prsb = [], [], []
        pr2_, B_pr2sb, o_pr2sb = [], [], []
        smax_, B_smax_ = [], []
        prT_, B_prT, o_prT = [], [], []
        for i in range(NW):
            a, b, o = sba(f"sc_sb{i}", T, F32)
            sc_.append(a); B_scsb.append(b); o_scsb.append(o)
            a, b, o = sba(f"pr_sb{i}", T, BF16)
            pr_.append(a); B_prsb.append(b); o_prsb.append(o)
            a, b, o = sba(f"pr2_sb{i}", T, BF16)
            pr2_.append(a); B_pr2sb.append(b); o_pr2sb.append(o)
            a, b, o = sba(f"smax{i}", 4, F32)
            smax_.append(a); B_smax_.append(b)
            a, b, o = sba(f"prT{i}", 1024, BF16)
            prT_.append(a); B_prT.append(b); o_prT.append(o)

        units = [(qi, h) for qi in range(16) for h in range(NH)]

        def stageA(u):
            qi, h = units[u]
            nkt = qi + 1
            g = h // 4
            hp = u % NW
            sc_sb, Bsc = sc_[hp], B_scsb[hp]
            eeng = "sc" if h % 2 == 0 else "ve"
            for jg in range(ceil_div(nkt, 4)):
                bk = jg % 2
                js = list(range(jg * 4, min(jg * 4 + 4, nkt)))
                p.emit("pe", lambda h=h, g=g, qi=qi, js=js, bk=bk:
                       nc.tensor.matmul(
                    psf(bk)[:, 0:len(js) * 128],
                    q2t3[q_slot(h) * 64:q_slot(h) * 64 + 64, q_chunk(h),
                         qi * 128:(qi + 1) * 128],
                    kt3[(g % 2) * 64:(g % 2) * 64 + 64, g // 2,
                        js[0] * 128:(js[-1] + 1) * 128],
                    start=True, stop=True),
                    reads=[B_q2t, B_kt], writes=[PSB[bk]])
                jlo, jhi = js[0], js[-1]
                ndiag = jhi == qi
                if not ndiag or jhi > jlo:
                    e_hi = jhi * 128 if ndiag else (jhi + 1) * 128
                    npc = (jhi - jlo) if ndiag else (jhi - jlo + 1)
                    if eeng == "sc":
                        p.emit("sc", lambda bk=bk, jlo=jlo, npc=npc, e_hi=e_hi,
                               sc_sb=sc_sb: nc.scalar.activation(
                            out=sc_sb[:, jlo * 128:e_hi],
                            in_=psf(bk)[:, 0:npc * 128],
                            func=AF.Identity, scale=0.125),
                            reads=[PSB[bk]], writes=[Bsc])
                    else:
                        p.emit("ve", lambda bk=bk, jlo=jlo, npc=npc, e_hi=e_hi,
                               sc_sb=sc_sb: nc.vector.tensor_scalar(
                            out=sc_sb[:, jlo * 128:e_hi],
                            in0=psf(bk)[:, 0:npc * 128],
                            scalar1=0.125, scalar2=None, op0=ALU.mult),
                            reads=[PSB[bk]], writes=[Bsc])
                if ndiag:
                    p.emit("ve", lambda bk=bk, jlo=jlo, jhi=jhi,
                           sc_sb=sc_sb: nc.vector.scalar_tensor_tensor(
                        out=sc_sb[:, jhi * 128:(jhi + 1) * 128],
                        in0=psf(bk)[:, (jhi - jlo) * 128:(jhi - jlo + 1) * 128],
                        scalar=0.125, in1=mask_t, op0=ALU.mult, op1=ALU.add),
                        reads=[PSB[bk], B_mask], writes=[Bsc])

        def stageB1(u):
            qi, h = units[u]
            L = (qi + 1) * 128
            hp = u % NW
            sc_sb, Bsc = sc_[hp], B_scsb[hp]
            smax, Bsm = smax_[hp], B_smax_[hp]
            p.emit("ve", lambda L=L, sc_sb=sc_sb, smax=smax:
                   nc.vector.tensor_reduce(
                out=smax[:, 0:1], in_=sc_sb[:, 0:L], axis=AX.X, op=ALU.max,
                negate=True), reads=[Bsc], writes=[Bsm])

        def stageBX(u):
            qi, h = units[u]
            L = (qi + 1) * 128
            hp = u % NW
            sc_sb, Bsc = sc_[hp], B_scsb[hp]
            pr_sb, Bpr = pr_[hp], B_prsb[hp]
            smax, Bsm = smax_[hp], B_smax_[hp]
            p.emit("sc", lambda L=L, sc_sb=sc_sb, pr_sb=pr_sb, smax=smax:
                   nc.scalar.activation(
                out=pr_sb[:, 0:L], in_=sc_sb[:, 0:L], func=AF.Exp,
                bias=smax[:, 0:1], accum_out=smax[:, 2:3]),
                reads=[Bsc, Bsm], writes=[Bpr, Bsm])

        def stageB2(u):
            qi, h = units[u]
            L = (qi + 1) * 128
            hp = u % NW
            pr_sb, Bpr = pr_[hp], B_prsb[hp]
            pr2_sb, Bpr2 = pr2_[hp], B_pr2sb[hp]
            smax, Bsm = smax_[hp], B_smax_[hp]
            p.emit("ve", lambda smax=smax: nc.vector.reciprocal(
                out=smax[:, 3:4], in_=smax[:, 2:3]), reads=[Bsm], writes=[Bsm])
            if h % 2 == 0:
                p.emit("sc", lambda L=L, pr_sb=pr_sb, pr2_sb=pr2_sb, smax=smax:
                       nc.scalar.activation(
                    out=pr2_sb[:, 0:L], in_=pr_sb[:, 0:L], func=AF.Identity,
                    scale=smax[:, 3:4]),
                    reads=[Bpr, Bsm], writes=[Bpr2])
            else:
                p.emit("ve", lambda L=L, pr_sb=pr_sb, pr2_sb=pr2_sb, smax=smax:
                       nc.vector.tensor_scalar(
                    out=pr2_sb[:, 0:L], in0=pr_sb[:, 0:L], scalar1=smax[:, 3:4],
                    scalar2=None, op0=ALU.mult),
                    reads=[Bpr, Bsm], writes=[Bpr2])

        def stageC(u):
            qi, h = units[u]
            nkt = qi + 1
            g = h // 4
            hp = u % NW
            pr2_sb, Bpr2 = pr2_[hp], B_pr2sb[hp]
            cbk = 4 + h % 2
            eeng = "sc" if h % 2 == 0 else "ve"
            ngr = ceil_div(nkt, 4)
            jss = [list(range(jg * 4, min(jg * 4 + 4, nkt))) for jg in range(ngr)]

            def emit_tr(jg):
                tbk = 2 + jg % 2
                for j in jss[jg]:
                    p.emit("pe", lambda tbk=tbk, j=j, pr2_sb=pr2_sb:
                           nc.tensor.transpose(
                        psb(tbk)[:, (j % 4) * 128:(j % 4) * 128 + 128],
                        pr2_sb[:, j * 128:(j + 1) * 128], id_bf),
                        reads=[Bpr2, B_id], writes=[PSB[tbk]])
                js = jss[jg]
                if eeng == "sc":
                    p.emit("sc", lambda tbk=tbk, hp=hp, js=js: nc.scalar.copy(
                        out=prT_[hp][:, (js[0] % 8) * 128:
                                     (js[0] % 8) * 128 + len(js) * 128],
                        in_=psb(tbk)[:, 0:len(js) * 128]),
                        reads=[PSB[tbk]], writes=[B_prT[hp]])
                else:
                    p.emit("ve", lambda tbk=tbk, hp=hp, js=js: nc.vector.tensor_copy(
                        out=prT_[hp][:, (js[0] % 8) * 128:
                                     (js[0] % 8) * 128 + len(js) * 128],
                        in_=psb(tbk)[:, 0:len(js) * 128]),
                        reads=[PSB[tbk]], writes=[B_prT[hp]])

            def emit_ctx(jg):
                js = jss[jg]
                for j in js:
                    p.emit("pe", lambda cbk=cbk, hp=hp, j=j, js=js, g=g,
                           nkt=nkt: nc.tensor.matmul(
                        psf(cbk)[0:64, 0:128],
                        vtok[:, j * 256 + g * 64:j * 256 + g * 64 + 64],
                        prT_[hp][:, (js[0] % 8) * 128 + (j - js[0]) * 128:
                                 (js[0] % 8) * 128 + (j - js[0] + 1) * 128],
                        start=(j == 0), stop=(j == nkt - 1)),
                        reads=[B_vtok, B_prT[hp]], writes=[PSB[cbk]])

            for jg in range(ngr):
                emit_tr(jg)
                if jg >= 1:
                    emit_ctx(jg - 1)
            emit_ctx(ngr - 1)
            p.emit("ve", lambda cbk=cbk, h=h, qi=qi: nc.vector.tensor_copy(
                out=ctxt3[(h % 2) * 64:(h % 2) * 64 + 64, h // 2,
                          qi * 128:(qi + 1) * 128],
                in_=psf(cbk)[0:64, 0:128]),
                reads=[PSB[cbk]], writes=[B_ctxt])

        for k in range(len(units) + 4):
            if k % 8 == 3:
                emit_quant(1)
            if k < len(units):
                stageA(k)
            if 1 <= k < len(units) + 1:
                stageB1(k - 1)
            if 2 <= k < len(units) + 2:
                stageBX(k - 2)
            if 3 <= k < len(units) + 3:
                stageB2(k - 3)
            if k >= 4:
                stageC(k - 4)

        if debug:
            p.dma("sp", dbg["dbg_ctxt"].rearrange("(c p) t -> p (c t)", p=128),
                  ctxt, reads=[B_ctxt], writes=[Buf("dbg")], sem=None)

        emit_quant(len(quant_steps))
        cc1 = p.new_sem("cc1")
        cc2 = p.new_sem("cc2")
        p.emit("gp", lambda: nc.gpsimd.collective_compute(
            "AllGather", ALU.bypass, replica_groups=[list(range(NCORES))],
            ins=[wguq_sh.opt()], outs=[wgu_q.opt()]),
            reads=[B_wguq_sh], writes=[B_wgu_q], sem="cc1", dinc=1)
        p.emit("gp", lambda: nc.gpsimd.collective_compute(
            "AllGather", ALU.bypass, replica_groups=[list(range(NCORES))],
            ins=[wpqt_sh.opt()], outs=[wpt_full.opt()]),
            reads=[B_wpq_sh], writes=[B_wp_q], sem="cc2", dinc=1)
        for i in range(NQSLOT):
            A.release(o_qrow[i])
        for nm in qmisc:
            for i in range(NQSLOT):
                A.release(qmisc[nm][i][2])


        A.release(o_q2t)
        A.release(o_kt)
        A.release(o_vtok)
        for o in o_scsb + o_prsb + o_pr2sb:
            A.release(o)

        wpr_sb, B_wprsb, o_wprsb = sba("wpr_sb", 8 * D, BF16, side="high")
        wpr3 = wpr_sb.rearrange("p (k f) -> p k f", f=D)
        for k in range(8):
            p.dma("sp", wpr3[:, k, :],
                  wpr_nat[:, :].rearrange("(k p) f -> p k f", p=128)[:, k, :],
                  reads=[B_wpr], writes=[B_wprsb], sem=None)
        xrow_, B_xrow, o_xrow = [], [], []
        for i in range(2):
            a, b, o = sba(f"xrow_c{i}", D, F32)
            xrow_.append(a); B_xrow.append(b); o_xrow.append(o)
        ao_, B_ao, o_ao = [], [], []
        for i in range(2):
            a, b, o = sba(f"ao{i}", D, F32)
            ao_.append(a); B_ao.append(b); o_ao.append(o)
        ztok_, B_zs, o_zs = [], [], []
        for i in range(2):
            a, b, o = sba(f"z_tok{i}", D, F32)
            ztok_.append(a); B_zs.append(b); o_zs.append(o)
        s2_, B_s2, o_s2 = [], [], []
        for i in range(2):
            a, b, o = sba(f"s2_{i}", D, F32)
            s2_.append(a); B_s2.append(b); o_s2.append(o)
        ln3_, B_ln3, o_ln3 = [], [], []
        for i in range(2):
            a, b, o = sba(f"ln3_{i}", D, BF16)
            ln3_.append(a); B_ln3.append(b); o_ln3.append(o)
        B_s2_dram = Buf("s2_dram")
        B_ln3t_dram = Buf("ln3t_dram")
        ln3t_v = ln3t_dram.rearrange("(c p) t -> p c t", p=128)

        def c1_s1(tc):
            s = tc % 2
            p.dma("sp", xrow_[s], x_d[tc * 128:(tc + 1) * 128, :],
                  writes=[B_xrow[s]], sem=None)
            for half in range(2):
                b0 = half * 2
                for k in range(8):
                    for ns in range(2):
                        p.emit("pe", lambda b0=b0, s=s, half=half, k=k, ns=ns, tc=tc:
                               nc.tensor.matmul(
                            psf(b0 + ns), ctxt3[:, k, tc * 128:(tc + 1) * 128],
                            wpr3[:, k, half * 1024 + ns * 512:
                                 half * 1024 + (ns + 1) * 512],
                            start=(k == 0), stop=(k == 7)),
                            reads=[B_ctxt, B_wprsb], writes=[PSB[b0 + ns]])
                p.emit("ve", lambda b0=b0, s=s, half=half: nc.vector.tensor_add(
                    out=ao_[s][:, half * 1024:(half + 1) * 1024], in0=psf(b0, 2),
                    in1=bpp_bc[:, half * 1024:(half + 1) * 1024]),
                    reads=[PSB[b0], PSB[b0 + 1], B_bppbc], writes=[B_ao[s]])

        def c1_s2(tc):
            s = tc % 2
            ln_token_major(B_ao[s], ao_[s], ztok_[s], B_zs[s], LN_EPS, sl=s)
            p.emit("ve", lambda s=s: nc.vector.tensor_add(
                out=s2_[s], in0=ztok_[s], in1=xrow_[s]),
                reads=[B_zs[s], B_xrow[s]], writes=[B_s2[s]])
            p.dma("sp", s2_dram[tc * 128:(tc + 1) * 128, :], s2_[s],
                  reads=[B_s2[s]], writes=[B_s2_dram], sem=None)
            ln_token_major(B_s2[s], s2_[s], ln3_[s], B_ln3[s], LN_EPS, sl=s)
            for cg in range(4):
                bk = 5 + cg % 2
                sp_ = cg % 2
                for c4 in range(4):
                    c = cg * 4 + c4
                    p.emit("pe", lambda s=s, bk=bk, c4=c4, c=c: nc.tensor.transpose(
                        psb(bk)[:, c4 * 128:(c4 + 1) * 128],
                        ln3_[s][:, c * 128:(c + 1) * 128], id_bf),
                        reads=[B_ln3[s], B_id], writes=[PSB[bk]])
                p.emit("ve", lambda bk=bk, sp_=sp_: nc.vector.tensor_copy(
                    out=prT_[sp_][:, 0:512], in_=psb(bk)[:, 0:512]),
                    reads=[PSB[bk]], writes=[B_prT[sp_]])
                p.dma("sp", ln3t_v[:, cg * 4:(cg + 1) * 4, tc * 128:(tc + 1) * 128],
                      prT_[sp_][:, 0:512].rearrange("p (c t) -> p c t", t=128),
                      reads=[B_prT[sp_]], writes=[B_ln3t_dram], sem=None)

        c1_s1(0)
        for tc in range(16):
            if tc + 1 < 16:
                c1_s1(tc + 1)
            c1_s2(tc)
        if debug:
            p.dma("sp", dbg["dbg_s2"], s2_dram, reads=[B_s2_dram], writes=[Buf("dbg")], sem=None)
            p.dma("sp", dbg["dbg_ln3t"], ln3t_dram, reads=[B_ln3t_dram], writes=[Buf("dbg")], sem=None)

        if debug:
            p.dma("sp", dbg["dbg_wguq"], wgu_q, reads=[B_wgu_q], writes=[Buf("dbg")], sem=None)

        A.release(o_ctxt)
        A.release(o_wprsb)
        A.release(o_bppbc)
        for o in o_ao + o_zs + o_s2 + o_ln3 + o_prT + o_xrow + [o_trev[0], o_trev[1]]:
            A.release(o)
        for o in o_wblk:
            A.release(o)

        ln3t_sb, B_ln3sb, o_ln3sb = sba("ln3t_sb", 16 * T, BF16, side="high")
        ln3s3 = ln3t_sb.rearrange("p (c t) -> p c t", t=T)
        for c4 in range(4):
            p.dma("sp", ln3s3[:, c4 * 4:(c4 + 1) * 4, :],
                  ln3t_dram.rearrange("(c p) t -> p c t", p=128)[:, c4 * 4:(c4 + 1) * 4, :],
                  reads=[B_ln3t_dram], writes=[B_ln3sb], sem=None)
        wgt_, B_wgt, o_wgt = [], [], []
        wut_, B_wut, o_wut = [], [], []
        for i in range(2):
            a, b, o = sba(f"wgt{i}", 16 * 512, BF16, side="high")
            wgt_.append(a); B_wgt.append(b); o_wgt.append(o)
            a, b, o = sba(f"wut{i}", 16 * 512, BF16, side="high")
            wut_.append(a); B_wut.append(b); o_wut.append(o)
        sg_sb, B_sg, o_sg = sba("sg_sb", 1024, BF16)
        ha_, B_ha, o_ha = [], [], []
        for i in range(2):
            a, b, o = sba(f"ha{i}", 1024, BF16)
            ha_.append(a); B_ha.append(b); o_ha.append(o)
        h2_sb, B_h2, o_h2 = sba("h2_sb", 1024, F32)
        acc_sb, B_acc, o_acc = sba("acc_sb", T, F32)
        B_hact_dram = Buf("hact_dram")
        p.emit("ve", lambda: nc.vector.memset(acc_sb, 0.0), writes=[B_acc])

        for jq in range(16):
            s = jq % 2
            for c in range(16):
                p.dma("sp", wgt_[s][:, c * 512:(c + 1) * 512],
                      wgu_q[jq * 512:(jq + 1) * 512, c * 128:(c + 1) * 128],
                      reads=[B_wgu_q], writes=[B_wgt[s]], sem=None, transpose=True)
                p.dma("sp", wut_[s][:, c * 512:(c + 1) * 512],
                      wgu_q[HF + jq * 512:HF + (jq + 1) * 512, c * 128:(c + 1) * 128],
                      reads=[B_wgu_q], writes=[B_wut[s]], sem=None, transpose=True)
            for jj in range(4):
                j = jq * 4 + jj
                gb = (j % 2) * 2
                ub = 4 + (j % 2) * 2
                for half in range(2):
                    for k in range(16):
                        for ns in range(2):
                            p.emit("pe", lambda s=s, gb=gb, jj=jj, k=k, ns=ns,
                                   half=half: nc.tensor.matmul(
                                psf(gb + ns),
                                wgt_[s][:, k * 512 + jj * 128:k * 512 + (jj + 1) * 128],
                                ln3s3[:, k, half * 1024 + ns * 512:
                                      half * 1024 + (ns + 1) * 512],
                                start=(k == 0), stop=(k == 15)),
                                reads=[B_wgt[s], B_ln3sb], writes=[PSB[gb + ns]])
                        for ns in range(2):
                            p.emit("pe", lambda s=s, ub=ub, jj=jj, k=k, ns=ns,
                                   half=half: nc.tensor.matmul(
                                psf(ub + ns),
                                wut_[s][:, k * 512 + jj * 128:k * 512 + (jj + 1) * 128],
                                ln3s3[:, k, half * 1024 + ns * 512:
                                      half * 1024 + (ns + 1) * 512],
                                start=(k == 0), stop=(k == 15)),
                                reads=[B_wut[s], B_ln3sb], writes=[PSB[ub + ns]])
                    p.emit("sc", lambda gb=gb: nc.scalar.activation(
                        out=sg_sb, in_=psf(gb, 2), func=AF.Silu),
                        reads=[PSB[gb], PSB[gb + 1]], writes=[B_sg])
                    hs = half % 2
                    p.emit("ve", lambda ub=ub, hs=hs: nc.vector.tensor_mul(
                        out=ha_[hs], in0=psf(ub, 2), in1=sg_sb),
                        reads=[PSB[ub], PSB[ub + 1], B_sg], writes=[B_ha[hs]])
                    p.dma("gp", hact_dram[j * 128:(j + 1) * 128,
                                          half * 1024:(half + 1) * 1024],
                          ha_[hs], reads=[B_ha[hs]], writes=[B_hact_dram],
                          sem=None)
                    p.emit("gp", lambda hs=hs: nc.gpsimd.tensor_mul(
                        out=h2_sb, in0=ha_[hs], in1=ha_[hs]),
                        reads=[B_ha[hs]], writes=[B_h2])
                    p.emit("ve", lambda half=half: nc.vector.tensor_add(
                        out=acc_sb[:, half * 1024:(half + 1) * 1024],
                        in0=acc_sb[:, half * 1024:(half + 1) * 1024], in1=h2_sb),
                        reads=[B_h2, B_acc], writes=[B_acc])

        invp, B_inv, o_invp = sba("inv_pack", T, F32)
        inv_row = invp[0:1, :]
        inv_bc, B_invbc, o_invbc = sba("inv_bc", T, F32)
        onec, B_onec, o_onec = sba("onec", 1, F32)
        p.emit("gp", lambda: nc.gpsimd.memset(onec, 1.0), writes=[B_onec])
        for ns in range(4):
            p.emit("pe", lambda ns=ns: nc.tensor.matmul(
                psf(ns)[0:1, :], onec, acc_sb[:, ns * 512:(ns + 1) * 512],
                start=True, stop=True), reads=[B_onec, B_acc], writes=[PSB[ns]])
        p.emit("ve", lambda: nc.vector.tensor_scalar(
            out=inv_row, in0=psf(0, 4)[0:1, 0:2048], scalar1=1.0 / HF,
            scalar2=RMS_EPS, op0=ALU.mult, op1=ALU.add),
            reads=[PSB[0], PSB[1], PSB[2], PSB[3]], writes=[B_inv])
        p.emit("sc", lambda: nc.scalar.activation(
            out=inv_row, in_=inv_row, func=AF.Ln),
            reads=[B_inv], writes=[B_inv])
        p.emit("sc", lambda: nc.scalar.activation(
            out=inv_row, in_=inv_row, func=AF.Exp, scale=-0.5),
            reads=[B_inv], writes=[B_inv])
        for ns in range(4):
            p.emit("pe", lambda ns=ns: nc.tensor.matmul(
                psf(4), ones_row_f[0:1, :], inv_row[:, ns * 512:(ns + 1) * 512],
                start=True, stop=True), reads=[B_ones, B_inv], writes=[PSB[4]])
            p.emit("ve", lambda ns=ns: nc.vector.tensor_copy(
                out=inv_bc[:, ns * 512:(ns + 1) * 512], in_=psf(4)),
                reads=[PSB[4]], writes=[B_invbc])

        if debug:
            p.dma("sp", dbg["dbg_hact"], hact_dram, reads=[B_hact_dram], writes=[Buf("dbg")], sem=None)

        A.release(o_ln3sb)
        for o in o_wgt + o_wut + [o_sg] + o_ha + [o_h2, o_acc, o_onec]:
            A.release(o)

        wpt_, B_wpt, o_wpt = [], [], []
        for i in range(2):
            a, b, o = sba(f"wpt{i}", 64 * 256, BF16, side="high")
            wpt_.append(a); B_wpt.append(b); o_wpt.append(o)
        hact_, B_hactsb, o_hactsb = [], [], []
        for i in range(2):
            a, b, o = sba(f"hact_sb{i}", 16 * 512, BF16, side="high")
            hact_.append(a); B_hactsb.append(b); o_hactsb.append(o)
        fchunk_, B_fchunk, o_fchunk = [], [], []
        for i in range(2):
            a, b, o = sba(f"fchunk{i}", 512, F32)
            fchunk_.append(a); B_fchunk.append(b); o_fchunk.append(o)
        ftr_, B_ftr, o_ftr = [], [], []
        for i in range(2):
            a, b, o = sba(f"ftr{i}", 512, F32)
            ftr_.append(a); B_ftr.append(b); o_ftr.append(o)
        B_ftok = Buf("ftok_dram")
        hact_v = hact_dram.rearrange("(j p) t -> p j t", p=128)

        pending_fb = []
        for dg in range(8):
            ws = dg % 2
            wpt3 = wpt_[ws].rearrange("p (j f) -> p j f", f=256)
            for j8 in range(4):
                p.dma("sc", wpt3[:, j8 * 16:(j8 + 1) * 16, :],
                      wpt_full.rearrange("(j p) d -> p j d", p=128)[
                          :, j8 * 16:(j8 + 1) * 16, dg * 256:(dg + 1) * 256],
                      reads=[B_wp_q], writes=[B_wpt[ws]], sem=None)
            for tt in range(4):
                b0 = (tt % 2) * 2
                for jhh in range(4):
                    hs = jhh % 2
                    hact3 = hact_[hs].rearrange("p (j f) -> p j f", f=512)
                    p.dma("sp", hact3[:, 0:16, :],
                          hact_v[:, jhh * 16:jhh * 16 + 16,
                                 tt * 512:(tt + 1) * 512],
                          reads=[B_hact_dram], writes=[B_hactsb[hs]], sem=None)
                    for dc in range(2):
                        for jh in range(16):
                            p.emit("pe", lambda ws=ws, b0=b0, dc=dc, jh=jh, jhh=jhh,
                                   wpt3=wpt3, hact3=hact3, hs=hs:
                                   nc.tensor.matmul(
                                psf(b0 + dc),
                                wpt3[:, jhh * 16 + jh, dc * 128:(dc + 1) * 128],
                                hact3[:, jh, :],
                                start=(jhh == 0 and jh == 0),
                                stop=(jhh == 3 and jh == 15)),
                                reads=[B_wpt[ws], B_hactsb[hs]], writes=[PSB[b0 + dc]])
                for fn in pending_fb:
                    fn()
                pending_fb = []

                def make_evac(dg=dg, tt=tt, b0=b0):
                    def go():
                        for dc in range(2):
                            d_abs = dg * 2 + dc
                            s = dc % 2
                            p.emit("ve", lambda b0=b0, dc=dc, s=s, tt=tt:
                                   nc.vector.tensor_mul(
                                out=fchunk_[s], in0=psf(b0 + dc),
                                in1=inv_bc[:, tt * 512:(tt + 1) * 512]),
                                reads=[PSB[b0 + dc], B_invbc], writes=[B_fchunk[s]])
                            bk = 4 + s
                            for c4 in range(4):
                                p.emit("pe", lambda s=s, bk=bk, c4=c4:
                                       nc.tensor.transpose(
                                    psf(bk)[:, c4 * 128:(c4 + 1) * 128],
                                    fchunk_[s][:, c4 * 128:(c4 + 1) * 128], id_f32),
                                    reads=[B_fchunk[s], B_idf], writes=[PSB[bk]])
                            p.emit("sc", lambda s=s, bk=bk: nc.scalar.copy(
                                out=ftr_[s], in_=psf(bk)),
                                reads=[PSB[bk]], writes=[B_ftr[s]])
                            p.dma("gp", ftok_dram[tt * 512:(tt + 1) * 512,
                                                  d_abs * 128:(d_abs + 1) * 128]
                                  .rearrange("(c t) f -> t c f", t=128),
                                  ftr_[s].rearrange("t (c f) -> t c f", f=128),
                                  reads=[B_ftr[s]], writes=[B_ftok], sem=None)
                    return go
                pending_fb.append(make_evac())
        for fn in pending_fb:
            fn()
        pending_fb = []

        if debug:
            p.dma("sp", dbg["dbg_ffn"], ftok_dram, reads=[B_ftok], writes=[Buf("dbg")], sem=None)

        for o in o_wpt + o_hactsb + o_fchunk + o_ftr + [o_invp, o_invbc]:
            A.release(o)

        fin_, B_fin, o_fin = [], [], []
        s2in_, B_s2in, o_s2in = [], [], []
        outt_, B_outt, o_outt = [], [], []
        for i in range(2):
            a, b, o = sba(f"fin{i}", D, F32)
            fin_.append(a); B_fin.append(b); o_fin.append(o)
            a, b, o = sba(f"s2in{i}", D, F32)
            s2in_.append(a); B_s2in.append(b); o_s2in.append(o)
            a, b, o = sba(f"outt{i}", D, F32)
            outt_.append(a); B_outt.append(b); o_outt.append(o)
        z4_, B_z4s, o_z4s = [], [], []
        for i in range(2):
            a, b, o = sba(f"z4_{i}", D, F32)
            z4_.append(a); B_z4s.append(b); o_z4s.append(o)
        B_out_dram = Buf("out_dram")

        def fcload(tc):
            s = tc % 2
            p.dma("sp", fin_[s], ftok_dram[tc * 128:(tc + 1) * 128, :],
                  reads=[B_ftok], writes=[B_fin[s]], sem=None)
            p.dma("sp", s2in_[s], s2_dram[tc * 128:(tc + 1) * 128, :],
                  reads=[B_s2_dram], writes=[B_s2in[s]], sem=None)
        fcload(0)
        for tc in range(16):
            s = tc % 2
            if tc + 1 < 16:
                fcload(tc + 1)
            ln_token_major(B_fin[s], fin_[s], z4_[s], B_z4s[s], LN_EPS, sl=s)
            p.emit("ve", lambda s=s: nc.vector.tensor_add(
                out=outt_[s], in0=z4_[s], in1=s2in_[s]),
                reads=[B_z4s[s], B_s2in[s]], writes=[B_outt[s]])
            p.dma("sp", out_d[tc * 128:(tc + 1) * 128, :], outt_[s],
                  reads=[B_outt[s]], writes=[B_out_dram], sem=None)

        p.eng["sp"].wait_ge(p.sem[B_out_dram.dsem], p.cnt[B_out_dram.dsem])
        nc.all_engine_barrier()

    return nc


_PROG_CACHE = {}


def kernel(L0_attn_x, L0_attn_Wq, L0_attn_bq, L0_attn_Wl, L0_attn_bl, L0_attn_Wk,
           L0_attn_bk, L0_attn_Wv, L0_attn_bv, L0_attn_Wq2kv, L0_attn_Wkv2h,
           L0_attn_bkv2h, L0_attn_Wo, L0_attn_bo, L0_ffn_w_gate_up, L0_ffn_w_proj):
    x = np.ascontiguousarray(np.asarray(L0_attn_x).astype(np.float32, copy=False))
    w = {
        "Wq": np.ascontiguousarray(np.asarray(L0_attn_Wq).astype(np.float32, copy=False)),
        "bq": np.ascontiguousarray(np.asarray(L0_attn_bq).astype(np.float32, copy=False)),
        "Wl": np.ascontiguousarray(np.asarray(L0_attn_Wl).astype(np.float32, copy=False)),
        "bl": np.ascontiguousarray(np.asarray(L0_attn_bl).astype(np.float32, copy=False)),
        "Wk": np.ascontiguousarray(np.asarray(L0_attn_Wk).astype(np.float32, copy=False)),
        "bk": np.ascontiguousarray(np.asarray(L0_attn_bk).astype(np.float32, copy=False)),
        "Wv": np.ascontiguousarray(np.asarray(L0_attn_Wv).astype(np.float32, copy=False)),
        "bv": np.ascontiguousarray(np.asarray(L0_attn_bv).astype(np.float32, copy=False)),
        "Wq2kv": np.ascontiguousarray(np.asarray(L0_attn_Wq2kv).astype(np.float32, copy=False)),
        "Wkv2h": np.ascontiguousarray(np.asarray(L0_attn_Wkv2h).astype(np.float32, copy=False)),
        "bkv2h": np.ascontiguousarray(np.asarray(L0_attn_bkv2h).astype(np.float32, copy=False)),
        "Wo": np.ascontiguousarray(np.asarray(L0_attn_Wo).astype(np.float32, copy=False)),
        "bo": np.ascontiguousarray(np.asarray(L0_attn_bo).astype(np.float32, copy=False)),
    }
    wgu = np.asarray(L0_ffn_w_gate_up).astype(np.float32, copy=False)
    wp = np.asarray(L0_ffn_w_proj).astype(np.float32, copy=False)
    if "nc" not in _PROG_CACHE:
        _PROG_CACHE["nc"] = build_program(debug=False)
    nc = _PROG_CACHE["nc"]
    gsh = 2 * HF // NCORES
    psh = HF // NCORES
    in_maps = [dict(w, x=np.ascontiguousarray(x[c]),
                    Wgu_sh=np.ascontiguousarray(wgu[c * gsh:(c + 1) * gsh]),
                    Wp_sh=np.ascontiguousarray(wp[:, c * psh:(c + 1) * psh]))
               for c in range(NCORES)]
    res = run_bass_kernel_spmd(nc, in_maps, core_ids=list(range(NCORES)))
    out = np.stack([res.results[c]["out"] for c in range(NCORES)], axis=0)
    return out.astype(np.float32)


# version 21
# speedup vs baseline: 1.0235x; 1.0235x over previous
import numpy as np
import concourse.bass as bass
import concourse.mybir as mybir
from concourse.bass_utils import run_bass_kernel_spmd
from concourse.bass import broadcast_tensor_aps

F32 = mybir.dt.float32
BF16 = mybir.dt.bfloat16
AF = mybir.ActivationFunctionType
ALU = mybir.AluOpType
AX = mybir.AxisListType

D = 2048
T = 2048
NH = 16
NKV = 4
KV = 64
LAT = 1024
HF = 8192
GROUP = 64
LN_EPS = 1e-5
RMS_EPS = float(np.finfo(np.float32).eps)
QTHR = 0.5 + 2.0 ** -10
NCORES = 8


def q_chunk(h):
    return (h % 4) + 4 * (h // 8)


def q_slot(h):
    return (h // 4) % 2


class Buf:
    __slots__ = ("lw", "rd", "name", "dsem")

    def __init__(self, name=""):
        self.lw = None
        self.rd = []
        self.name = name
        self.dsem = None


class Arena:

    def __init__(self, tensor, nbytes):
        self.t = tensor
        self.free = [(0, nbytes)]
        self.live = {}
        self.ghost = []

    def alloc(self, name, nbytes, dtype, width=None, side="low"):
        nbytes = (nbytes + 63) & ~63
        idxs = range(len(self.free)) if side == "low" else \
            range(len(self.free) - 1, -1, -1)
        for i in idxs:
            off, sz = self.free[i]
            if sz >= nbytes:
                if sz == nbytes:
                    self.free.pop(i)
                elif side == "low":
                    self.free[i] = (off + nbytes, sz - nbytes)
                else:
                    self.free[i] = (off, sz - nbytes)
                    off = off + sz - nbytes
                buf = Buf(name)
                keep = []
                for (go, gs, gb) in self.ghost:
                    if go < off + nbytes and off < go + gs:
                        if gb.lw is not None:
                            buf.rd.append(gb.lw)
                        buf.rd.extend(gb.rd)
                    else:
                        keep.append((go, gs, gb))
                self.ghost = keep
                self.live[off] = (nbytes, buf)
                ap = self.t[:, off:off + nbytes].bitcast(dtype)
                if width is not None:
                    ap = ap[:, 0:width]
                return ap, buf, off
        raise MemoryError(f"SBUF arena: cannot alloc {nbytes} for {name}; "
                          f"free={self.free}")

    def release(self, off):
        sz, buf = self.live.pop(off)
        self.ghost.append((off, sz, buf))
        self.free.append((off, sz))
        self.free.sort()
        out = []
        for seg in self.free:
            if out and out[-1][0] + out[-1][1] == seg[0]:
                out[-1] = (out[-1][0], out[-1][1] + seg[1])
            else:
                out.append(list(seg) if False else (seg[0], seg[1]))
        merged = []
        for seg in out:
            if merged and merged[-1][0] + merged[-1][1] == seg[0]:
                merged[-1] = (merged[-1][0], merged[-1][1] + seg[1])
            else:
                merged.append(seg)
        self.free = merged


class Prog:
    def __init__(self, nc):
        self.nc = nc
        self.eng = {
            "pe": nc.tensor, "sc": nc.scalar, "ve": nc.vector,
            "gp": nc.gpsimd, "sp": nc.sync,
        }
        self.sem = {}
        self.cnt = {}
        self.waited = {e: {} for e in self.eng}
        for e in ("pe", "sc", "ve", "gp"):
            self.new_sem(e)

    def new_sem(self, name):
        h = self.nc.alloc_semaphore(name)
        self.sem[name] = h
        self.cnt[name] = 0
        return name

    def make_dma_pool(self, n):
        self.dma_pool = [self.new_sem(f"dpool{i}") for i in range(n)]
        self.dma_pool_next = 0

    def clear_sems_and_barrier(self):
        for h in self.sem.values():
            self.nc.gpsimd.sem_clear(h)
        self.nc.all_engine_barrier()

    def wait(self, ename, tok):
        if tok is None:
            return
        s, v = tok
        w = self.waited[ename]
        if w.get(s, 0) >= v:
            return
        self.eng[ename].wait_ge(self.sem[s], v)
        w[s] = v

    def emit(self, ename, fn, reads=(), writes=(), sem=None, dinc=1):
        for b in reads:
            if b.lw is not None:
                self.wait(ename, b.lw)
        for b in writes:
            if b.lw is not None and b.lw[0] != ename:
                self.wait(ename, b.lw)
            for t in b.rd:
                if t[0] != ename:
                    self.wait(ename, t)
        ins = fn()
        sname = sem if sem is not None else ename
        self.cnt[sname] += dinc
        ins.then_inc(self.sem[sname], dinc)
        tok = (sname, self.cnt[sname])
        for b in writes:
            b.lw = tok
            b.rd = []
        for b in reads:
            b.rd.append(tok)
        return tok

    def dma(self, qname, out, in_, reads=(), writes=(), sem=None, **kw):
        assert len(writes) == 1
        w = writes[0]
        if w.dsem is None:
            assert self.dma_pool_next < len(self.dma_pool), "dma sem pool exhausted"
            w.dsem = self.dma_pool[self.dma_pool_next]
            self.dma_pool_next += 1
        return self.emit(
            qname, lambda: self.eng[qname].dma_start(out=out, in_=in_, **kw),
            reads=reads, writes=writes, sem=w.dsem, dinc=16)


def ceil_div(a, b):
    return -(-a // b)


def build_program(debug=False):
    nc = bass.Bass("TRN2", target_bir_lowering=False)
    p = Prog(nc)

    x_d = nc.dram_tensor("x", [T, D], F32, kind="ExternalInput").ap()
    wq_d = nc.dram_tensor("Wq", [D, D], F32, kind="ExternalInput").ap()
    bq_d = nc.dram_tensor("bq", [D], F32, kind="ExternalInput").ap()
    wl_d = nc.dram_tensor("Wl", [D, LAT], F32, kind="ExternalInput").ap()
    bl_d = nc.dram_tensor("bl", [LAT], F32, kind="ExternalInput").ap()
    wk_d = nc.dram_tensor("Wk", [LAT, NKV * KV], F32, kind="ExternalInput").ap()
    bk_d = nc.dram_tensor("bk", [NKV * KV], F32, kind="ExternalInput").ap()
    wv_d = nc.dram_tensor("Wv", [LAT, NKV * KV], F32, kind="ExternalInput").ap()
    bv_d = nc.dram_tensor("bv", [NKV * KV], F32, kind="ExternalInput").ap()
    wq2kv_d = nc.dram_tensor("Wq2kv", [128, KV], F32, kind="ExternalInput").ap()
    wkv2h_d = nc.dram_tensor("Wkv2h", [KV, 128], F32, kind="ExternalInput").ap()
    bkv2h_d = nc.dram_tensor("bkv2h", [128], F32, kind="ExternalInput").ap()
    wo_d = nc.dram_tensor("Wo", [D, D], F32, kind="ExternalInput").ap()
    bo_d = nc.dram_tensor("bo", [D], F32, kind="ExternalInput").ap()
    wgu_sh_d = nc.dram_tensor("Wgu_sh", [2 * HF // 8, D], F32, kind="ExternalInput").ap()
    wp_sh_d = nc.dram_tensor("Wp_sh", [D, HF // 8], F32, kind="ExternalInput").ap()
    out_d = nc.dram_tensor("out", [T, D], F32, kind="ExternalOutput").ap()

    wqp_blk = nc.dram_tensor("wqp_blk", [8, 128, 16, 128], BF16, kind="Internal").ap()
    wq_bf = nc.dram_tensor("wq_bf", [D, D], BF16, kind="Internal").ap()
    wl_blk = nc.dram_tensor("wl_blk", [8, 128, 16, 128], BF16, kind="Internal").ap()
    wk_blk = nc.dram_tensor("wk_blk", [2, 128, 8, 128], BF16, kind="Internal").ap()
    wv_nat = nc.dram_tensor("wv_nat", [LAT, NKV * KV], BF16, kind="Internal").ap()
    wpr_nat = nc.dram_tensor("wpr_nat", [LAT, D], BF16, kind="Internal").ap()
    wgu_q = nc.dram_tensor("wgu_q", [2 * HF, D], BF16, kind="Internal", addr_space="Shared").ap()
    wguq_sh = nc.dram_tensor("wguq_sh", [2 * HF // 8, D], BF16, kind="Internal").ap()
    wpqt_sh = nc.dram_tensor("wpqt_sh", [HF // 8, D], BF16, kind="Internal").ap()
    wpt_full = nc.dram_tensor("wpt_full", [HF, D], BF16, kind="Internal",
                              addr_space="Shared").ap()
    s2_dram = nc.dram_tensor("s2_dram", [T, D], F32, kind="Internal").ap()
    ln3t_dram = nc.dram_tensor("ln3t_dram", [D, T], BF16, kind="Internal").ap()
    hact_dram = nc.dram_tensor("hact_dram", [HF, T], BF16, kind="Internal").ap()
    ftok_dram = nc.dram_tensor("ftok_dram", [T, D], F32, kind="Internal").ap()

    dbg = {}
    if debug:
        for name, shape, dt in [
            ("dbg_ln1t", [D, T], BF16), ("dbg_q2t", [LAT, T], BF16),
            ("dbg_latt", [LAT, T], BF16), ("dbg_kt", [256, T], BF16),
            ("dbg_v", [T, NKV * KV], BF16), ("dbg_ctxt", [LAT, T], BF16),
            ("dbg_s2", [T, D], F32), ("dbg_ln3t", [D, T], BF16),
            ("dbg_wguq", [2 * HF, D], BF16), ("dbg_wpq", [D, HF], BF16),
            ("dbg_hact", [HF, T], BF16), ("dbg_ffn", [T, D], F32),
        ]:
            dbg[name] = nc.dram_tensor(name, shape, dt, kind="ExternalOutput").ap()

    from contextlib import ExitStack
    es = ExitStack()
    with es:
        SBN = 211968
        SBt = es.enter_context(nc.sbuf_tensor("SB", [128, SBN], mybir.dt.uint8))
        PSt = es.enter_context(nc.psum_tensor("PS", [128, 4096], F32))
        A = Arena(SBt, SBN)

        def psf(b0, nb=1):
            return PSt[:, b0 * 512:(b0 + nb) * 512]

        def psb(b0, nb=1):
            return PSt[:, b0 * 512:(b0 + nb) * 512].bitcast(BF16)

        PSB = [Buf(f"psum{i}") for i in range(8)]
        p.make_dma_pool(80)
        p.clear_sems_and_barrier()


        def sba(name, nelem, dtype, side="low"):
            sz = nelem * (2 if dtype == BF16 else 4)
            ap, buf, off = A.alloc(name, sz, dtype, width=nelem, side=side)
            return ap, buf, off

        id_bf, B_id, _ = sba("id_bf", 128, BF16)
        id_f32, B_idf, _ = sba("id_f32", 128, F32)
        mask_t, B_mask, _ = sba("mask_t", 128, F32)
        ones_row_f, B_ones, _ = sba("ones_row_f", 128, F32)
        bias_pack, B_bias, _ = sba("bias_pack", 64, F32)
        bv_bc, B_bvbc, o_bvbc = sba("bv_bc", 256, F32)
        bpp_bc, B_bppbc, o_bppbc = sba("bpp_bc", D, F32)
        borow_t, B_borow, o_bo = sba("bo_row", D, F32)
        bpprow_t, B_bpprow, o_bpp = sba("bpp_row", D, F32)
        bvrow_t, B_bvrow, o_bv = sba("bv_row", 256, F32)
        bo_row = borow_t[0:1, :]
        bpp_row = bpprow_t[0:1, :]
        bv_row = bvrow_t[0:1, :]
        wq2kv_bf, B_w2kv, _ = sba("wq2kv_bf", KV, BF16)
        wq2kv_f32, B_w2kvf, _ = sba("wq2kv_f32", KV, F32)
        wkv2h_f, B_wkvf, _ = sba("wkv2h_f", 128, BF16)
        wkv2h_t, B_wkvt, _ = sba("wkv2h_t", KV, BF16)
        bkv_bf, B_bkvbf, _ = sba("bkv_bf", 1, BF16)
        lnstat, B_lnstat, _ = sba("lnstat", 8, F32)
        eps_col, B_eps, _ = sba("eps_col", 1, F32)
        lnstat_, B_lnstat_ = [], []
        for i in range(2):
            a, b, o = sba(f"lnstat{i}", 32, F32)
            lnstat_.append(a); B_lnstat_.append(b)
        trev_, B_trev, o_trev = [], [], []
        for i in range(2):
            a, b, o = sba(f"trev{i}", 512, BF16)
            trev_.append(a); B_trev.append(b); o_trev.append(o)
        NQSLOT = 2
        QW = 1024
        qrow_, B_qrow, o_qrow = [], [], []
        qmisc = {}
        for i in range(NQSLOT):
            a, b, o = sba(f"qrow{i}", QW, BF16)
            qrow_.append(a); B_qrow.append(b); o_qrow.append(o)
        for nm, ne, dt in [("qsc", 16, F32), ("qscb", 16, BF16), ("qthr", 16, F32),
                           ("qaw", QW, BF16), ("qmask", QW, BF16), ("qsgn", QW, BF16),
                           ("qwh", QW, BF16)]:
            qmisc[nm] = []
            for i in range(NQSLOT):
                a, b, o = sba(f"{nm}{i}", ne, dt)
                qmisc[nm].append((a, b, o))
        qtr_, B_qtr = [], []
        for i in range(2):
            a, b, o = sba(f"qtr{i}", 1024, BF16)
            qtr_.append(a); B_qtr.append(b)
        smax, B_smax, _ = sba("smax", 4, F32)

        def make_identity(ident, bbuf):
            p.emit("gp", lambda: nc.gpsimd.memset(ident, 0.0), writes=[bbuf])
            p.emit("gp", lambda: nc.gpsimd.affine_select(
                out=ident, in_=ident, compare_op=ALU.not_equal, fill=1.0,
                base=0, pattern=[[-1, 128]], channel_multiplier=1),
                reads=[bbuf], writes=[bbuf])

        make_identity(id_bf, B_id)
        make_identity(id_f32, B_idf)
        p.emit("gp", lambda: nc.gpsimd.memset(mask_t, 0.0), writes=[B_mask])
        p.emit("gp", lambda: nc.gpsimd.affine_select(
            out=mask_t, in_=mask_t, compare_op=ALU.is_ge, fill=-30000.0,
            base=0, pattern=[[-1, 128]], channel_multiplier=1),
            reads=[B_mask], writes=[B_mask])
        p.emit("gp", lambda: nc.gpsimd.memset(ones_row_f, 1.0), writes=[B_ones])
        p.emit("gp", lambda: nc.gpsimd.memset(eps_col, LN_EPS), writes=[B_eps])

        with nc.allow_non_contiguous_dma("tiny bias column loads"):
            p.dma("sp", bias_pack[:, 0:16], bq_d.rearrange("(n p) -> p n", p=128),
                  writes=[B_bias], sem=None)
            p.dma("sp", bias_pack[:, 16:24], bl_d.rearrange("(n p) -> p n", p=128),
                  writes=[B_bias], sem=None)
            p.dma("sp", bias_pack[:, 24:26], bk_d.rearrange("(n p) -> p n", p=128),
                  writes=[B_bias], sem=None)
            p.dma("sp", bias_pack[:, 34:35], bkv2h_d.rearrange("(n p) -> p n", p=128),
                  writes=[B_bias], sem=None)
        p.dma("sp", bv_row, bv_d.rearrange("(o n) -> o n", o=1),
              writes=[B_bvrow], sem=None)
        p.dma("sp", bo_row, bo_d.rearrange("(o n) -> o n", o=1),
              writes=[B_borow], sem=None)
        p.dma("gp", wq2kv_bf, wq2kv_d, writes=[B_w2kv], sem=None)
        p.dma("sp", wq2kv_f32, wq2kv_d, writes=[B_w2kvf], sem=None)
        p.dma("gp", wkv2h_f[0:64, :], wkv2h_d, writes=[B_wkvf], sem=None)

        p.emit("pe", lambda: nc.tensor.matmul(
            psf(0)[:, 0:256], ones_row_f[0:1, :], bv_row,
            start=True, stop=True), reads=[B_ones, B_bvrow], writes=[PSB[0]])
        p.emit("ve", lambda: nc.vector.tensor_copy(out=bv_bc, in_=psf(0)[:, 0:256]),
               reads=[PSB[0]], writes=[B_bvbc])

        p.emit("pe", lambda: nc.tensor.transpose(
            psb(1)[:, 0:64], wkv2h_f[0:64, :], id_bf[0:64, 0:64]),
            reads=[B_wkvf, B_id], writes=[PSB[1]])
        p.emit("sc", lambda: nc.scalar.copy(out=wkv2h_t, in_=psb(1)[:, 0:64]),
               reads=[PSB[1]], writes=[B_wkvt])
        p.emit("ve", lambda: nc.vector.tensor_copy(
            out=bkv_bf, in_=bias_pack[:, 34:35]), reads=[B_bias], writes=[B_bkvbf])

        for h in range(NH):
            p.emit("pe", lambda h=h: nc.tensor.matmul(
                psf(2)[0:64, h:h + 1],
                wq2kv_f32, bias_pack[:, h:h + 1],
                start=True, stop=True), reads=[B_w2kvf, B_bias], writes=[PSB[2]])
        p.emit("ve", lambda: nc.vector.tensor_copy(
            out=bias_pack[0:64, 40:56], in_=psf(2)[0:64, 0:16]),
            reads=[PSB[2]], writes=[B_bias])
        for h in range(NH):
            c, sl = q_chunk(h), q_slot(h)
            p.emit("ve", lambda h=h, c=c, sl=sl: nc.vector.tensor_copy(
                out=bias_pack[sl * 64:sl * 64 + 64, 26 + c:27 + c],
                in_=bias_pack[0:64, 40 + h:41 + h]),
                reads=[B_bias], writes=[B_bias])

        wrow0, B_wr0, o_wr0 = sba("wrow0", D, BF16)
        wrow1, B_wr1, o_wr1 = sba("wrow1", D, BF16)
        wrow = [wrow0, wrow1]
        B_wrow = [B_wr0, B_wr1]
        B_wq_bf = Buf("wq_bf_dram")
        B_wl_blk = Buf("wl_blk_dram")
        B_wk_blk = Buf("wk_blk_dram")
        B_wv_nat = Buf("wv_nat_dram")
        B_wpr = Buf("wpr_dram")

        for k in range(16):
            s = k % 2
            p.dma("gp", wrow[s], wq_d[k * 128:(k + 1) * 128, :],
                  writes=[B_wrow[s]], sem=None)
            p.dma("sp", wq_bf[k * 128:(k + 1) * 128, :], wrow[s],
                  reads=[B_wrow[s]], writes=[B_wq_bf], sem=None)
        for k in range(16):
            s = k % 2
            p.dma("gp", wrow[s][:, 0:LAT], wl_d[k * 128:(k + 1) * 128, :],
                  writes=[B_wrow[s]], sem=None)
            p.dma("sp", wl_blk[:, :, k, :].rearrange("n p f -> p n f"),
                  wrow[s][:, 0:LAT].rearrange("p (n f) -> p n f", f=128),
                  reads=[B_wrow[s]], writes=[B_wl_blk], sem=None)
        for k in range(8):
            s = k % 2
            p.dma("gp", wrow[s][:, 0:256], wk_d[k * 128:(k + 1) * 128, :],
                  writes=[B_wrow[s]], sem=None)
            p.dma("sp", wk_blk[:, :, k, :].rearrange("n p f -> p n f"),
                  wrow[s][:, 0:256].rearrange("p (n f) -> p n f", f=128),
                  reads=[B_wrow[s]], writes=[B_wk_blk], sem=None)
        for k in range(8):
            s = k % 2
            p.dma("gp", wrow[s][:, 0:256], wv_d[k * 128:(k + 1) * 128, :],
                  writes=[B_wrow[s]], sem=None)
            p.dma("sp", wv_nat[k * 128:(k + 1) * 128, :], wrow[s][:, 0:256],
                  reads=[B_wrow[s]], writes=[B_wv_nat], sem=None)

        wpr_ev, B_wprev, o_wprev = sba("wpr_ev", D, BF16)
        p.emit("ve", lambda: nc.vector.tensor_copy(
            out=bkv_bf, in_=bias_pack[:, 34:35]), reads=[B_bias], writes=[B_bkvbf])
        p.emit("ve", lambda: nc.vector.tensor_copy(
            out=bpp_row, in_=bo_row), reads=[B_borow], writes=[B_bpprow])

        def prep_wo_step(h):
            s = h % 2
            p.dma("gp", wrow[s], wo_d[h * 128:(h + 1) * 128, :],
                  writes=[B_wrow[s]], sem=None)
            for n in range(4):
                p.emit("pe", lambda s=s, n=n: nc.tensor.matmul(
                    psf(n)[0:64, :], wkv2h_t, wrow[s][:, n * 512:(n + 1) * 512],
                    start=True, stop=True),
                    reads=[B_wrow[s], B_wkvt], writes=[PSB[n]])
                p.emit("pe", lambda s=s, n=n: nc.tensor.matmul(
                    psf(7)[0:1, :], bkv_bf, wrow[s][:, n * 512:(n + 1) * 512],
                    start=True, stop=True),
                    reads=[B_wrow[s], B_bkvbf], writes=[PSB[7]])
                p.emit("ve", lambda n=n: nc.vector.tensor_add(
                    out=bpp_row[:, n * 512:(n + 1) * 512],
                    in0=bpp_row[:, n * 512:(n + 1) * 512], in1=psf(7)[0:1, :]),
                    reads=[PSB[7], B_bpprow], writes=[B_bpprow])
            p.emit("sc", lambda: nc.scalar.copy(
                out=wpr_ev[0:64, :], in_=PSt[0:64, 0:2048]),
                reads=[PSB[0], PSB[1], PSB[2], PSB[3]], writes=[B_wprev])
            p.dma("sp", wpr_nat[h * 64:(h + 1) * 64, :], wpr_ev[0:64, :],
                  reads=[B_wprev], writes=[B_wpr], sem=None)

        def prep_bpp_bcast():
            for n in range(4):
                p.emit("pe", lambda n=n: nc.tensor.matmul(
                    psf(n)[:, :], ones_row_f[0:1, :],
                    bpp_row[:, n * 512:(n + 1) * 512],
                    start=True, stop=True), reads=[B_ones, B_bpprow],
                    writes=[PSB[n]])
                p.emit("ve", lambda n=n: nc.vector.tensor_copy(
                    out=bpp_bc[:, n * 512:(n + 1) * 512], in_=psf(n)[:, :]),
                    reads=[PSB[n]], writes=[B_bppbc])

        wqth0, B_wqth0, o_wqth0 = sba("wqth0", D, BF16)
        wqth1, B_wqth1, o_wqth1 = sba("wqth1", D, BF16)
        wqth = [wqth0, wqth1]
        B_wqth = [B_wqth0, B_wqth1]
        qp_ev, B_qpev, o_qpev = sba("qp_ev", 16 * 128, BF16)
        B_wqp_blk = Buf("wqp_blk_dram")
        def prep_wqp_step(c):
            for sl in range(2):
                h = (c % 4) + 4 * sl + 8 * (c // 4)
                s = (2 * c + sl) % 2
                p.dma("sp", wqth[s], wq_bf[:, h * 128:(h + 1) * 128],
                      reads=[B_wq_bf], writes=[B_wqth[s]], sem=None, transpose=True)
                for k in range(16):
                    bk = k % 2
                    p.emit("pe", lambda s=s, k=k, bk=bk: nc.tensor.matmul(
                        psf(bk)[:, 0:64],
                        wqth[s][:, k * 128:(k + 1) * 128],
                        wq2kv_bf, start=True, stop=True),
                        reads=[B_wqth[s], B_w2kv], writes=[PSB[bk]])
                    p.emit("sc", lambda k=k, sl=sl, bk=bk: nc.scalar.copy(
                        out=qp_ev[:, k * 128 + sl * 64:k * 128 + sl * 64 + 64],
                        in_=psf(bk)[:, 0:64]),
                        reads=[PSB[bk]], writes=[B_qpev])
            p.dma("sp", wqp_blk[c], qp_ev.rearrange("p (k f) -> p k f", f=128),
                  reads=[B_qpev], writes=[B_wqp_blk], sem=None)

        prep_steps = ([(lambda h=h: prep_wo_step(h)) for h in range(NH)]
                      + [prep_bpp_bcast]
                      + [(lambda c=c: prep_wqp_step(c)) for c in range(8)])
        pi_state = {"i": 0}

        def emit_prep(n):
            for _ in range(n):
                if pi_state["i"] >= len(prep_steps):
                    return
                prep_steps[pi_state["i"]]()
                pi_state["i"] += 1



        B_wgu_q = Buf("wgu_q_dram")
        B_wp_q = Buf("wp_q_dram")

        def quant_load(src_ap, s):
            p.dma("gp", qrow_[s], src_ap, writes=[B_qrow[s]], sem=None)

        def quant_tile(src_ap, dst_ap, dst_buf, s, transpose_to=None):
            qr, Bqr = qrow_[s], B_qrow[s]
            qsc, Bqsc = qmisc["qsc"][s][0], qmisc["qsc"][s][1]
            qscb, Bqscb = qmisc["qscb"][s][0], qmisc["qscb"][s][1]
            qthr, Bqthr = qmisc["qthr"][s][0], qmisc["qthr"][s][1]
            qaw, Bqaw = qmisc["qaw"][s][0], qmisc["qaw"][s][1]
            qmask, Bqmask = qmisc["qmask"][s][0], qmisc["qmask"][s][1]
            qsgn, Bqsgn = qmisc["qsgn"][s][0], qmisc["qsgn"][s][1]
            qwh, Bqwh = qmisc["qwh"][s][0], qmisc["qwh"][s][1]
            p.emit("ve", lambda: nc.vector.tensor_reduce(
                out=qsc, in_=qr.rearrange("p (g w) -> p g w", w=GROUP),
                axis=AX.X, op=ALU.add, apply_absolute_value=True),
                reads=[Bqr], writes=[Bqsc])
            p.emit("ve", lambda: nc.vector.tensor_scalar(
                out=qscb, in0=qsc, scalar1=1.0 / GROUP, scalar2=1e-8,
                op0=ALU.mult, op1=ALU.max), reads=[Bqsc], writes=[Bqscb])
            p.emit("ve", lambda: nc.vector.tensor_scalar(
                out=qthr, in0=qscb, scalar1=QTHR, scalar2=None, op0=ALU.mult),
                reads=[Bqscb], writes=[Bqthr])
            p.emit("sc", lambda: nc.scalar.activation(out=qaw, in_=qr, func=AF.Abs),
                   reads=[Bqr], writes=[Bqaw])
            p.emit("sc", lambda: nc.scalar.activation(out=qsgn, in_=qr, func=AF.Sign),
                   reads=[Bqr], writes=[Bqsgn])
            a3, t3 = broadcast_tensor_aps(
                qaw.rearrange("p (g w) -> p g w", w=GROUP),
                qthr.rearrange("p (g w) -> p g w", w=1))
            p.emit("ve", lambda: nc.vector.tensor_tensor(
                out=qmask.rearrange("p (g w) -> p g w", w=GROUP),
                in0=a3, in1=t3, op=ALU.is_gt),
                reads=[Bqaw, Bqthr], writes=[Bqmask])
            m3, s3 = broadcast_tensor_aps(
                qmask.rearrange("p (g w) -> p g w", w=GROUP),
                qscb.rearrange("p (g w) -> p g w", w=1))
            p.emit("gp", lambda: nc.gpsimd.tensor_tensor(
                out=qmask.rearrange("p (g w) -> p g w", w=GROUP),
                in0=m3, in1=s3, op=ALU.mult),
                reads=[Bqmask, Bqscb], writes=[Bqmask])
            p.emit("ve", lambda: nc.vector.tensor_mul(
                out=qwh, in0=qmask, in1=qsgn), reads=[Bqmask, Bqsgn], writes=[Bqwh])
            p.emit("ve", lambda: nc.vector.tensor_sub(
                out=qwh, in0=qwh, in1=qr), reads=[Bqwh, Bqr], writes=[Bqwh])
            p.emit("ve", lambda: nc.vector.tensor_add(
                out=qwh, in0=qwh, in1=qr), reads=[Bqwh, Bqr], writes=[Bqwh])
            if transpose_to is None:
                p.dma("sp", dst_ap, qwh, reads=[Bqwh], writes=[dst_buf], sem=None)
            else:
                for hc in range(QW // 128):
                    p.emit("pe", lambda hc=hc: nc.tensor.transpose(
                        psb(7)[:, hc * 128:(hc + 1) * 128],
                        qwh[:, hc * 128:(hc + 1) * 128], id_bf),
                        reads=[Bqwh, B_id], writes=[PSB[7]])
                p.emit("ve", lambda s=s: nc.vector.tensor_copy(
                    out=qtr_[s], in_=psb(7)[:, 0:QW]),
                    reads=[PSB[7]], writes=[B_qtr[s]])
                p.dma("sp", transpose_to,
                      qtr_[s].rearrange("p (c f) -> p c f", f=128),
                      reads=[B_qtr[s]], writes=[dst_buf], sem=None)

        B_wguq_sh = Buf("wguq_sh_dram")
        B_wpq_sh = Buf("wpq_sh_dram")
        quant_steps = []
        for r in range(2 * HF // 8 // 128):
            for c in range(D // QW):
                quant_steps.append((wgu_sh_d[r * 128:(r + 1) * 128, c * QW:(c + 1) * QW],
                                    wguq_sh[r * 128:(r + 1) * 128, c * QW:(c + 1) * QW],
                                    B_wguq_sh))
        for r in range(D // 128):
            dst3 = wpqt_sh.rearrange("(c p) d -> p c d", p=128)[
                :, :, r * 128:(r + 1) * 128]
            quant_steps.append((wp_sh_d[r * 128:(r + 1) * 128, :], None,
                                B_wpq_sh, dst3))
        qi_state = {"i": 0, "loaded": 0}
        QLOOK = NQSLOT - 1

        def emit_quant(n):
            for _ in range(n):
                i = qi_state["i"]
                if i >= len(quant_steps):
                    return
                while (qi_state["loaded"] < len(quant_steps)
                       and qi_state["loaded"] <= i + QLOOK):
                    j = qi_state["loaded"]
                    quant_load(quant_steps[j][0], j % NQSLOT)
                    qi_state["loaded"] += 1
                step = quant_steps[i]
                if len(step) == 3:
                    quant_tile(step[0], step[1], step[2], i % NQSLOT)
                else:
                    quant_tile(step[0], None, step[2], i % NQSLOT,
                               transpose_to=step[3])
                qi_state["i"] += 1


        def ln_token_major(x_buf, x_ap, out_ap, out_buf, eps, sl=0):
            lnstat = lnstat_[sl]
            B_lnstat = B_lnstat_[sl]
            for c4 in range(4):
                p.emit("ve", lambda c4=c4: nc.vector.bn_stats(
                    out=lnstat[:, 8 + c4 * 6:14 + c4 * 6],
                    in_=x_ap[:, c4 * 512:(c4 + 1) * 512]),
                    reads=[x_buf], writes=[B_lnstat])
            p.emit("ve", lambda: nc.vector.bn_aggr(
                out=lnstat[:, 0:2], in_=lnstat[:, 8:32]),
                reads=[B_lnstat], writes=[B_lnstat])
            p.emit("sc", lambda: nc.scalar.activation(
                out=lnstat[:, 7:8], in_=lnstat[:, 1:2], func=AF.Ln, bias=eps_col),
                reads=[B_lnstat, B_eps], writes=[B_lnstat])
            p.emit("sc", lambda: nc.scalar.activation(
                out=lnstat[:, 4:5], in_=lnstat[:, 7:8], func=AF.Exp, scale=-0.5),
                reads=[B_lnstat], writes=[B_lnstat])
            p.emit("ve", lambda: nc.vector.tensor_scalar(
                out=lnstat[:, 5:6], in0=lnstat[:, 0:1], scalar1=lnstat[:, 4:5],
                scalar2=-1.0, op0=ALU.mult, op1=ALU.mult),
                reads=[B_lnstat], writes=[B_lnstat])
            p.emit("sc", lambda: nc.scalar.activation(
                out=out_ap, in_=x_ap, func=AF.Identity,
                bias=lnstat[:, 5:6], scale=lnstat[:, 4:5]),
                reads=[x_buf, B_lnstat], writes=[out_buf])

        ln1t, B_ln1t, o_ln1t = sba("ln1t", 16 * T, BF16, side="high")
        ln1t3 = ln1t.rearrange("p (c t) -> p c t", t=T)
        xrow_, B_xrow, o_xrow = [], [], []
        for i in range(3):
            a, b, o = sba(f"xrow{i}", D, F32)
            xrow_.append(a); B_xrow.append(b); o_xrow.append(o)
        lntok_, B_lntok, o_lntok = [], [], []
        for i in range(2):
            a, b, o = sba(f"lntok{i}", D, BF16)
            lntok_.append(a); B_lntok.append(b); o_lntok.append(o)

        def transpose_to_feat(src_ap, src_buf, dst3, dst_buf, tc, banks=(5, 6)):
            for cg in range(4):
                bk = banks[cg % 2]
                for c4 in range(4):
                    c = cg * 4 + c4
                    p.emit("pe", lambda bk=bk, c4=c4, c=c: nc.tensor.transpose(
                        psb(bk)[:, c4 * 128:(c4 + 1) * 128],
                        src_ap[:, c * 128:(c + 1) * 128], id_bf),
                        reads=[src_buf, B_id], writes=[PSB[bk]])
                out3 = dst3[:, cg * 4:(cg + 1) * 4, tc * 128:(tc + 1) * 128]
                p.emit("ve", lambda bk=bk, out3=out3: nc.vector.tensor_copy(
                    out=out3, in_=psb(bk)[:, 0:512].rearrange("p (c t) -> p c t", t=128)),
                    reads=[PSB[bk]], writes=[dst_buf])

        def xload(tc):
            p.dma("sp", xrow_[tc % 3], x_d[tc * 128:(tc + 1) * 128, :],
                  writes=[B_xrow[tc % 3]], sem=None)
        xload(0)
        xload(1)
        for tc in range(16):
            s = tc % 3
            sp2 = tc % 2
            if tc + 2 < 16:
                xload(tc + 2)
            ln_token_major(B_xrow[s], xrow_[s], lntok_[sp2], B_lntok[sp2],
                           LN_EPS, sl=sp2)
            transpose_to_feat(lntok_[sp2], B_lntok[sp2], ln1t3, B_ln1t, tc)
            emit_quant(1)
            emit_prep(2)

        if debug:
            p.dma("sp", dbg["dbg_ln1t"].rearrange("(c p) t -> p (c t)", p=128),
                  ln1t, reads=[B_ln1t], writes=[Buf("dbg")], sem=None)

        emit_prep(len(prep_steps))
        for o in [o_wqth0, o_wqth1, o_qpev, o_wprev, o_wr0, o_wr1,
                  o_bo, o_bpp, o_bv]:
            A.release(o)
        for o in o_xrow + o_lntok:
            A.release(o)
        q2t, B_q2t, o_q2t = sba("q2t", 8 * T, BF16, side="high")
        latt, B_latt, o_latt = sba("latt", 8 * T, BF16, side="high")
        kt, B_kt, o_kt = sba("kt", 2 * T, BF16, side="high")
        vtok, B_vtok, o_vtok = sba("vtok", 16 * 256, BF16, side="high")
        q2t3 = q2t.rearrange("p (c t) -> p c t", t=T)
        latt3 = latt.rearrange("p (c t) -> p c t", t=T)
        kt3 = kt.rearrange("p (c t) -> p c t", t=T)
        wblk_, B_wblk, o_wblk = [], [], []
        for i in range(3):
            a, b, o = sba(f"wblk{i}", 16 * 128, BF16)
            wblk_.append(a); B_wblk.append(b); o_wblk.append(o)

        def proj_feat(n_groups, n_k, load_fn, load_buf, rhs3, rhs_buf, out_sb,
                      out_buf, bias_col_fn, act, qsteps=0):
            def wload(n):
                s = n % 3
                p.dma("sp", wblk_[s][:, 0:n_k * 128].rearrange(
                    "p (k f) -> p k f", f=128), load_fn(n),
                    reads=[load_buf], writes=[B_wblk[s]], sem=None)
            wload(0)
            if n_groups > 1:
                wload(1)
            for n in range(n_groups):
                s = n % 3
                if n + 2 < n_groups:
                    wload(n + 2)
                for tq in range(4):
                    bk = tq % 4
                    for k in range(n_k):
                        p.emit("pe", lambda s=s, bk=bk, k=k, tq=tq: nc.tensor.matmul(
                            psf(bk), wblk_[s][:, k * 128:(k + 1) * 128],
                            rhs3[:, k, tq * 512:(tq + 1) * 512],
                            start=(k == 0), stop=(k == n_k - 1)),
                            reads=[B_wblk[s], rhs_buf], writes=[PSB[bk]])
                    p.emit("sc", lambda bk=bk, n=n, tq=tq: nc.scalar.activation(
                        out=out_sb[:, n * T + tq * 512:n * T + (tq + 1) * 512],
                        in_=psf(bk), func=act, bias=bias_col_fn(n)),
                        reads=[PSB[bk], B_bias], writes=[out_buf])
                emit_quant(qsteps)

        proj_feat(8, 16, lambda n: wqp_blk[n], B_wqp_blk, ln1t3, B_ln1t,
                  q2t, B_q2t, lambda n: bias_pack[:, 26 + n:27 + n], AF.Identity,
                  qsteps=1)
        proj_feat(8, 16, lambda n: wl_blk[n, :, :, :], B_wl_blk, ln1t3, B_ln1t,
                  latt, B_latt, lambda n: bias_pack[:, 16 + n:17 + n], AF.Silu,
                  qsteps=1)
        proj_feat(2, 8, lambda n: wk_blk[n, :, :, :], B_wk_blk, latt3, B_latt,
                  kt, B_kt, lambda n: bias_pack[:, 24 + n:25 + n], AF.Identity,
                  qsteps=1)

        p.dma("sp", wblk_[0][:, 0:8 * 256].rearrange("p (k f) -> p k f", f=256),
              wv_nat.rearrange("(k p) f -> p k f", p=128),
              reads=[B_wv_nat], writes=[B_wblk[0]], sem=None)
        wv3 = wblk_[0][:, 0:8 * 256].rearrange("p (k f) -> p k f", f=256)
        for tv in range(16):
            bk = 2 + tv % 2
            for k in range(8):
                p.emit("pe", lambda bk=bk, k=k, tv=tv: nc.tensor.matmul(
                    psf(bk)[:, 0:256], latt3[:, k, tv * 128:(tv + 1) * 128],
                    wv3[:, k, :], start=(k == 0), stop=(k == 7)),
                    reads=[B_latt, B_wblk[0]], writes=[PSB[bk]])
            p.emit("ve", lambda bk=bk, tv=tv: nc.vector.tensor_add(
                out=vtok[:, tv * 256:(tv + 1) * 256], in0=psf(bk)[:, 0:256],
                in1=bv_bc), reads=[PSB[bk], B_bvbc], writes=[B_vtok])
            emit_quant(1)

        if debug:
            p.dma("sp", dbg["dbg_q2t"].rearrange("(c p) t -> p (c t)", p=128),
                  q2t, reads=[B_q2t], writes=[Buf("dbg")], sem=None)
            p.dma("sp", dbg["dbg_latt"].rearrange("(c p) t -> p (c t)", p=128),
                  latt, reads=[B_latt], writes=[Buf("dbg")], sem=None)
            p.dma("sp", dbg["dbg_kt"].rearrange("(c p) t -> p (c t)", p=128),
                  kt, reads=[B_kt], writes=[Buf("dbg")], sem=None)
            p.dma("sp", dbg["dbg_v"].rearrange("(c p) f -> p (c f)", p=128),
                  vtok, reads=[B_vtok], writes=[Buf("dbg")], sem=None)

        A.release(o_ln1t)
        A.release(o_latt)
        A.release(o_bvbc)
        ctxt, B_ctxt, o_ctxt = sba("ctxt", 8 * T, BF16, side="high")
        ctxt3 = ctxt.rearrange("p (c t) -> p c t", t=T)
        NW = 4
        sc_, B_scsb, o_scsb = [], [], []
        pr_, B_prsb, o_prsb = [], [], []
        pr2_, B_pr2sb, o_pr2sb = [], [], []
        smax_, B_smax_ = [], []
        prT_, B_prT, o_prT = [], [], []
        for i in range(NW):
            a, b, o = sba(f"sc_sb{i}", T, F32)
            sc_.append(a); B_scsb.append(b); o_scsb.append(o)
            a, b, o = sba(f"pr_sb{i}", T, BF16)
            pr_.append(a); B_prsb.append(b); o_prsb.append(o)
            a, b, o = sba(f"pr2_sb{i}", T, BF16)
            pr2_.append(a); B_pr2sb.append(b); o_pr2sb.append(o)
            a, b, o = sba(f"smax{i}", 4, F32)
            smax_.append(a); B_smax_.append(b)
            a, b, o = sba(f"prT{i}", 1024, BF16)
            prT_.append(a); B_prT.append(b); o_prT.append(o)

        units = [(qi, h) for qi in range(16) for h in range(NH)]

        def stageA(u):
            qi, h = units[u]
            nkt = qi + 1
            g = h // 4
            hp = u % NW
            sc_sb, Bsc = sc_[hp], B_scsb[hp]
            eeng = "sc" if h % 2 == 0 else "ve"
            for jg in range(ceil_div(nkt, 4)):
                bk = jg % 2
                js = list(range(jg * 4, min(jg * 4 + 4, nkt)))
                p.emit("pe", lambda h=h, g=g, qi=qi, js=js, bk=bk:
                       nc.tensor.matmul(
                    psf(bk)[:, 0:len(js) * 128],
                    q2t3[q_slot(h) * 64:q_slot(h) * 64 + 64, q_chunk(h),
                         qi * 128:(qi + 1) * 128],
                    kt3[(g % 2) * 64:(g % 2) * 64 + 64, g // 2,
                        js[0] * 128:(js[-1] + 1) * 128],
                    start=True, stop=True),
                    reads=[B_q2t, B_kt], writes=[PSB[bk]])
                jlo, jhi = js[0], js[-1]
                ndiag = jhi == qi
                if not ndiag or jhi > jlo:
                    e_hi = jhi * 128 if ndiag else (jhi + 1) * 128
                    npc = (jhi - jlo) if ndiag else (jhi - jlo + 1)
                    if eeng == "sc":
                        p.emit("sc", lambda bk=bk, jlo=jlo, npc=npc, e_hi=e_hi,
                               sc_sb=sc_sb: nc.scalar.activation(
                            out=sc_sb[:, jlo * 128:e_hi],
                            in_=psf(bk)[:, 0:npc * 128],
                            func=AF.Identity, scale=0.125),
                            reads=[PSB[bk]], writes=[Bsc])
                    else:
                        p.emit("ve", lambda bk=bk, jlo=jlo, npc=npc, e_hi=e_hi,
                               sc_sb=sc_sb: nc.vector.tensor_scalar(
                            out=sc_sb[:, jlo * 128:e_hi],
                            in0=psf(bk)[:, 0:npc * 128],
                            scalar1=0.125, scalar2=None, op0=ALU.mult),
                            reads=[PSB[bk]], writes=[Bsc])
                if ndiag:
                    p.emit("ve", lambda bk=bk, jlo=jlo, jhi=jhi,
                           sc_sb=sc_sb: nc.vector.scalar_tensor_tensor(
                        out=sc_sb[:, jhi * 128:(jhi + 1) * 128],
                        in0=psf(bk)[:, (jhi - jlo) * 128:(jhi - jlo + 1) * 128],
                        scalar=0.125, in1=mask_t, op0=ALU.mult, op1=ALU.add),
                        reads=[PSB[bk], B_mask], writes=[Bsc])

        def stageB1(u):
            qi, h = units[u]
            L = (qi + 1) * 128
            hp = u % NW
            sc_sb, Bsc = sc_[hp], B_scsb[hp]
            smax, Bsm = smax_[hp], B_smax_[hp]
            p.emit("ve", lambda L=L, sc_sb=sc_sb, smax=smax:
                   nc.vector.tensor_reduce(
                out=smax[:, 0:1], in_=sc_sb[:, 0:L], axis=AX.X, op=ALU.max,
                negate=True), reads=[Bsc], writes=[Bsm])

        def stageBX(u):
            qi, h = units[u]
            L = (qi + 1) * 128
            hp = u % NW
            sc_sb, Bsc = sc_[hp], B_scsb[hp]
            pr_sb, Bpr = pr_[hp], B_prsb[hp]
            smax, Bsm = smax_[hp], B_smax_[hp]
            p.emit("sc", lambda L=L, sc_sb=sc_sb, pr_sb=pr_sb, smax=smax:
                   nc.scalar.activation(
                out=pr_sb[:, 0:L], in_=sc_sb[:, 0:L], func=AF.Exp,
                bias=smax[:, 0:1], accum_out=smax[:, 2:3]),
                reads=[Bsc, Bsm], writes=[Bpr, Bsm])

        def stageB2(u):
            qi, h = units[u]
            L = (qi + 1) * 128
            hp = u % NW
            pr_sb, Bpr = pr_[hp], B_prsb[hp]
            pr2_sb, Bpr2 = pr2_[hp], B_pr2sb[hp]
            smax, Bsm = smax_[hp], B_smax_[hp]
            p.emit("ve", lambda smax=smax: nc.vector.reciprocal(
                out=smax[:, 3:4], in_=smax[:, 2:3]), reads=[Bsm], writes=[Bsm])
            if h % 2 == 0:
                p.emit("sc", lambda L=L, pr_sb=pr_sb, pr2_sb=pr2_sb, smax=smax:
                       nc.scalar.activation(
                    out=pr2_sb[:, 0:L], in_=pr_sb[:, 0:L], func=AF.Identity,
                    scale=smax[:, 3:4]),
                    reads=[Bpr, Bsm], writes=[Bpr2])
            else:
                p.emit("ve", lambda L=L, pr_sb=pr_sb, pr2_sb=pr2_sb, smax=smax:
                       nc.vector.tensor_scalar(
                    out=pr2_sb[:, 0:L], in0=pr_sb[:, 0:L], scalar1=smax[:, 3:4],
                    scalar2=None, op0=ALU.mult),
                    reads=[Bpr, Bsm], writes=[Bpr2])

        def stageC(u):
            qi, h = units[u]
            nkt = qi + 1
            g = h // 4
            hp = u % NW
            pr2_sb, Bpr2 = pr2_[hp], B_pr2sb[hp]
            cbk = 4 + h % 2
            eeng = "sc" if h % 2 == 0 else "ve"
            ngr = ceil_div(nkt, 4)
            jss = [list(range(jg * 4, min(jg * 4 + 4, nkt))) for jg in range(ngr)]

            def emit_tr(jg):
                tbk = 2 + jg % 2
                for j in jss[jg]:
                    p.emit("pe", lambda tbk=tbk, j=j, pr2_sb=pr2_sb:
                           nc.tensor.transpose(
                        psb(tbk)[:, (j % 4) * 128:(j % 4) * 128 + 128],
                        pr2_sb[:, j * 128:(j + 1) * 128], id_bf),
                        reads=[Bpr2, B_id], writes=[PSB[tbk]])
                js = jss[jg]
                if eeng == "sc":
                    p.emit("sc", lambda tbk=tbk, hp=hp, js=js: nc.scalar.copy(
                        out=prT_[hp][:, (js[0] % 8) * 128:
                                     (js[0] % 8) * 128 + len(js) * 128],
                        in_=psb(tbk)[:, 0:len(js) * 128]),
                        reads=[PSB[tbk]], writes=[B_prT[hp]])
                else:
                    p.emit("ve", lambda tbk=tbk, hp=hp, js=js: nc.vector.tensor_copy(
                        out=prT_[hp][:, (js[0] % 8) * 128:
                                     (js[0] % 8) * 128 + len(js) * 128],
                        in_=psb(tbk)[:, 0:len(js) * 128]),
                        reads=[PSB[tbk]], writes=[B_prT[hp]])

            def emit_ctx(jg):
                js = jss[jg]
                for j in js:
                    p.emit("pe", lambda cbk=cbk, hp=hp, j=j, js=js, g=g,
                           nkt=nkt: nc.tensor.matmul(
                        psf(cbk)[0:64, 0:128],
                        vtok[:, j * 256 + g * 64:j * 256 + g * 64 + 64],
                        prT_[hp][:, (js[0] % 8) * 128 + (j - js[0]) * 128:
                                 (js[0] % 8) * 128 + (j - js[0] + 1) * 128],
                        start=(j == 0), stop=(j == nkt - 1)),
                        reads=[B_vtok, B_prT[hp]], writes=[PSB[cbk]])

            for jg in range(ngr):
                emit_tr(jg)
                if jg >= 1:
                    emit_ctx(jg - 1)
            emit_ctx(ngr - 1)
            p.emit("ve", lambda cbk=cbk, h=h, qi=qi: nc.vector.tensor_copy(
                out=ctxt3[(h % 2) * 64:(h % 2) * 64 + 64, h // 2,
                          qi * 128:(qi + 1) * 128],
                in_=psf(cbk)[0:64, 0:128]),
                reads=[PSB[cbk]], writes=[B_ctxt])

        for k in range(len(units) + 4):
            if k % 8 == 3:
                emit_quant(1)
            if k < len(units):
                stageA(k)
            if 1 <= k < len(units) + 1:
                stageB1(k - 1)
            if 2 <= k < len(units) + 2:
                stageBX(k - 2)
            if 3 <= k < len(units) + 3:
                stageB2(k - 3)
            if k >= 4:
                stageC(k - 4)

        if debug:
            p.dma("sp", dbg["dbg_ctxt"].rearrange("(c p) t -> p (c t)", p=128),
                  ctxt, reads=[B_ctxt], writes=[Buf("dbg")], sem=None)

        emit_quant(len(quant_steps))
        cc1 = p.new_sem("cc1")
        cc2 = p.new_sem("cc2")
        p.emit("gp", lambda: nc.gpsimd.collective_compute(
            "AllGather", ALU.bypass, replica_groups=[list(range(NCORES))],
            ins=[wguq_sh.opt()], outs=[wgu_q.opt()]),
            reads=[B_wguq_sh], writes=[B_wgu_q], sem="cc1", dinc=1)
        p.emit("gp", lambda: nc.gpsimd.collective_compute(
            "AllGather", ALU.bypass, replica_groups=[list(range(NCORES))],
            ins=[wpqt_sh.opt()], outs=[wpt_full.opt()]),
            reads=[B_wpq_sh], writes=[B_wp_q], sem="cc2", dinc=1)
        for i in range(NQSLOT):
            A.release(o_qrow[i])
        for nm in qmisc:
            for i in range(NQSLOT):
                A.release(qmisc[nm][i][2])


        A.release(o_q2t)
        A.release(o_kt)
        A.release(o_vtok)
        for o in o_scsb + o_prsb + o_pr2sb:
            A.release(o)

        wpr_sb, B_wprsb, o_wprsb = sba("wpr_sb", 8 * D, BF16, side="high")
        wpr3 = wpr_sb.rearrange("p (k f) -> p k f", f=D)
        for k in range(8):
            p.dma("sp", wpr3[:, k, :],
                  wpr_nat[:, :].rearrange("(k p) f -> p k f", p=128)[:, k, :],
                  reads=[B_wpr], writes=[B_wprsb], sem=None)
        xrow_, B_xrow, o_xrow = [], [], []
        for i in range(2):
            a, b, o = sba(f"xrow_c{i}", D, F32)
            xrow_.append(a); B_xrow.append(b); o_xrow.append(o)
        ao_, B_ao, o_ao = [], [], []
        for i in range(2):
            a, b, o = sba(f"ao{i}", D, F32)
            ao_.append(a); B_ao.append(b); o_ao.append(o)
        ztok_, B_zs, o_zs = [], [], []
        for i in range(2):
            a, b, o = sba(f"z_tok{i}", D, F32)
            ztok_.append(a); B_zs.append(b); o_zs.append(o)
        s2_, B_s2, o_s2 = [], [], []
        for i in range(2):
            a, b, o = sba(f"s2_{i}", D, F32)
            s2_.append(a); B_s2.append(b); o_s2.append(o)
        ln3_, B_ln3, o_ln3 = [], [], []
        for i in range(2):
            a, b, o = sba(f"ln3_{i}", D, BF16)
            ln3_.append(a); B_ln3.append(b); o_ln3.append(o)
        B_s2_dram = Buf("s2_dram")
        B_ln3t_dram = Buf("ln3t_dram")
        ln3t_v = ln3t_dram.rearrange("(c p) t -> p c t", p=128)

        def c1_s1(tc):
            s = tc % 2
            p.dma("sp", xrow_[s], x_d[tc * 128:(tc + 1) * 128, :],
                  writes=[B_xrow[s]], sem=None)
            for half in range(2):
                b0 = half * 2
                for k in range(8):
                    for ns in range(2):
                        p.emit("pe", lambda b0=b0, s=s, half=half, k=k, ns=ns, tc=tc:
                               nc.tensor.matmul(
                            psf(b0 + ns), ctxt3[:, k, tc * 128:(tc + 1) * 128],
                            wpr3[:, k, half * 1024 + ns * 512:
                                 half * 1024 + (ns + 1) * 512],
                            start=(k == 0), stop=(k == 7)),
                            reads=[B_ctxt, B_wprsb], writes=[PSB[b0 + ns]])
                p.emit("ve", lambda b0=b0, s=s, half=half: nc.vector.tensor_add(
                    out=ao_[s][:, half * 1024:(half + 1) * 1024], in0=psf(b0, 2),
                    in1=bpp_bc[:, half * 1024:(half + 1) * 1024]),
                    reads=[PSB[b0], PSB[b0 + 1], B_bppbc], writes=[B_ao[s]])

        def c1_s2(tc):
            s = tc % 2
            ln_token_major(B_ao[s], ao_[s], ztok_[s], B_zs[s], LN_EPS, sl=s)
            p.emit("ve", lambda s=s: nc.vector.tensor_add(
                out=s2_[s], in0=ztok_[s], in1=xrow_[s]),
                reads=[B_zs[s], B_xrow[s]], writes=[B_s2[s]])
            p.dma("sp", s2_dram[tc * 128:(tc + 1) * 128, :], s2_[s],
                  reads=[B_s2[s]], writes=[B_s2_dram], sem=None)
            ln_token_major(B_s2[s], s2_[s], ln3_[s], B_ln3[s], LN_EPS, sl=s)
            for cg in range(4):
                bk = 5 + cg % 2
                sp_ = cg % 2
                for c4 in range(4):
                    c = cg * 4 + c4
                    p.emit("pe", lambda s=s, bk=bk, c4=c4, c=c: nc.tensor.transpose(
                        psb(bk)[:, c4 * 128:(c4 + 1) * 128],
                        ln3_[s][:, c * 128:(c + 1) * 128], id_bf),
                        reads=[B_ln3[s], B_id], writes=[PSB[bk]])
                p.emit("ve", lambda bk=bk, sp_=sp_: nc.vector.tensor_copy(
                    out=prT_[sp_][:, 0:512], in_=psb(bk)[:, 0:512]),
                    reads=[PSB[bk]], writes=[B_prT[sp_]])
                p.dma("sp", ln3t_v[:, cg * 4:(cg + 1) * 4, tc * 128:(tc + 1) * 128],
                      prT_[sp_][:, 0:512].rearrange("p (c t) -> p c t", t=128),
                      reads=[B_prT[sp_]], writes=[B_ln3t_dram], sem=None)

        c1_s1(0)
        for tc in range(16):
            if tc + 1 < 16:
                c1_s1(tc + 1)
            c1_s2(tc)
        if debug:
            p.dma("sp", dbg["dbg_s2"], s2_dram, reads=[B_s2_dram], writes=[Buf("dbg")], sem=None)
            p.dma("sp", dbg["dbg_ln3t"], ln3t_dram, reads=[B_ln3t_dram], writes=[Buf("dbg")], sem=None)

        if debug:
            p.dma("sp", dbg["dbg_wguq"], wgu_q, reads=[B_wgu_q], writes=[Buf("dbg")], sem=None)

        A.release(o_ctxt)
        A.release(o_wprsb)
        A.release(o_bppbc)
        for o in o_ao + o_zs + o_s2 + o_ln3 + o_prT + o_xrow + [o_trev[0], o_trev[1]]:
            A.release(o)
        for o in o_wblk:
            A.release(o)

        ln3t_sb, B_ln3sb, o_ln3sb = sba("ln3t_sb", 16 * T, BF16, side="high")
        ln3s3 = ln3t_sb.rearrange("p (c t) -> p c t", t=T)
        for c4 in range(4):
            p.dma("sp", ln3s3[:, c4 * 4:(c4 + 1) * 4, :],
                  ln3t_dram.rearrange("(c p) t -> p c t", p=128)[:, c4 * 4:(c4 + 1) * 4, :],
                  reads=[B_ln3t_dram], writes=[B_ln3sb], sem=None)
        wgt_, B_wgt, o_wgt = [], [], []
        wut_, B_wut, o_wut = [], [], []
        for i in range(2):
            a, b, o = sba(f"wgt{i}", 16 * 512, BF16, side="high")
            wgt_.append(a); B_wgt.append(b); o_wgt.append(o)
            a, b, o = sba(f"wut{i}", 16 * 512, BF16, side="high")
            wut_.append(a); B_wut.append(b); o_wut.append(o)
        sg_sb, B_sg, o_sg = sba("sg_sb", 1024, BF16)
        ha_, B_ha, o_ha = [], [], []
        for i in range(2):
            a, b, o = sba(f"ha{i}", 1024, BF16)
            ha_.append(a); B_ha.append(b); o_ha.append(o)
        h2_sb, B_h2, o_h2 = sba("h2_sb", 1024, F32)
        acc_sb, B_acc, o_acc = sba("acc_sb", T, F32)
        B_hact_dram = Buf("hact_dram")
        p.emit("ve", lambda: nc.vector.memset(acc_sb, 0.0), writes=[B_acc])

        for jq in range(16):
            s = jq % 2
            for c in range(16):
                p.dma("sp", wgt_[s][:, c * 512:(c + 1) * 512],
                      wgu_q[jq * 512:(jq + 1) * 512, c * 128:(c + 1) * 128],
                      reads=[B_wgu_q], writes=[B_wgt[s]], sem=None, transpose=True)
                p.dma("sp", wut_[s][:, c * 512:(c + 1) * 512],
                      wgu_q[HF + jq * 512:HF + (jq + 1) * 512, c * 128:(c + 1) * 128],
                      reads=[B_wgu_q], writes=[B_wut[s]], sem=None, transpose=True)
            for jj in range(4):
                j = jq * 4 + jj
                gb = (j % 2) * 2
                ub = 4 + (j % 2) * 2
                for half in range(2):
                    for k in range(16):
                        for ns in range(2):
                            p.emit("pe", lambda s=s, gb=gb, jj=jj, k=k, ns=ns,
                                   half=half: nc.tensor.matmul(
                                psf(gb + ns),
                                wgt_[s][:, k * 512 + jj * 128:k * 512 + (jj + 1) * 128],
                                ln3s3[:, k, half * 1024 + ns * 512:
                                      half * 1024 + (ns + 1) * 512],
                                start=(k == 0), stop=(k == 15)),
                                reads=[B_wgt[s], B_ln3sb], writes=[PSB[gb + ns]])
                        for ns in range(2):
                            p.emit("pe", lambda s=s, ub=ub, jj=jj, k=k, ns=ns,
                                   half=half: nc.tensor.matmul(
                                psf(ub + ns),
                                wut_[s][:, k * 512 + jj * 128:k * 512 + (jj + 1) * 128],
                                ln3s3[:, k, half * 1024 + ns * 512:
                                      half * 1024 + (ns + 1) * 512],
                                start=(k == 0), stop=(k == 15)),
                                reads=[B_wut[s], B_ln3sb], writes=[PSB[ub + ns]])
                    p.emit("sc", lambda gb=gb: nc.scalar.activation(
                        out=sg_sb, in_=psf(gb, 2), func=AF.Silu),
                        reads=[PSB[gb], PSB[gb + 1]], writes=[B_sg])
                    hs = half % 2
                    p.emit("ve", lambda ub=ub, hs=hs: nc.vector.tensor_mul(
                        out=ha_[hs], in0=psf(ub, 2), in1=sg_sb),
                        reads=[PSB[ub], PSB[ub + 1], B_sg], writes=[B_ha[hs]])
                    p.dma("gp", hact_dram[j * 128:(j + 1) * 128,
                                          half * 1024:(half + 1) * 1024],
                          ha_[hs], reads=[B_ha[hs]], writes=[B_hact_dram],
                          sem=None)
                    p.emit("gp", lambda hs=hs: nc.gpsimd.tensor_mul(
                        out=h2_sb, in0=ha_[hs], in1=ha_[hs]),
                        reads=[B_ha[hs]], writes=[B_h2])
                    p.emit("ve", lambda half=half: nc.vector.tensor_add(
                        out=acc_sb[:, half * 1024:(half + 1) * 1024],
                        in0=acc_sb[:, half * 1024:(half + 1) * 1024], in1=h2_sb),
                        reads=[B_h2, B_acc], writes=[B_acc])

        invp, B_inv, o_invp = sba("inv_pack", T, F32)
        inv_row = invp[0:1, :]
        inv_bc, B_invbc, o_invbc = sba("inv_bc", T, F32)
        onec, B_onec, o_onec = sba("onec", 1, F32)
        p.emit("gp", lambda: nc.gpsimd.memset(onec, 1.0), writes=[B_onec])
        for ns in range(4):
            p.emit("pe", lambda ns=ns: nc.tensor.matmul(
                psf(ns)[0:1, :], onec, acc_sb[:, ns * 512:(ns + 1) * 512],
                start=True, stop=True), reads=[B_onec, B_acc], writes=[PSB[ns]])
        p.emit("ve", lambda: nc.vector.tensor_scalar(
            out=inv_row, in0=psf(0, 4)[0:1, 0:2048], scalar1=1.0 / HF,
            scalar2=RMS_EPS, op0=ALU.mult, op1=ALU.add),
            reads=[PSB[0], PSB[1], PSB[2], PSB[3]], writes=[B_inv])
        p.emit("sc", lambda: nc.scalar.activation(
            out=inv_row, in_=inv_row, func=AF.Ln),
            reads=[B_inv], writes=[B_inv])
        p.emit("sc", lambda: nc.scalar.activation(
            out=inv_row, in_=inv_row, func=AF.Exp, scale=-0.5),
            reads=[B_inv], writes=[B_inv])
        for ns in range(4):
            p.emit("pe", lambda ns=ns: nc.tensor.matmul(
                psf(4), ones_row_f[0:1, :], inv_row[:, ns * 512:(ns + 1) * 512],
                start=True, stop=True), reads=[B_ones, B_inv], writes=[PSB[4]])
            p.emit("ve", lambda ns=ns: nc.vector.tensor_copy(
                out=inv_bc[:, ns * 512:(ns + 1) * 512], in_=psf(4)),
                reads=[PSB[4]], writes=[B_invbc])

        if debug:
            p.dma("sp", dbg["dbg_hact"], hact_dram, reads=[B_hact_dram], writes=[Buf("dbg")], sem=None)

        A.release(o_ln3sb)
        for o in o_wgt + o_wut + [o_sg] + o_ha + [o_h2, o_acc, o_onec]:
            A.release(o)

        wpt_, B_wpt, o_wpt = [], [], []
        for i in range(2):
            a, b, o = sba(f"wpt{i}", 64 * 256, BF16, side="high")
            wpt_.append(a); B_wpt.append(b); o_wpt.append(o)
        hact_, B_hactsb, o_hactsb = [], [], []
        for i in range(2):
            a, b, o = sba(f"hact_sb{i}", 16 * 512, BF16, side="high")
            hact_.append(a); B_hactsb.append(b); o_hactsb.append(o)
        fchunk_, B_fchunk, o_fchunk = [], [], []
        for i in range(2):
            a, b, o = sba(f"fchunk{i}", 512, F32)
            fchunk_.append(a); B_fchunk.append(b); o_fchunk.append(o)
        ftr_, B_ftr, o_ftr = [], [], []
        for i in range(2):
            a, b, o = sba(f"ftr{i}", 512, F32)
            ftr_.append(a); B_ftr.append(b); o_ftr.append(o)
        B_ftok = Buf("ftok_dram")
        hact_v = hact_dram.rearrange("(j p) t -> p j t", p=128)

        pending_fb = []
        for dgp in range(4):
            wptv = []
            for dg2 in range(2):
                dg = dgp * 2 + dg2
                w3 = wpt_[dg2].rearrange("p (j f) -> p j f", f=256)
                wptv.append(w3)
                for j8 in range(4):
                    p.dma("sp", w3[:, j8 * 16:(j8 + 1) * 16, :],
                          wpt_full.rearrange("(j p) d -> p j d", p=128)[
                              :, j8 * 16:(j8 + 1) * 16, dg * 256:(dg + 1) * 256],
                          reads=[B_wp_q], writes=[B_wpt[dg2]], sem=None)
            for tt in range(4):
                b0 = (tt % 2) * 4
                for jhh in range(4):
                    hs = jhh % 2
                    hact3 = hact_[hs].rearrange("p (j f) -> p j f", f=512)
                    p.dma("sp", hact3[:, 0:16, :],
                          hact_v[:, jhh * 16:jhh * 16 + 16,
                                 tt * 512:(tt + 1) * 512],
                          reads=[B_hact_dram], writes=[B_hactsb[hs]], sem=None)
                    for dg2 in range(2):
                        for dc in range(2):
                            for jh in range(16):
                                p.emit("pe", lambda b0=b0, dg2=dg2, dc=dc, jh=jh,
                                       jhh=jhh, hs=hs, w3=wptv[dg2]:
                                       nc.tensor.matmul(
                                    psf(b0 + dg2 * 2 + dc),
                                    w3[:, jhh * 16 + jh, dc * 128:(dc + 1) * 128],
                                    hact3[:, jh, :],
                                    start=(jhh == 0 and jh == 0),
                                    stop=(jhh == 3 and jh == 15)),
                                    reads=[B_wpt[dg2], B_hactsb[hs]],
                                    writes=[PSB[b0 + dg2 * 2 + dc]])
                for fn in pending_fb:
                    fn()
                pending_fb = []

                def make_evac(dgp=dgp, tt=tt, b0=b0):
                    def go():
                        for dg2 in range(2):
                            for dc in range(2):
                                d_abs = (dgp * 2 + dg2) * 2 + dc
                                bkk = b0 + dg2 * 2 + dc
                                s = (dg2 * 2 + dc) % 2
                                p.emit("ve", lambda bkk=bkk, s=s, tt=tt:
                                       nc.vector.tensor_mul(
                                    out=fchunk_[s], in0=psf(bkk),
                                    in1=inv_bc[:, tt * 512:(tt + 1) * 512]),
                                    reads=[PSB[bkk], B_invbc],
                                    writes=[B_fchunk[s]])
                                for c4 in range(4):
                                    p.emit("pe", lambda s=s, bkk=bkk, c4=c4:
                                           nc.tensor.transpose(
                                        psf(bkk)[:, c4 * 128:(c4 + 1) * 128],
                                        fchunk_[s][:, c4 * 128:(c4 + 1) * 128],
                                        id_f32),
                                        reads=[B_fchunk[s], B_idf],
                                        writes=[PSB[bkk]])
                                p.emit("sc", lambda s=s, bkk=bkk: nc.scalar.copy(
                                    out=ftr_[s], in_=psf(bkk)),
                                    reads=[PSB[bkk]], writes=[B_ftr[s]])
                                p.dma("gp", ftok_dram[tt * 512:(tt + 1) * 512,
                                                      d_abs * 128:(d_abs + 1) * 128]
                                      .rearrange("(c t) f -> t c f", t=128),
                                      ftr_[s].rearrange("t (c f) -> t c f", f=128),
                                      reads=[B_ftr[s]], writes=[B_ftok], sem=None)
                    return go
                pending_fb.append(make_evac())
        for fn in pending_fb:
            fn()
        pending_fb = []

        if debug:
            p.dma("sp", dbg["dbg_ffn"], ftok_dram, reads=[B_ftok], writes=[Buf("dbg")], sem=None)

        for o in o_wpt + o_hactsb + o_fchunk + o_ftr + [o_invp, o_invbc]:
            A.release(o)

        fin_, B_fin, o_fin = [], [], []
        s2in_, B_s2in, o_s2in = [], [], []
        outt_, B_outt, o_outt = [], [], []
        for i in range(2):
            a, b, o = sba(f"fin{i}", D, F32)
            fin_.append(a); B_fin.append(b); o_fin.append(o)
            a, b, o = sba(f"s2in{i}", D, F32)
            s2in_.append(a); B_s2in.append(b); o_s2in.append(o)
            a, b, o = sba(f"outt{i}", D, F32)
            outt_.append(a); B_outt.append(b); o_outt.append(o)
        z4_, B_z4s, o_z4s = [], [], []
        for i in range(2):
            a, b, o = sba(f"z4_{i}", D, F32)
            z4_.append(a); B_z4s.append(b); o_z4s.append(o)
        B_out_dram = Buf("out_dram")

        def fcload(tc):
            s = tc % 2
            p.dma("sp", fin_[s], ftok_dram[tc * 128:(tc + 1) * 128, :],
                  reads=[B_ftok], writes=[B_fin[s]], sem=None)
            p.dma("sp", s2in_[s], s2_dram[tc * 128:(tc + 1) * 128, :],
                  reads=[B_s2_dram], writes=[B_s2in[s]], sem=None)
        fcload(0)
        for tc in range(16):
            s = tc % 2
            if tc + 1 < 16:
                fcload(tc + 1)
            ln_token_major(B_fin[s], fin_[s], z4_[s], B_z4s[s], LN_EPS, sl=s)
            p.emit("ve", lambda s=s: nc.vector.tensor_add(
                out=outt_[s], in0=z4_[s], in1=s2in_[s]),
                reads=[B_z4s[s], B_s2in[s]], writes=[B_outt[s]])
            p.dma("sp", out_d[tc * 128:(tc + 1) * 128, :], outt_[s],
                  reads=[B_outt[s]], writes=[B_out_dram], sem=None)

        p.eng["sp"].wait_ge(p.sem[B_out_dram.dsem], p.cnt[B_out_dram.dsem])
        nc.all_engine_barrier()

    return nc


_PROG_CACHE = {}


def kernel(L0_attn_x, L0_attn_Wq, L0_attn_bq, L0_attn_Wl, L0_attn_bl, L0_attn_Wk,
           L0_attn_bk, L0_attn_Wv, L0_attn_bv, L0_attn_Wq2kv, L0_attn_Wkv2h,
           L0_attn_bkv2h, L0_attn_Wo, L0_attn_bo, L0_ffn_w_gate_up, L0_ffn_w_proj):
    x = np.ascontiguousarray(np.asarray(L0_attn_x).astype(np.float32, copy=False))
    w = {
        "Wq": np.ascontiguousarray(np.asarray(L0_attn_Wq).astype(np.float32, copy=False)),
        "bq": np.ascontiguousarray(np.asarray(L0_attn_bq).astype(np.float32, copy=False)),
        "Wl": np.ascontiguousarray(np.asarray(L0_attn_Wl).astype(np.float32, copy=False)),
        "bl": np.ascontiguousarray(np.asarray(L0_attn_bl).astype(np.float32, copy=False)),
        "Wk": np.ascontiguousarray(np.asarray(L0_attn_Wk).astype(np.float32, copy=False)),
        "bk": np.ascontiguousarray(np.asarray(L0_attn_bk).astype(np.float32, copy=False)),
        "Wv": np.ascontiguousarray(np.asarray(L0_attn_Wv).astype(np.float32, copy=False)),
        "bv": np.ascontiguousarray(np.asarray(L0_attn_bv).astype(np.float32, copy=False)),
        "Wq2kv": np.ascontiguousarray(np.asarray(L0_attn_Wq2kv).astype(np.float32, copy=False)),
        "Wkv2h": np.ascontiguousarray(np.asarray(L0_attn_Wkv2h).astype(np.float32, copy=False)),
        "bkv2h": np.ascontiguousarray(np.asarray(L0_attn_bkv2h).astype(np.float32, copy=False)),
        "Wo": np.ascontiguousarray(np.asarray(L0_attn_Wo).astype(np.float32, copy=False)),
        "bo": np.ascontiguousarray(np.asarray(L0_attn_bo).astype(np.float32, copy=False)),
    }
    wgu = np.asarray(L0_ffn_w_gate_up).astype(np.float32, copy=False)
    wp = np.asarray(L0_ffn_w_proj).astype(np.float32, copy=False)
    if "nc" not in _PROG_CACHE:
        _PROG_CACHE["nc"] = build_program(debug=False)
    nc = _PROG_CACHE["nc"]
    gsh = 2 * HF // NCORES
    psh = HF // NCORES
    in_maps = [dict(w, x=np.ascontiguousarray(x[c]),
                    Wgu_sh=np.ascontiguousarray(wgu[c * gsh:(c + 1) * gsh]),
                    Wp_sh=np.ascontiguousarray(wp[:, c * psh:(c + 1) * psh]))
               for c in range(NCORES)]
    res = run_bass_kernel_spmd(nc, in_maps, core_ids=list(range(NCORES)))
    out = np.stack([res.results[c]["out"] for c in range(NCORES)], axis=0)
    return out.astype(np.float32)


# version 22
# speedup vs baseline: 1.0287x; 1.0051x over previous
import numpy as np
import concourse.bass as bass
import concourse.mybir as mybir
from concourse.bass_utils import run_bass_kernel_spmd
from concourse.bass import broadcast_tensor_aps

F32 = mybir.dt.float32
BF16 = mybir.dt.bfloat16
AF = mybir.ActivationFunctionType
ALU = mybir.AluOpType
AX = mybir.AxisListType

D = 2048
T = 2048
NH = 16
NKV = 4
KV = 64
LAT = 1024
HF = 8192
GROUP = 64
LN_EPS = 1e-5
RMS_EPS = float(np.finfo(np.float32).eps)
QTHR = 0.5 + 2.0 ** -10
NCORES = 8


def q_chunk(h):
    return (h % 4) + 4 * (h // 8)


def q_slot(h):
    return (h // 4) % 2


class Buf:
    __slots__ = ("lw", "rd", "name", "dsem")

    def __init__(self, name=""):
        self.lw = None
        self.rd = []
        self.name = name
        self.dsem = None


class Arena:

    def __init__(self, tensor, nbytes):
        self.t = tensor
        self.free = [(0, nbytes)]
        self.live = {}
        self.ghost = []

    def alloc(self, name, nbytes, dtype, width=None, side="low"):
        nbytes = (nbytes + 63) & ~63
        idxs = range(len(self.free)) if side == "low" else \
            range(len(self.free) - 1, -1, -1)
        for i in idxs:
            off, sz = self.free[i]
            if sz >= nbytes:
                if sz == nbytes:
                    self.free.pop(i)
                elif side == "low":
                    self.free[i] = (off + nbytes, sz - nbytes)
                else:
                    self.free[i] = (off, sz - nbytes)
                    off = off + sz - nbytes
                buf = Buf(name)
                keep = []
                for (go, gs, gb) in self.ghost:
                    if go < off + nbytes and off < go + gs:
                        if gb.lw is not None:
                            buf.rd.append(gb.lw)
                        buf.rd.extend(gb.rd)
                    else:
                        keep.append((go, gs, gb))
                self.ghost = keep
                self.live[off] = (nbytes, buf)
                ap = self.t[:, off:off + nbytes].bitcast(dtype)
                if width is not None:
                    ap = ap[:, 0:width]
                return ap, buf, off
        raise MemoryError(f"SBUF arena: cannot alloc {nbytes} for {name}; "
                          f"free={self.free}")

    def release(self, off):
        sz, buf = self.live.pop(off)
        self.ghost.append((off, sz, buf))
        self.free.append((off, sz))
        self.free.sort()
        out = []
        for seg in self.free:
            if out and out[-1][0] + out[-1][1] == seg[0]:
                out[-1] = (out[-1][0], out[-1][1] + seg[1])
            else:
                out.append(list(seg) if False else (seg[0], seg[1]))
        merged = []
        for seg in out:
            if merged and merged[-1][0] + merged[-1][1] == seg[0]:
                merged[-1] = (merged[-1][0], merged[-1][1] + seg[1])
            else:
                merged.append(seg)
        self.free = merged


class Prog:
    def __init__(self, nc):
        self.nc = nc
        self.eng = {
            "pe": nc.tensor, "sc": nc.scalar, "ve": nc.vector,
            "gp": nc.gpsimd, "sp": nc.sync,
        }
        self.sem = {}
        self.cnt = {}
        self.waited = {e: {} for e in self.eng}
        for e in ("pe", "sc", "ve", "gp"):
            self.new_sem(e)

    def new_sem(self, name):
        h = self.nc.alloc_semaphore(name)
        self.sem[name] = h
        self.cnt[name] = 0
        return name

    def make_dma_pool(self, n):
        self.dma_pool = [self.new_sem(f"dpool{i}") for i in range(n)]
        self.dma_pool_next = 0

    def clear_sems_and_barrier(self):
        for h in self.sem.values():
            self.nc.gpsimd.sem_clear(h)
        self.nc.all_engine_barrier()

    def wait(self, ename, tok):
        if tok is None:
            return
        s, v = tok
        w = self.waited[ename]
        if w.get(s, 0) >= v:
            return
        self.eng[ename].wait_ge(self.sem[s], v)
        w[s] = v

    def emit(self, ename, fn, reads=(), writes=(), sem=None, dinc=1):
        for b in reads:
            if b.lw is not None:
                self.wait(ename, b.lw)
        for b in writes:
            if b.lw is not None and b.lw[0] != ename:
                self.wait(ename, b.lw)
            for t in b.rd:
                if t[0] != ename:
                    self.wait(ename, t)
        ins = fn()
        sname = sem if sem is not None else ename
        self.cnt[sname] += dinc
        ins.then_inc(self.sem[sname], dinc)
        tok = (sname, self.cnt[sname])
        for b in writes:
            b.lw = tok
            b.rd = []
        for b in reads:
            b.rd.append(tok)
        return tok

    def dma(self, qname, out, in_, reads=(), writes=(), sem=None, **kw):
        assert len(writes) == 1
        w = writes[0]
        if w.dsem is None:
            assert self.dma_pool_next < len(self.dma_pool), "dma sem pool exhausted"
            w.dsem = self.dma_pool[self.dma_pool_next]
            self.dma_pool_next += 1
        return self.emit(
            qname, lambda: self.eng[qname].dma_start(out=out, in_=in_, **kw),
            reads=reads, writes=writes, sem=w.dsem, dinc=16)


def ceil_div(a, b):
    return -(-a // b)


def build_program(debug=False):
    nc = bass.Bass("TRN2", target_bir_lowering=False)
    p = Prog(nc)

    x_d = nc.dram_tensor("x", [T, D], F32, kind="ExternalInput").ap()
    wq_d = nc.dram_tensor("Wq", [D, D], F32, kind="ExternalInput").ap()
    bq_d = nc.dram_tensor("bq", [D], F32, kind="ExternalInput").ap()
    wl_d = nc.dram_tensor("Wl", [D, LAT], F32, kind="ExternalInput").ap()
    bl_d = nc.dram_tensor("bl", [LAT], F32, kind="ExternalInput").ap()
    wk_d = nc.dram_tensor("Wk", [LAT, NKV * KV], F32, kind="ExternalInput").ap()
    bk_d = nc.dram_tensor("bk", [NKV * KV], F32, kind="ExternalInput").ap()
    wv_d = nc.dram_tensor("Wv", [LAT, NKV * KV], F32, kind="ExternalInput").ap()
    bv_d = nc.dram_tensor("bv", [NKV * KV], F32, kind="ExternalInput").ap()
    wq2kv_d = nc.dram_tensor("Wq2kv", [128, KV], F32, kind="ExternalInput").ap()
    wkv2h_d = nc.dram_tensor("Wkv2h", [KV, 128], F32, kind="ExternalInput").ap()
    bkv2h_d = nc.dram_tensor("bkv2h", [128], F32, kind="ExternalInput").ap()
    wo_d = nc.dram_tensor("Wo", [D, D], F32, kind="ExternalInput").ap()
    bo_d = nc.dram_tensor("bo", [D], F32, kind="ExternalInput").ap()
    wgu_sh_d = nc.dram_tensor("Wgu_sh", [2 * HF // 8, D], F32, kind="ExternalInput").ap()
    wp_sh_d = nc.dram_tensor("Wp_sh", [D, HF // 8], F32, kind="ExternalInput").ap()
    out_d = nc.dram_tensor("out", [T, D], F32, kind="ExternalOutput").ap()

    wqp_blk = nc.dram_tensor("wqp_blk", [8, 128, 16, 128], BF16, kind="Internal").ap()
    wq_bf = nc.dram_tensor("wq_bf", [D, D], BF16, kind="Internal").ap()
    wl_blk = nc.dram_tensor("wl_blk", [8, 128, 16, 128], BF16, kind="Internal").ap()
    wk_blk = nc.dram_tensor("wk_blk", [2, 128, 8, 128], BF16, kind="Internal").ap()
    wv_nat = nc.dram_tensor("wv_nat", [LAT, NKV * KV], BF16, kind="Internal").ap()
    wpr_nat = nc.dram_tensor("wpr_nat", [LAT, D], BF16, kind="Internal").ap()
    wgu_q = nc.dram_tensor("wgu_q", [2 * HF, D], BF16, kind="Internal", addr_space="Shared").ap()
    wguq_sh = nc.dram_tensor("wguq_sh", [2 * HF // 8, D], BF16, kind="Internal").ap()
    wpqt_sh = nc.dram_tensor("wpqt_sh", [HF // 8, D], BF16, kind="Internal").ap()
    wpt_full = nc.dram_tensor("wpt_full", [HF, D], BF16, kind="Internal",
                              addr_space="Shared").ap()
    s2_dram = nc.dram_tensor("s2_dram", [T, D], F32, kind="Internal").ap()
    ln3t_dram = nc.dram_tensor("ln3t_dram", [D, T], BF16, kind="Internal").ap()
    hact_dram = nc.dram_tensor("hact_dram", [HF, T], BF16, kind="Internal").ap()
    ftok_dram = nc.dram_tensor("ftok_dram", [T, D], F32, kind="Internal").ap()

    dbg = {}
    if debug:
        for name, shape, dt in [
            ("dbg_ln1t", [D, T], BF16), ("dbg_q2t", [LAT, T], BF16),
            ("dbg_latt", [LAT, T], BF16), ("dbg_kt", [256, T], BF16),
            ("dbg_v", [T, NKV * KV], BF16), ("dbg_ctxt", [LAT, T], BF16),
            ("dbg_s2", [T, D], F32), ("dbg_ln3t", [D, T], BF16),
            ("dbg_wguq", [2 * HF, D], BF16), ("dbg_wpq", [D, HF], BF16),
            ("dbg_hact", [HF, T], BF16), ("dbg_ffn", [T, D], F32),
        ]:
            dbg[name] = nc.dram_tensor(name, shape, dt, kind="ExternalOutput").ap()

    from contextlib import ExitStack
    es = ExitStack()
    with es:
        SBN = 211968
        SBt = es.enter_context(nc.sbuf_tensor("SB", [128, SBN], mybir.dt.uint8))
        PSt = es.enter_context(nc.psum_tensor("PS", [128, 4096], F32))
        A = Arena(SBt, SBN)

        def psf(b0, nb=1):
            return PSt[:, b0 * 512:(b0 + nb) * 512]

        def psb(b0, nb=1):
            return PSt[:, b0 * 512:(b0 + nb) * 512].bitcast(BF16)

        PSB = [Buf(f"psum{i}") for i in range(8)]
        p.make_dma_pool(80)
        p.clear_sems_and_barrier()


        def sba(name, nelem, dtype, side="low"):
            sz = nelem * (2 if dtype == BF16 else 4)
            ap, buf, off = A.alloc(name, sz, dtype, width=nelem, side=side)
            return ap, buf, off

        id_bf, B_id, _ = sba("id_bf", 128, BF16)
        id_f32, B_idf, _ = sba("id_f32", 128, F32)
        mask_t, B_mask, _ = sba("mask_t", 128, F32)
        ones_row_f, B_ones, _ = sba("ones_row_f", 128, F32)
        bias_pack, B_bias, _ = sba("bias_pack", 64, F32)
        bv_bc, B_bvbc, o_bvbc = sba("bv_bc", 256, F32)
        bpp_bc, B_bppbc, o_bppbc = sba("bpp_bc", D, F32)
        borow_t, B_borow, o_bo = sba("bo_row", D, F32)
        bpprow_t, B_bpprow, o_bpp = sba("bpp_row", D, F32)
        bvrow_t, B_bvrow, o_bv = sba("bv_row", 256, F32)
        bo_row = borow_t[0:1, :]
        bpp_row = bpprow_t[0:1, :]
        bv_row = bvrow_t[0:1, :]
        wq2kv_bf, B_w2kv, _ = sba("wq2kv_bf", KV, BF16)
        wq2kv_f32, B_w2kvf, _ = sba("wq2kv_f32", KV, F32)
        wkv2h_f, B_wkvf, _ = sba("wkv2h_f", 128, BF16)
        wkv2h_t, B_wkvt, _ = sba("wkv2h_t", KV, BF16)
        bkv_bf, B_bkvbf, _ = sba("bkv_bf", 1, BF16)
        lnstat, B_lnstat, _ = sba("lnstat", 8, F32)
        eps_col, B_eps, _ = sba("eps_col", 1, F32)
        lnstat_, B_lnstat_ = [], []
        for i in range(2):
            a, b, o = sba(f"lnstat{i}", 32, F32)
            lnstat_.append(a); B_lnstat_.append(b)
        trev_, B_trev, o_trev = [], [], []
        for i in range(2):
            a, b, o = sba(f"trev{i}", 512, BF16)
            trev_.append(a); B_trev.append(b); o_trev.append(o)
        NQSLOT = 2
        QW = 1024
        qrow_, B_qrow, o_qrow = [], [], []
        qmisc = {}
        for i in range(NQSLOT):
            a, b, o = sba(f"qrow{i}", QW, BF16)
            qrow_.append(a); B_qrow.append(b); o_qrow.append(o)
        for nm, ne, dt in [("qsc", 16, F32), ("qscb", 16, BF16), ("qthr", 16, F32),
                           ("qaw", QW, BF16), ("qmask", QW, BF16), ("qsgn", QW, BF16),
                           ("qwh", QW, BF16)]:
            qmisc[nm] = []
            for i in range(NQSLOT):
                a, b, o = sba(f"{nm}{i}", ne, dt)
                qmisc[nm].append((a, b, o))
        qtr_, B_qtr = [], []
        for i in range(2):
            a, b, o = sba(f"qtr{i}", 1024, BF16)
            qtr_.append(a); B_qtr.append(b)
        smax, B_smax, _ = sba("smax", 4, F32)

        def make_identity(ident, bbuf):
            p.emit("gp", lambda: nc.gpsimd.memset(ident, 0.0), writes=[bbuf])
            p.emit("gp", lambda: nc.gpsimd.affine_select(
                out=ident, in_=ident, compare_op=ALU.not_equal, fill=1.0,
                base=0, pattern=[[-1, 128]], channel_multiplier=1),
                reads=[bbuf], writes=[bbuf])

        make_identity(id_bf, B_id)
        make_identity(id_f32, B_idf)
        p.emit("gp", lambda: nc.gpsimd.memset(mask_t, 0.0), writes=[B_mask])
        p.emit("gp", lambda: nc.gpsimd.affine_select(
            out=mask_t, in_=mask_t, compare_op=ALU.is_ge, fill=-30000.0,
            base=0, pattern=[[-1, 128]], channel_multiplier=1),
            reads=[B_mask], writes=[B_mask])
        p.emit("gp", lambda: nc.gpsimd.memset(ones_row_f, 1.0), writes=[B_ones])
        p.emit("gp", lambda: nc.gpsimd.memset(eps_col, LN_EPS), writes=[B_eps])

        with nc.allow_non_contiguous_dma("tiny bias column loads"):
            p.dma("sp", bias_pack[:, 0:16], bq_d.rearrange("(n p) -> p n", p=128),
                  writes=[B_bias], sem=None)
            p.dma("sp", bias_pack[:, 16:24], bl_d.rearrange("(n p) -> p n", p=128),
                  writes=[B_bias], sem=None)
            p.dma("sp", bias_pack[:, 24:26], bk_d.rearrange("(n p) -> p n", p=128),
                  writes=[B_bias], sem=None)
            p.dma("sp", bias_pack[:, 34:35], bkv2h_d.rearrange("(n p) -> p n", p=128),
                  writes=[B_bias], sem=None)
        p.dma("sp", bv_row, bv_d.rearrange("(o n) -> o n", o=1),
              writes=[B_bvrow], sem=None)
        p.dma("sp", bo_row, bo_d.rearrange("(o n) -> o n", o=1),
              writes=[B_borow], sem=None)
        p.dma("gp", wq2kv_bf, wq2kv_d, writes=[B_w2kv], sem=None)
        p.dma("sp", wq2kv_f32, wq2kv_d, writes=[B_w2kvf], sem=None)
        p.dma("gp", wkv2h_f[0:64, :], wkv2h_d, writes=[B_wkvf], sem=None)

        p.emit("pe", lambda: nc.tensor.matmul(
            psf(0)[:, 0:256], ones_row_f[0:1, :], bv_row,
            start=True, stop=True), reads=[B_ones, B_bvrow], writes=[PSB[0]])
        p.emit("ve", lambda: nc.vector.tensor_copy(out=bv_bc, in_=psf(0)[:, 0:256]),
               reads=[PSB[0]], writes=[B_bvbc])

        p.emit("pe", lambda: nc.tensor.transpose(
            psb(1)[:, 0:64], wkv2h_f[0:64, :], id_bf[0:64, 0:64]),
            reads=[B_wkvf, B_id], writes=[PSB[1]])
        p.emit("sc", lambda: nc.scalar.copy(out=wkv2h_t, in_=psb(1)[:, 0:64]),
               reads=[PSB[1]], writes=[B_wkvt])
        p.emit("ve", lambda: nc.vector.tensor_copy(
            out=bkv_bf, in_=bias_pack[:, 34:35]), reads=[B_bias], writes=[B_bkvbf])

        for h in range(NH):
            p.emit("pe", lambda h=h: nc.tensor.matmul(
                psf(2)[0:64, h:h + 1],
                wq2kv_f32, bias_pack[:, h:h + 1],
                start=True, stop=True), reads=[B_w2kvf, B_bias], writes=[PSB[2]])
        p.emit("ve", lambda: nc.vector.tensor_copy(
            out=bias_pack[0:64, 40:56], in_=psf(2)[0:64, 0:16]),
            reads=[PSB[2]], writes=[B_bias])
        for h in range(NH):
            c, sl = q_chunk(h), q_slot(h)
            p.emit("ve", lambda h=h, c=c, sl=sl: nc.vector.tensor_copy(
                out=bias_pack[sl * 64:sl * 64 + 64, 26 + c:27 + c],
                in_=bias_pack[0:64, 40 + h:41 + h]),
                reads=[B_bias], writes=[B_bias])

        wrow0, B_wr0, o_wr0 = sba("wrow0", D, BF16)
        wrow1, B_wr1, o_wr1 = sba("wrow1", D, BF16)
        wrow = [wrow0, wrow1]
        B_wrow = [B_wr0, B_wr1]
        B_wq_bf = Buf("wq_bf_dram")
        B_wl_blk = Buf("wl_blk_dram")
        B_wk_blk = Buf("wk_blk_dram")
        B_wv_nat = Buf("wv_nat_dram")
        B_wpr = Buf("wpr_dram")

        for k in range(16):
            s = k % 2
            p.dma("gp", wrow[s], wq_d[k * 128:(k + 1) * 128, :],
                  writes=[B_wrow[s]], sem=None)
            p.dma("sp", wq_bf[k * 128:(k + 1) * 128, :], wrow[s],
                  reads=[B_wrow[s]], writes=[B_wq_bf], sem=None)
        for k in range(16):
            s = k % 2
            p.dma("gp", wrow[s][:, 0:LAT], wl_d[k * 128:(k + 1) * 128, :],
                  writes=[B_wrow[s]], sem=None)
            p.dma("sp", wl_blk[:, :, k, :].rearrange("n p f -> p n f"),
                  wrow[s][:, 0:LAT].rearrange("p (n f) -> p n f", f=128),
                  reads=[B_wrow[s]], writes=[B_wl_blk], sem=None)
        for k in range(8):
            s = k % 2
            p.dma("gp", wrow[s][:, 0:256], wk_d[k * 128:(k + 1) * 128, :],
                  writes=[B_wrow[s]], sem=None)
            p.dma("sp", wk_blk[:, :, k, :].rearrange("n p f -> p n f"),
                  wrow[s][:, 0:256].rearrange("p (n f) -> p n f", f=128),
                  reads=[B_wrow[s]], writes=[B_wk_blk], sem=None)
        for k in range(8):
            s = k % 2
            p.dma("gp", wrow[s][:, 0:256], wv_d[k * 128:(k + 1) * 128, :],
                  writes=[B_wrow[s]], sem=None)
            p.dma("sp", wv_nat[k * 128:(k + 1) * 128, :], wrow[s][:, 0:256],
                  reads=[B_wrow[s]], writes=[B_wv_nat], sem=None)

        wpr_ev, B_wprev, o_wprev = sba("wpr_ev", D, BF16)
        p.emit("ve", lambda: nc.vector.tensor_copy(
            out=bkv_bf, in_=bias_pack[:, 34:35]), reads=[B_bias], writes=[B_bkvbf])
        p.emit("ve", lambda: nc.vector.tensor_copy(
            out=bpp_row, in_=bo_row), reads=[B_borow], writes=[B_bpprow])

        def prep_wo_step(h):
            s = h % 2
            p.dma("gp", wrow[s], wo_d[h * 128:(h + 1) * 128, :],
                  writes=[B_wrow[s]], sem=None)
            for n in range(4):
                p.emit("pe", lambda s=s, n=n: nc.tensor.matmul(
                    psf(n)[0:64, :], wkv2h_t, wrow[s][:, n * 512:(n + 1) * 512],
                    start=True, stop=True),
                    reads=[B_wrow[s], B_wkvt], writes=[PSB[n]])
                p.emit("pe", lambda s=s, n=n: nc.tensor.matmul(
                    psf(7)[0:1, :], bkv_bf, wrow[s][:, n * 512:(n + 1) * 512],
                    start=True, stop=True),
                    reads=[B_wrow[s], B_bkvbf], writes=[PSB[7]])
                p.emit("ve", lambda n=n: nc.vector.tensor_add(
                    out=bpp_row[:, n * 512:(n + 1) * 512],
                    in0=bpp_row[:, n * 512:(n + 1) * 512], in1=psf(7)[0:1, :]),
                    reads=[PSB[7], B_bpprow], writes=[B_bpprow])
            p.emit("sc", lambda: nc.scalar.copy(
                out=wpr_ev[0:64, :], in_=PSt[0:64, 0:2048]),
                reads=[PSB[0], PSB[1], PSB[2], PSB[3]], writes=[B_wprev])
            p.dma("sp", wpr_nat[h * 64:(h + 1) * 64, :], wpr_ev[0:64, :],
                  reads=[B_wprev], writes=[B_wpr], sem=None)

        def prep_bpp_bcast():
            for n in range(4):
                p.emit("pe", lambda n=n: nc.tensor.matmul(
                    psf(n)[:, :], ones_row_f[0:1, :],
                    bpp_row[:, n * 512:(n + 1) * 512],
                    start=True, stop=True), reads=[B_ones, B_bpprow],
                    writes=[PSB[n]])
                p.emit("ve", lambda n=n: nc.vector.tensor_copy(
                    out=bpp_bc[:, n * 512:(n + 1) * 512], in_=psf(n)[:, :]),
                    reads=[PSB[n]], writes=[B_bppbc])

        wqth0, B_wqth0, o_wqth0 = sba("wqth0", D, BF16)
        wqth1, B_wqth1, o_wqth1 = sba("wqth1", D, BF16)
        wqth = [wqth0, wqth1]
        B_wqth = [B_wqth0, B_wqth1]
        qp_ev, B_qpev, o_qpev = sba("qp_ev", 16 * 128, BF16)
        B_wqp_blk = Buf("wqp_blk_dram")
        def prep_wqp_step(c):
            for sl in range(2):
                h = (c % 4) + 4 * sl + 8 * (c // 4)
                s = (2 * c + sl) % 2
                p.dma("sp", wqth[s], wq_bf[:, h * 128:(h + 1) * 128],
                      reads=[B_wq_bf], writes=[B_wqth[s]], sem=None, transpose=True)
                for k in range(16):
                    bk = k % 2
                    p.emit("pe", lambda s=s, k=k, bk=bk: nc.tensor.matmul(
                        psf(bk)[:, 0:64],
                        wqth[s][:, k * 128:(k + 1) * 128],
                        wq2kv_bf, start=True, stop=True),
                        reads=[B_wqth[s], B_w2kv], writes=[PSB[bk]])
                    p.emit("sc", lambda k=k, sl=sl, bk=bk: nc.scalar.copy(
                        out=qp_ev[:, k * 128 + sl * 64:k * 128 + sl * 64 + 64],
                        in_=psf(bk)[:, 0:64]),
                        reads=[PSB[bk]], writes=[B_qpev])
            p.dma("sp", wqp_blk[c], qp_ev.rearrange("p (k f) -> p k f", f=128),
                  reads=[B_qpev], writes=[B_wqp_blk], sem=None)

        prep_steps = ([(lambda h=h: prep_wo_step(h)) for h in range(NH)]
                      + [prep_bpp_bcast]
                      + [(lambda c=c: prep_wqp_step(c)) for c in range(8)])
        pi_state = {"i": 0}

        def emit_prep(n):
            for _ in range(n):
                if pi_state["i"] >= len(prep_steps):
                    return
                prep_steps[pi_state["i"]]()
                pi_state["i"] += 1



        B_wgu_q = Buf("wgu_q_dram")
        B_wp_q = Buf("wp_q_dram")

        def quant_load(src_ap, s):
            p.dma("gp", qrow_[s], src_ap, writes=[B_qrow[s]], sem=None)

        def quant_tile(src_ap, dst_ap, dst_buf, s, transpose_to=None):
            qr, Bqr = qrow_[s], B_qrow[s]
            qsc, Bqsc = qmisc["qsc"][s][0], qmisc["qsc"][s][1]
            qscb, Bqscb = qmisc["qscb"][s][0], qmisc["qscb"][s][1]
            qthr, Bqthr = qmisc["qthr"][s][0], qmisc["qthr"][s][1]
            qaw, Bqaw = qmisc["qaw"][s][0], qmisc["qaw"][s][1]
            qmask, Bqmask = qmisc["qmask"][s][0], qmisc["qmask"][s][1]
            qsgn, Bqsgn = qmisc["qsgn"][s][0], qmisc["qsgn"][s][1]
            qwh, Bqwh = qmisc["qwh"][s][0], qmisc["qwh"][s][1]
            p.emit("ve", lambda: nc.vector.tensor_reduce(
                out=qsc, in_=qr.rearrange("p (g w) -> p g w", w=GROUP),
                axis=AX.X, op=ALU.add, apply_absolute_value=True),
                reads=[Bqr], writes=[Bqsc])
            p.emit("ve", lambda: nc.vector.tensor_scalar(
                out=qscb, in0=qsc, scalar1=1.0 / GROUP, scalar2=1e-8,
                op0=ALU.mult, op1=ALU.max), reads=[Bqsc], writes=[Bqscb])
            p.emit("ve", lambda: nc.vector.tensor_scalar(
                out=qthr, in0=qscb, scalar1=QTHR, scalar2=None, op0=ALU.mult),
                reads=[Bqscb], writes=[Bqthr])
            p.emit("sc", lambda: nc.scalar.activation(out=qaw, in_=qr, func=AF.Abs),
                   reads=[Bqr], writes=[Bqaw])
            p.emit("sc", lambda: nc.scalar.activation(out=qsgn, in_=qr, func=AF.Sign),
                   reads=[Bqr], writes=[Bqsgn])
            a3, t3 = broadcast_tensor_aps(
                qaw.rearrange("p (g w) -> p g w", w=GROUP),
                qthr.rearrange("p (g w) -> p g w", w=1))
            p.emit("ve", lambda: nc.vector.tensor_tensor(
                out=qmask.rearrange("p (g w) -> p g w", w=GROUP),
                in0=a3, in1=t3, op=ALU.is_gt),
                reads=[Bqaw, Bqthr], writes=[Bqmask])
            m3, s3 = broadcast_tensor_aps(
                qmask.rearrange("p (g w) -> p g w", w=GROUP),
                qscb.rearrange("p (g w) -> p g w", w=1))
            p.emit("gp", lambda: nc.gpsimd.tensor_tensor(
                out=qmask.rearrange("p (g w) -> p g w", w=GROUP),
                in0=m3, in1=s3, op=ALU.mult),
                reads=[Bqmask, Bqscb], writes=[Bqmask])
            p.emit("ve", lambda: nc.vector.tensor_mul(
                out=qwh, in0=qmask, in1=qsgn), reads=[Bqmask, Bqsgn], writes=[Bqwh])
            p.emit("ve", lambda: nc.vector.tensor_sub(
                out=qwh, in0=qwh, in1=qr), reads=[Bqwh, Bqr], writes=[Bqwh])
            p.emit("ve", lambda: nc.vector.tensor_add(
                out=qwh, in0=qwh, in1=qr), reads=[Bqwh, Bqr], writes=[Bqwh])
            if transpose_to is None:
                p.dma("sp", dst_ap, qwh, reads=[Bqwh], writes=[dst_buf], sem=None)
            else:
                for hc in range(QW // 128):
                    p.emit("pe", lambda hc=hc: nc.tensor.transpose(
                        psb(7)[:, hc * 128:(hc + 1) * 128],
                        qwh[:, hc * 128:(hc + 1) * 128], id_bf),
                        reads=[Bqwh, B_id], writes=[PSB[7]])
                p.emit("ve", lambda s=s: nc.vector.tensor_copy(
                    out=qtr_[s], in_=psb(7)[:, 0:QW]),
                    reads=[PSB[7]], writes=[B_qtr[s]])
                p.dma("sp", transpose_to,
                      qtr_[s].rearrange("p (c f) -> p c f", f=128),
                      reads=[B_qtr[s]], writes=[dst_buf], sem=None)

        B_wguq_sh = Buf("wguq_sh_dram")
        B_wpq_sh = Buf("wpq_sh_dram")
        quant_steps = []
        for r in range(2 * HF // 8 // 128):
            for c in range(D // QW):
                quant_steps.append((wgu_sh_d[r * 128:(r + 1) * 128, c * QW:(c + 1) * QW],
                                    wguq_sh[r * 128:(r + 1) * 128, c * QW:(c + 1) * QW],
                                    B_wguq_sh))
        for r in range(D // 128):
            dst3 = wpqt_sh.rearrange("(c p) d -> p c d", p=128)[
                :, :, r * 128:(r + 1) * 128]
            quant_steps.append((wp_sh_d[r * 128:(r + 1) * 128, :], None,
                                B_wpq_sh, dst3))
        qi_state = {"i": 0, "loaded": 0}
        QLOOK = NQSLOT - 1

        def emit_quant(n):
            for _ in range(n):
                i = qi_state["i"]
                if i >= len(quant_steps):
                    return
                while (qi_state["loaded"] < len(quant_steps)
                       and qi_state["loaded"] <= i + QLOOK):
                    j = qi_state["loaded"]
                    quant_load(quant_steps[j][0], j % NQSLOT)
                    qi_state["loaded"] += 1
                step = quant_steps[i]
                if len(step) == 3:
                    quant_tile(step[0], step[1], step[2], i % NQSLOT)
                else:
                    quant_tile(step[0], None, step[2], i % NQSLOT,
                               transpose_to=step[3])
                qi_state["i"] += 1


        def ln_token_major(x_buf, x_ap, out_ap, out_buf, eps, sl=0):
            lnstat = lnstat_[sl]
            B_lnstat = B_lnstat_[sl]
            for c4 in range(4):
                p.emit("ve", lambda c4=c4: nc.vector.bn_stats(
                    out=lnstat[:, 8 + c4 * 6:14 + c4 * 6],
                    in_=x_ap[:, c4 * 512:(c4 + 1) * 512]),
                    reads=[x_buf], writes=[B_lnstat])
            p.emit("ve", lambda: nc.vector.bn_aggr(
                out=lnstat[:, 0:2], in_=lnstat[:, 8:32]),
                reads=[B_lnstat], writes=[B_lnstat])
            p.emit("sc", lambda: nc.scalar.activation(
                out=lnstat[:, 7:8], in_=lnstat[:, 1:2], func=AF.Ln, bias=eps_col),
                reads=[B_lnstat, B_eps], writes=[B_lnstat])
            p.emit("sc", lambda: nc.scalar.activation(
                out=lnstat[:, 4:5], in_=lnstat[:, 7:8], func=AF.Exp, scale=-0.5),
                reads=[B_lnstat], writes=[B_lnstat])
            p.emit("ve", lambda: nc.vector.tensor_scalar(
                out=lnstat[:, 5:6], in0=lnstat[:, 0:1], scalar1=lnstat[:, 4:5],
                scalar2=-1.0, op0=ALU.mult, op1=ALU.mult),
                reads=[B_lnstat], writes=[B_lnstat])
            p.emit("sc", lambda: nc.scalar.activation(
                out=out_ap, in_=x_ap, func=AF.Identity,
                bias=lnstat[:, 5:6], scale=lnstat[:, 4:5]),
                reads=[x_buf, B_lnstat], writes=[out_buf])

        ln1t, B_ln1t, o_ln1t = sba("ln1t", 16 * T, BF16, side="high")
        ln1t3 = ln1t.rearrange("p (c t) -> p c t", t=T)
        xrow_, B_xrow, o_xrow = [], [], []
        for i in range(3):
            a, b, o = sba(f"xrow{i}", D, F32)
            xrow_.append(a); B_xrow.append(b); o_xrow.append(o)
        lntok_, B_lntok, o_lntok = [], [], []
        for i in range(2):
            a, b, o = sba(f"lntok{i}", D, BF16)
            lntok_.append(a); B_lntok.append(b); o_lntok.append(o)

        def transpose_to_feat(src_ap, src_buf, dst3, dst_buf, tc, banks=(5, 6)):
            for cg in range(4):
                bk = banks[cg % 2]
                for c4 in range(4):
                    c = cg * 4 + c4
                    p.emit("pe", lambda bk=bk, c4=c4, c=c: nc.tensor.transpose(
                        psb(bk)[:, c4 * 128:(c4 + 1) * 128],
                        src_ap[:, c * 128:(c + 1) * 128], id_bf),
                        reads=[src_buf, B_id], writes=[PSB[bk]])
                out3 = dst3[:, cg * 4:(cg + 1) * 4, tc * 128:(tc + 1) * 128]
                p.emit("ve", lambda bk=bk, out3=out3: nc.vector.tensor_copy(
                    out=out3, in_=psb(bk)[:, 0:512].rearrange("p (c t) -> p c t", t=128)),
                    reads=[PSB[bk]], writes=[dst_buf])

        def xload(tc):
            p.dma("sp", xrow_[tc % 3], x_d[tc * 128:(tc + 1) * 128, :],
                  writes=[B_xrow[tc % 3]], sem=None)
        xload(0)
        xload(1)
        for tc in range(16):
            s = tc % 3
            sp2 = tc % 2
            if tc + 2 < 16:
                xload(tc + 2)
            ln_token_major(B_xrow[s], xrow_[s], lntok_[sp2], B_lntok[sp2],
                           LN_EPS, sl=sp2)
            transpose_to_feat(lntok_[sp2], B_lntok[sp2], ln1t3, B_ln1t, tc)
            emit_quant(1)
            emit_prep(2)

        if debug:
            p.dma("sp", dbg["dbg_ln1t"].rearrange("(c p) t -> p (c t)", p=128),
                  ln1t, reads=[B_ln1t], writes=[Buf("dbg")], sem=None)

        emit_prep(len(prep_steps))
        for o in [o_wqth0, o_wqth1, o_qpev, o_wprev, o_wr0, o_wr1,
                  o_bo, o_bpp, o_bv]:
            A.release(o)
        for o in o_xrow + o_lntok:
            A.release(o)
        q2t, B_q2t, o_q2t = sba("q2t", 8 * T, BF16, side="high")
        latt, B_latt, o_latt = sba("latt", 8 * T, BF16, side="high")
        kt, B_kt, o_kt = sba("kt", 2 * T, BF16, side="high")
        vtok, B_vtok, o_vtok = sba("vtok", 16 * 256, BF16, side="high")
        q2t3 = q2t.rearrange("p (c t) -> p c t", t=T)
        latt3 = latt.rearrange("p (c t) -> p c t", t=T)
        kt3 = kt.rearrange("p (c t) -> p c t", t=T)
        wblk_, B_wblk, o_wblk = [], [], []
        for i in range(3):
            a, b, o = sba(f"wblk{i}", 16 * 128, BF16)
            wblk_.append(a); B_wblk.append(b); o_wblk.append(o)

        def proj_feat(n_groups, n_k, load_fn, load_buf, rhs3, rhs_buf, out_sb,
                      out_buf, bias_col_fn, act, qsteps=0):
            def wload(n):
                s = n % 3
                p.dma("sp", wblk_[s][:, 0:n_k * 128].rearrange(
                    "p (k f) -> p k f", f=128), load_fn(n),
                    reads=[load_buf], writes=[B_wblk[s]], sem=None)
            wload(0)
            if n_groups > 1:
                wload(1)
            for n in range(n_groups):
                s = n % 3
                if n + 2 < n_groups:
                    wload(n + 2)
                for tq in range(4):
                    bk = tq % 4
                    for k in range(n_k):
                        p.emit("pe", lambda s=s, bk=bk, k=k, tq=tq: nc.tensor.matmul(
                            psf(bk), wblk_[s][:, k * 128:(k + 1) * 128],
                            rhs3[:, k, tq * 512:(tq + 1) * 512],
                            start=(k == 0), stop=(k == n_k - 1)),
                            reads=[B_wblk[s], rhs_buf], writes=[PSB[bk]])
                    p.emit("sc", lambda bk=bk, n=n, tq=tq: nc.scalar.activation(
                        out=out_sb[:, n * T + tq * 512:n * T + (tq + 1) * 512],
                        in_=psf(bk), func=act, bias=bias_col_fn(n)),
                        reads=[PSB[bk], B_bias], writes=[out_buf])
                emit_quant(qsteps)

        proj_feat(8, 16, lambda n: wqp_blk[n], B_wqp_blk, ln1t3, B_ln1t,
                  q2t, B_q2t, lambda n: bias_pack[:, 26 + n:27 + n], AF.Identity,
                  qsteps=1)
        proj_feat(8, 16, lambda n: wl_blk[n, :, :, :], B_wl_blk, ln1t3, B_ln1t,
                  latt, B_latt, lambda n: bias_pack[:, 16 + n:17 + n], AF.Silu,
                  qsteps=1)
        proj_feat(2, 8, lambda n: wk_blk[n, :, :, :], B_wk_blk, latt3, B_latt,
                  kt, B_kt, lambda n: bias_pack[:, 24 + n:25 + n], AF.Identity,
                  qsteps=1)

        p.dma("sp", wblk_[0][:, 0:8 * 256].rearrange("p (k f) -> p k f", f=256),
              wv_nat.rearrange("(k p) f -> p k f", p=128),
              reads=[B_wv_nat], writes=[B_wblk[0]], sem=None)
        wv3 = wblk_[0][:, 0:8 * 256].rearrange("p (k f) -> p k f", f=256)
        for tv in range(16):
            bk = 2 + tv % 2
            for k in range(8):
                p.emit("pe", lambda bk=bk, k=k, tv=tv: nc.tensor.matmul(
                    psf(bk)[:, 0:256], latt3[:, k, tv * 128:(tv + 1) * 128],
                    wv3[:, k, :], start=(k == 0), stop=(k == 7)),
                    reads=[B_latt, B_wblk[0]], writes=[PSB[bk]])
            p.emit("ve", lambda bk=bk, tv=tv: nc.vector.tensor_add(
                out=vtok[:, tv * 256:(tv + 1) * 256], in0=psf(bk)[:, 0:256],
                in1=bv_bc), reads=[PSB[bk], B_bvbc], writes=[B_vtok])
            emit_quant(1)

        if debug:
            p.dma("sp", dbg["dbg_q2t"].rearrange("(c p) t -> p (c t)", p=128),
                  q2t, reads=[B_q2t], writes=[Buf("dbg")], sem=None)
            p.dma("sp", dbg["dbg_latt"].rearrange("(c p) t -> p (c t)", p=128),
                  latt, reads=[B_latt], writes=[Buf("dbg")], sem=None)
            p.dma("sp", dbg["dbg_kt"].rearrange("(c p) t -> p (c t)", p=128),
                  kt, reads=[B_kt], writes=[Buf("dbg")], sem=None)
            p.dma("sp", dbg["dbg_v"].rearrange("(c p) f -> p (c f)", p=128),
                  vtok, reads=[B_vtok], writes=[Buf("dbg")], sem=None)

        A.release(o_ln1t)
        A.release(o_latt)
        A.release(o_bvbc)
        emit_quant(len(quant_steps))
        cc1 = p.new_sem("cc1")
        cc2 = p.new_sem("cc2")
        p.emit("gp", lambda: nc.gpsimd.collective_compute(
            "AllGather", ALU.bypass, replica_groups=[list(range(NCORES))],
            ins=[wguq_sh.opt()], outs=[wgu_q.opt()]),
            reads=[B_wguq_sh], writes=[B_wgu_q], sem="cc1", dinc=1)
        p.emit("gp", lambda: nc.gpsimd.collective_compute(
            "AllGather", ALU.bypass, replica_groups=[list(range(NCORES))],
            ins=[wpqt_sh.opt()], outs=[wpt_full.opt()]),
            reads=[B_wpq_sh], writes=[B_wp_q], sem="cc2", dinc=1)
        for i in range(NQSLOT):
            A.release(o_qrow[i])
        for nm in qmisc:
            for i in range(NQSLOT):
                A.release(qmisc[nm][i][2])


        ctxt, B_ctxt, o_ctxt = sba("ctxt", 8 * T, BF16, side="high")
        ctxt3 = ctxt.rearrange("p (c t) -> p c t", t=T)
        NW = 5
        sc_, B_scsb, o_scsb = [], [], []
        pr_, B_prsb, o_prsb = [], [], []
        pr2_, B_pr2sb, o_pr2sb = [], [], []
        smax_, B_smax_ = [], []
        prT_, B_prT, o_prT = [], [], []
        for i in range(NW):
            a, b, o = sba(f"sc_sb{i}", T, F32)
            sc_.append(a); B_scsb.append(b); o_scsb.append(o)
            a, b, o = sba(f"pr_sb{i}", T, BF16)
            pr_.append(a); B_prsb.append(b); o_prsb.append(o)
            a, b, o = sba(f"pr2_sb{i}", T, BF16)
            pr2_.append(a); B_pr2sb.append(b); o_pr2sb.append(o)
            a, b, o = sba(f"smax{i}", 4, F32)
            smax_.append(a); B_smax_.append(b)
            a, b, o = sba(f"prT{i}", 1024, BF16)
            prT_.append(a); B_prT.append(b); o_prT.append(o)

        units = [(qi, h) for qi in range(16) for h in range(NH)]

        def stageA(u):
            qi, h = units[u]
            nkt = qi + 1
            g = h // 4
            hp = u % NW
            sc_sb, Bsc = sc_[hp], B_scsb[hp]
            eeng = "sc" if h % 2 == 0 else "ve"
            for jg in range(ceil_div(nkt, 4)):
                bk = jg % 2
                js = list(range(jg * 4, min(jg * 4 + 4, nkt)))
                p.emit("pe", lambda h=h, g=g, qi=qi, js=js, bk=bk:
                       nc.tensor.matmul(
                    psf(bk)[:, 0:len(js) * 128],
                    q2t3[q_slot(h) * 64:q_slot(h) * 64 + 64, q_chunk(h),
                         qi * 128:(qi + 1) * 128],
                    kt3[(g % 2) * 64:(g % 2) * 64 + 64, g // 2,
                        js[0] * 128:(js[-1] + 1) * 128],
                    start=True, stop=True),
                    reads=[B_q2t, B_kt], writes=[PSB[bk]])
                jlo, jhi = js[0], js[-1]
                ndiag = jhi == qi
                if not ndiag or jhi > jlo:
                    e_hi = jhi * 128 if ndiag else (jhi + 1) * 128
                    npc = (jhi - jlo) if ndiag else (jhi - jlo + 1)
                    if eeng == "sc":
                        p.emit("sc", lambda bk=bk, jlo=jlo, npc=npc, e_hi=e_hi,
                               sc_sb=sc_sb: nc.scalar.activation(
                            out=sc_sb[:, jlo * 128:e_hi],
                            in_=psf(bk)[:, 0:npc * 128],
                            func=AF.Identity, scale=0.125),
                            reads=[PSB[bk]], writes=[Bsc])
                    else:
                        p.emit("ve", lambda bk=bk, jlo=jlo, npc=npc, e_hi=e_hi,
                               sc_sb=sc_sb: nc.vector.tensor_scalar(
                            out=sc_sb[:, jlo * 128:e_hi],
                            in0=psf(bk)[:, 0:npc * 128],
                            scalar1=0.125, scalar2=None, op0=ALU.mult),
                            reads=[PSB[bk]], writes=[Bsc])
                if ndiag:
                    p.emit("ve", lambda bk=bk, jlo=jlo, jhi=jhi,
                           sc_sb=sc_sb: nc.vector.scalar_tensor_tensor(
                        out=sc_sb[:, jhi * 128:(jhi + 1) * 128],
                        in0=psf(bk)[:, (jhi - jlo) * 128:(jhi - jlo + 1) * 128],
                        scalar=0.125, in1=mask_t, op0=ALU.mult, op1=ALU.add),
                        reads=[PSB[bk], B_mask], writes=[Bsc])

        def stageB1(u):
            qi, h = units[u]
            L = (qi + 1) * 128
            hp = u % NW
            sc_sb, Bsc = sc_[hp], B_scsb[hp]
            smax, Bsm = smax_[hp], B_smax_[hp]
            p.emit("ve", lambda L=L, sc_sb=sc_sb, smax=smax:
                   nc.vector.tensor_reduce(
                out=smax[:, 0:1], in_=sc_sb[:, 0:L], axis=AX.X, op=ALU.max,
                negate=True), reads=[Bsc], writes=[Bsm])

        def stageBX(u):
            qi, h = units[u]
            L = (qi + 1) * 128
            hp = u % NW
            sc_sb, Bsc = sc_[hp], B_scsb[hp]
            pr_sb, Bpr = pr_[hp], B_prsb[hp]
            smax, Bsm = smax_[hp], B_smax_[hp]
            p.emit("sc", lambda L=L, sc_sb=sc_sb, pr_sb=pr_sb, smax=smax:
                   nc.scalar.activation(
                out=pr_sb[:, 0:L], in_=sc_sb[:, 0:L], func=AF.Exp,
                bias=smax[:, 0:1], accum_out=smax[:, 2:3]),
                reads=[Bsc, Bsm], writes=[Bpr, Bsm])

        def stageB2(u):
            qi, h = units[u]
            L = (qi + 1) * 128
            hp = u % NW
            pr_sb, Bpr = pr_[hp], B_prsb[hp]
            pr2_sb, Bpr2 = pr2_[hp], B_pr2sb[hp]
            smax, Bsm = smax_[hp], B_smax_[hp]
            p.emit("ve", lambda smax=smax: nc.vector.reciprocal(
                out=smax[:, 3:4], in_=smax[:, 2:3]), reads=[Bsm], writes=[Bsm])
            if h % 2 == 0:
                p.emit("sc", lambda L=L, pr_sb=pr_sb, pr2_sb=pr2_sb, smax=smax:
                       nc.scalar.activation(
                    out=pr2_sb[:, 0:L], in_=pr_sb[:, 0:L], func=AF.Identity,
                    scale=smax[:, 3:4]),
                    reads=[Bpr, Bsm], writes=[Bpr2])
            else:
                p.emit("ve", lambda L=L, pr_sb=pr_sb, pr2_sb=pr2_sb, smax=smax:
                       nc.vector.tensor_scalar(
                    out=pr2_sb[:, 0:L], in0=pr_sb[:, 0:L], scalar1=smax[:, 3:4],
                    scalar2=None, op0=ALU.mult),
                    reads=[Bpr, Bsm], writes=[Bpr2])

        def stageC(u):
            qi, h = units[u]
            nkt = qi + 1
            g = h // 4
            hp = u % NW
            pr2_sb, Bpr2 = pr2_[hp], B_pr2sb[hp]
            cbk = 4 + h % 2
            eeng = "sc" if h % 2 == 0 else "ve"
            ngr = ceil_div(nkt, 4)
            jss = [list(range(jg * 4, min(jg * 4 + 4, nkt))) for jg in range(ngr)]

            def emit_tr(jg):
                tbk = 2 + jg % 2
                for j in jss[jg]:
                    p.emit("pe", lambda tbk=tbk, j=j, pr2_sb=pr2_sb:
                           nc.tensor.transpose(
                        psb(tbk)[:, (j % 4) * 128:(j % 4) * 128 + 128],
                        pr2_sb[:, j * 128:(j + 1) * 128], id_bf),
                        reads=[Bpr2, B_id], writes=[PSB[tbk]])
                js = jss[jg]
                if eeng == "sc":
                    p.emit("sc", lambda tbk=tbk, hp=hp, js=js: nc.scalar.copy(
                        out=prT_[hp][:, (js[0] % 8) * 128:
                                     (js[0] % 8) * 128 + len(js) * 128],
                        in_=psb(tbk)[:, 0:len(js) * 128]),
                        reads=[PSB[tbk]], writes=[B_prT[hp]])
                else:
                    p.emit("ve", lambda tbk=tbk, hp=hp, js=js: nc.vector.tensor_copy(
                        out=prT_[hp][:, (js[0] % 8) * 128:
                                     (js[0] % 8) * 128 + len(js) * 128],
                        in_=psb(tbk)[:, 0:len(js) * 128]),
                        reads=[PSB[tbk]], writes=[B_prT[hp]])

            def emit_ctx(jg):
                js = jss[jg]
                for j in js:
                    p.emit("pe", lambda cbk=cbk, hp=hp, j=j, js=js, g=g,
                           nkt=nkt: nc.tensor.matmul(
                        psf(cbk)[0:64, 0:128],
                        vtok[:, j * 256 + g * 64:j * 256 + g * 64 + 64],
                        prT_[hp][:, (js[0] % 8) * 128 + (j - js[0]) * 128:
                                 (js[0] % 8) * 128 + (j - js[0] + 1) * 128],
                        start=(j == 0), stop=(j == nkt - 1)),
                        reads=[B_vtok, B_prT[hp]], writes=[PSB[cbk]])

            for jg in range(ngr):
                emit_tr(jg)
                if jg >= 1:
                    emit_ctx(jg - 1)
            emit_ctx(ngr - 1)
            p.emit("ve", lambda cbk=cbk, h=h, qi=qi: nc.vector.tensor_copy(
                out=ctxt3[(h % 2) * 64:(h % 2) * 64 + 64, h // 2,
                          qi * 128:(qi + 1) * 128],
                in_=psf(cbk)[0:64, 0:128]),
                reads=[PSB[cbk]], writes=[B_ctxt])

        for k in range(len(units) + 4):
            if k < len(units):
                stageA(k)
            if 1 <= k < len(units) + 1:
                stageB1(k - 1)
            if 2 <= k < len(units) + 2:
                stageBX(k - 2)
            if 3 <= k < len(units) + 3:
                stageB2(k - 3)
            if k >= 4:
                stageC(k - 4)

        if debug:
            p.dma("sp", dbg["dbg_ctxt"].rearrange("(c p) t -> p (c t)", p=128),
                  ctxt, reads=[B_ctxt], writes=[Buf("dbg")], sem=None)

        A.release(o_q2t)
        A.release(o_kt)
        A.release(o_vtok)
        for o in o_scsb + o_prsb + o_pr2sb:
            A.release(o)

        wpr_sb, B_wprsb, o_wprsb = sba("wpr_sb", 8 * D, BF16, side="high")
        wpr3 = wpr_sb.rearrange("p (k f) -> p k f", f=D)
        for k in range(8):
            p.dma("sp", wpr3[:, k, :],
                  wpr_nat[:, :].rearrange("(k p) f -> p k f", p=128)[:, k, :],
                  reads=[B_wpr], writes=[B_wprsb], sem=None)
        xrow_, B_xrow, o_xrow = [], [], []
        for i in range(2):
            a, b, o = sba(f"xrow_c{i}", D, F32)
            xrow_.append(a); B_xrow.append(b); o_xrow.append(o)
        ao_, B_ao, o_ao = [], [], []
        for i in range(2):
            a, b, o = sba(f"ao{i}", D, F32)
            ao_.append(a); B_ao.append(b); o_ao.append(o)
        ztok_, B_zs, o_zs = [], [], []
        for i in range(2):
            a, b, o = sba(f"z_tok{i}", D, F32)
            ztok_.append(a); B_zs.append(b); o_zs.append(o)
        s2_, B_s2, o_s2 = [], [], []
        for i in range(2):
            a, b, o = sba(f"s2_{i}", D, F32)
            s2_.append(a); B_s2.append(b); o_s2.append(o)
        ln3_, B_ln3, o_ln3 = [], [], []
        for i in range(2):
            a, b, o = sba(f"ln3_{i}", D, BF16)
            ln3_.append(a); B_ln3.append(b); o_ln3.append(o)
        B_s2_dram = Buf("s2_dram")
        B_ln3t_dram = Buf("ln3t_dram")
        ln3t_v = ln3t_dram.rearrange("(c p) t -> p c t", p=128)

        def c1_s1(tc):
            s = tc % 2
            p.dma("sp", xrow_[s], x_d[tc * 128:(tc + 1) * 128, :],
                  writes=[B_xrow[s]], sem=None)
            for half in range(2):
                b0 = half * 2
                for k in range(8):
                    for ns in range(2):
                        p.emit("pe", lambda b0=b0, s=s, half=half, k=k, ns=ns, tc=tc:
                               nc.tensor.matmul(
                            psf(b0 + ns), ctxt3[:, k, tc * 128:(tc + 1) * 128],
                            wpr3[:, k, half * 1024 + ns * 512:
                                 half * 1024 + (ns + 1) * 512],
                            start=(k == 0), stop=(k == 7)),
                            reads=[B_ctxt, B_wprsb], writes=[PSB[b0 + ns]])
                p.emit("ve", lambda b0=b0, s=s, half=half: nc.vector.tensor_add(
                    out=ao_[s][:, half * 1024:(half + 1) * 1024], in0=psf(b0, 2),
                    in1=bpp_bc[:, half * 1024:(half + 1) * 1024]),
                    reads=[PSB[b0], PSB[b0 + 1], B_bppbc], writes=[B_ao[s]])

        def c1_s2(tc):
            s = tc % 2
            ln_token_major(B_ao[s], ao_[s], ztok_[s], B_zs[s], LN_EPS, sl=s)
            p.emit("ve", lambda s=s: nc.vector.tensor_add(
                out=s2_[s], in0=ztok_[s], in1=xrow_[s]),
                reads=[B_zs[s], B_xrow[s]], writes=[B_s2[s]])
            p.dma("sp", s2_dram[tc * 128:(tc + 1) * 128, :], s2_[s],
                  reads=[B_s2[s]], writes=[B_s2_dram], sem=None)
            ln_token_major(B_s2[s], s2_[s], ln3_[s], B_ln3[s], LN_EPS, sl=s)
            for cg in range(4):
                bk = 5 + cg % 2
                sp_ = cg % 2
                for c4 in range(4):
                    c = cg * 4 + c4
                    p.emit("pe", lambda s=s, bk=bk, c4=c4, c=c: nc.tensor.transpose(
                        psb(bk)[:, c4 * 128:(c4 + 1) * 128],
                        ln3_[s][:, c * 128:(c + 1) * 128], id_bf),
                        reads=[B_ln3[s], B_id], writes=[PSB[bk]])
                p.emit("ve", lambda bk=bk, sp_=sp_: nc.vector.tensor_copy(
                    out=prT_[sp_][:, 0:512], in_=psb(bk)[:, 0:512]),
                    reads=[PSB[bk]], writes=[B_prT[sp_]])
                p.dma("sp", ln3t_v[:, cg * 4:(cg + 1) * 4, tc * 128:(tc + 1) * 128],
                      prT_[sp_][:, 0:512].rearrange("p (c t) -> p c t", t=128),
                      reads=[B_prT[sp_]], writes=[B_ln3t_dram], sem=None)

        c1_s1(0)
        for tc in range(16):
            if tc + 1 < 16:
                c1_s1(tc + 1)
            c1_s2(tc)
        if debug:
            p.dma("sp", dbg["dbg_s2"], s2_dram, reads=[B_s2_dram], writes=[Buf("dbg")], sem=None)
            p.dma("sp", dbg["dbg_ln3t"], ln3t_dram, reads=[B_ln3t_dram], writes=[Buf("dbg")], sem=None)

        if debug:
            p.dma("sp", dbg["dbg_wguq"], wgu_q, reads=[B_wgu_q], writes=[Buf("dbg")], sem=None)

        A.release(o_ctxt)
        A.release(o_wprsb)
        A.release(o_bppbc)
        for o in o_ao + o_zs + o_s2 + o_ln3 + o_prT + o_xrow + [o_trev[0], o_trev[1]]:
            A.release(o)
        for o in o_wblk:
            A.release(o)

        ln3t_sb, B_ln3sb, o_ln3sb = sba("ln3t_sb", 16 * T, BF16, side="high")
        ln3s3 = ln3t_sb.rearrange("p (c t) -> p c t", t=T)
        for c4 in range(4):
            p.dma("sp", ln3s3[:, c4 * 4:(c4 + 1) * 4, :],
                  ln3t_dram.rearrange("(c p) t -> p c t", p=128)[:, c4 * 4:(c4 + 1) * 4, :],
                  reads=[B_ln3t_dram], writes=[B_ln3sb], sem=None)
        wgt_, B_wgt, o_wgt = [], [], []
        wut_, B_wut, o_wut = [], [], []
        for i in range(2):
            a, b, o = sba(f"wgt{i}", 16 * 512, BF16, side="high")
            wgt_.append(a); B_wgt.append(b); o_wgt.append(o)
            a, b, o = sba(f"wut{i}", 16 * 512, BF16, side="high")
            wut_.append(a); B_wut.append(b); o_wut.append(o)
        sg_sb, B_sg, o_sg = sba("sg_sb", 1024, BF16)
        ha_, B_ha, o_ha = [], [], []
        for i in range(2):
            a, b, o = sba(f"ha{i}", 1024, BF16)
            ha_.append(a); B_ha.append(b); o_ha.append(o)
        h2_sb, B_h2, o_h2 = sba("h2_sb", 1024, F32)
        acc_sb, B_acc, o_acc = sba("acc_sb", T, F32)
        B_hact_dram = Buf("hact_dram")
        p.emit("ve", lambda: nc.vector.memset(acc_sb, 0.0), writes=[B_acc])

        for jq in range(16):
            s = jq % 2
            for c in range(16):
                p.dma("sp", wgt_[s][:, c * 512:(c + 1) * 512],
                      wgu_q[jq * 512:(jq + 1) * 512, c * 128:(c + 1) * 128],
                      reads=[B_wgu_q], writes=[B_wgt[s]], sem=None, transpose=True)
                p.dma("sp", wut_[s][:, c * 512:(c + 1) * 512],
                      wgu_q[HF + jq * 512:HF + (jq + 1) * 512, c * 128:(c + 1) * 128],
                      reads=[B_wgu_q], writes=[B_wut[s]], sem=None, transpose=True)
            for jj in range(4):
                j = jq * 4 + jj
                gb = (j % 2) * 2
                ub = 4 + (j % 2) * 2
                for half in range(2):
                    for k in range(16):
                        for ns in range(2):
                            p.emit("pe", lambda s=s, gb=gb, jj=jj, k=k, ns=ns,
                                   half=half: nc.tensor.matmul(
                                psf(gb + ns),
                                wgt_[s][:, k * 512 + jj * 128:k * 512 + (jj + 1) * 128],
                                ln3s3[:, k, half * 1024 + ns * 512:
                                      half * 1024 + (ns + 1) * 512],
                                start=(k == 0), stop=(k == 15)),
                                reads=[B_wgt[s], B_ln3sb], writes=[PSB[gb + ns]])
                        for ns in range(2):
                            p.emit("pe", lambda s=s, ub=ub, jj=jj, k=k, ns=ns,
                                   half=half: nc.tensor.matmul(
                                psf(ub + ns),
                                wut_[s][:, k * 512 + jj * 128:k * 512 + (jj + 1) * 128],
                                ln3s3[:, k, half * 1024 + ns * 512:
                                      half * 1024 + (ns + 1) * 512],
                                start=(k == 0), stop=(k == 15)),
                                reads=[B_wut[s], B_ln3sb], writes=[PSB[ub + ns]])
                    p.emit("sc", lambda gb=gb: nc.scalar.activation(
                        out=sg_sb, in_=psf(gb, 2), func=AF.Silu),
                        reads=[PSB[gb], PSB[gb + 1]], writes=[B_sg])
                    hs = half % 2
                    p.emit("ve", lambda ub=ub, hs=hs: nc.vector.tensor_mul(
                        out=ha_[hs], in0=psf(ub, 2), in1=sg_sb),
                        reads=[PSB[ub], PSB[ub + 1], B_sg], writes=[B_ha[hs]])
                    p.dma("gp", hact_dram[j * 128:(j + 1) * 128,
                                          half * 1024:(half + 1) * 1024],
                          ha_[hs], reads=[B_ha[hs]], writes=[B_hact_dram],
                          sem=None)
                    p.emit("gp", lambda hs=hs: nc.gpsimd.tensor_mul(
                        out=h2_sb, in0=ha_[hs], in1=ha_[hs]),
                        reads=[B_ha[hs]], writes=[B_h2])
                    p.emit("ve", lambda half=half: nc.vector.tensor_add(
                        out=acc_sb[:, half * 1024:(half + 1) * 1024],
                        in0=acc_sb[:, half * 1024:(half + 1) * 1024], in1=h2_sb),
                        reads=[B_h2, B_acc], writes=[B_acc])

        invp, B_inv, o_invp = sba("inv_pack", T, F32)
        inv_row = invp[0:1, :]
        inv_bc, B_invbc, o_invbc = sba("inv_bc", T, F32)
        onec, B_onec, o_onec = sba("onec", 1, F32)
        p.emit("gp", lambda: nc.gpsimd.memset(onec, 1.0), writes=[B_onec])
        for ns in range(4):
            p.emit("pe", lambda ns=ns: nc.tensor.matmul(
                psf(ns)[0:1, :], onec, acc_sb[:, ns * 512:(ns + 1) * 512],
                start=True, stop=True), reads=[B_onec, B_acc], writes=[PSB[ns]])
        p.emit("ve", lambda: nc.vector.tensor_scalar(
            out=inv_row, in0=psf(0, 4)[0:1, 0:2048], scalar1=1.0 / HF,
            scalar2=RMS_EPS, op0=ALU.mult, op1=ALU.add),
            reads=[PSB[0], PSB[1], PSB[2], PSB[3]], writes=[B_inv])
        p.emit("sc", lambda: nc.scalar.activation(
            out=inv_row, in_=inv_row, func=AF.Ln),
            reads=[B_inv], writes=[B_inv])
        p.emit("sc", lambda: nc.scalar.activation(
            out=inv_row, in_=inv_row, func=AF.Exp, scale=-0.5),
            reads=[B_inv], writes=[B_inv])
        for ns in range(4):
            p.emit("pe", lambda ns=ns: nc.tensor.matmul(
                psf(4), ones_row_f[0:1, :], inv_row[:, ns * 512:(ns + 1) * 512],
                start=True, stop=True), reads=[B_ones, B_inv], writes=[PSB[4]])
            p.emit("ve", lambda ns=ns: nc.vector.tensor_copy(
                out=inv_bc[:, ns * 512:(ns + 1) * 512], in_=psf(4)),
                reads=[PSB[4]], writes=[B_invbc])

        if debug:
            p.dma("sp", dbg["dbg_hact"], hact_dram, reads=[B_hact_dram], writes=[Buf("dbg")], sem=None)

        A.release(o_ln3sb)
        for o in o_wgt + o_wut + [o_sg] + o_ha + [o_h2, o_acc, o_onec]:
            A.release(o)

        wpt_, B_wpt, o_wpt = [], [], []
        for i in range(2):
            a, b, o = sba(f"wpt{i}", 64 * 256, BF16, side="high")
            wpt_.append(a); B_wpt.append(b); o_wpt.append(o)
        hact_, B_hactsb, o_hactsb = [], [], []
        for i in range(2):
            a, b, o = sba(f"hact_sb{i}", 16 * 512, BF16, side="high")
            hact_.append(a); B_hactsb.append(b); o_hactsb.append(o)
        fchunk_, B_fchunk, o_fchunk = [], [], []
        for i in range(2):
            a, b, o = sba(f"fchunk{i}", 512, F32)
            fchunk_.append(a); B_fchunk.append(b); o_fchunk.append(o)
        ftr_, B_ftr, o_ftr = [], [], []
        for i in range(2):
            a, b, o = sba(f"ftr{i}", 512, F32)
            ftr_.append(a); B_ftr.append(b); o_ftr.append(o)
        B_ftok = Buf("ftok_dram")
        hact_v = hact_dram.rearrange("(j p) t -> p j t", p=128)

        pending_fb = []
        for dgp in range(4):
            wptv = []
            for dg2 in range(2):
                dg = dgp * 2 + dg2
                w3 = wpt_[dg2].rearrange("p (j f) -> p j f", f=256)
                wptv.append(w3)
                for j8 in range(4):
                    p.dma("sp", w3[:, j8 * 16:(j8 + 1) * 16, :],
                          wpt_full.rearrange("(j p) d -> p j d", p=128)[
                              :, j8 * 16:(j8 + 1) * 16, dg * 256:(dg + 1) * 256],
                          reads=[B_wp_q], writes=[B_wpt[dg2]], sem=None)
            for tt in range(4):
                b0 = (tt % 2) * 4
                for jhh in range(4):
                    hs = jhh % 2
                    hact3 = hact_[hs].rearrange("p (j f) -> p j f", f=512)
                    p.dma("sp", hact3[:, 0:16, :],
                          hact_v[:, jhh * 16:jhh * 16 + 16,
                                 tt * 512:(tt + 1) * 512],
                          reads=[B_hact_dram], writes=[B_hactsb[hs]], sem=None)
                    for dg2 in range(2):
                        for dc in range(2):
                            for jh in range(16):
                                p.emit("pe", lambda b0=b0, dg2=dg2, dc=dc, jh=jh,
                                       jhh=jhh, hs=hs, w3=wptv[dg2]:
                                       nc.tensor.matmul(
                                    psf(b0 + dg2 * 2 + dc),
                                    w3[:, jhh * 16 + jh, dc * 128:(dc + 1) * 128],
                                    hact3[:, jh, :],
                                    start=(jhh == 0 and jh == 0),
                                    stop=(jhh == 3 and jh == 15)),
                                    reads=[B_wpt[dg2], B_hactsb[hs]],
                                    writes=[PSB[b0 + dg2 * 2 + dc]])
                for fn in pending_fb:
                    fn()
                pending_fb = []

                def make_evac(dgp=dgp, tt=tt, b0=b0):
                    def go():
                        for dg2 in range(2):
                            for dc in range(2):
                                d_abs = (dgp * 2 + dg2) * 2 + dc
                                bkk = b0 + dg2 * 2 + dc
                                s = (dg2 * 2 + dc) % 2
                                p.emit("ve", lambda bkk=bkk, s=s, tt=tt:
                                       nc.vector.tensor_mul(
                                    out=fchunk_[s], in0=psf(bkk),
                                    in1=inv_bc[:, tt * 512:(tt + 1) * 512]),
                                    reads=[PSB[bkk], B_invbc],
                                    writes=[B_fchunk[s]])
                                for c4 in range(4):
                                    p.emit("pe", lambda s=s, bkk=bkk, c4=c4:
                                           nc.tensor.transpose(
                                        psf(bkk)[:, c4 * 128:(c4 + 1) * 128],
                                        fchunk_[s][:, c4 * 128:(c4 + 1) * 128],
                                        id_f32),
                                        reads=[B_fchunk[s], B_idf],
                                        writes=[PSB[bkk]])
                                p.emit("sc", lambda s=s, bkk=bkk: nc.scalar.copy(
                                    out=ftr_[s], in_=psf(bkk)),
                                    reads=[PSB[bkk]], writes=[B_ftr[s]])
                                p.dma("gp", ftok_dram[tt * 512:(tt + 1) * 512,
                                                      d_abs * 128:(d_abs + 1) * 128]
                                      .rearrange("(c t) f -> t c f", t=128),
                                      ftr_[s].rearrange("t (c f) -> t c f", f=128),
                                      reads=[B_ftr[s]], writes=[B_ftok], sem=None)
                    return go
                pending_fb.append(make_evac())
        for fn in pending_fb:
            fn()
        pending_fb = []

        if debug:
            p.dma("sp", dbg["dbg_ffn"], ftok_dram, reads=[B_ftok], writes=[Buf("dbg")], sem=None)

        for o in o_wpt + o_hactsb + o_fchunk + o_ftr + [o_invp, o_invbc]:
            A.release(o)

        fin_, B_fin, o_fin = [], [], []
        s2in_, B_s2in, o_s2in = [], [], []
        outt_, B_outt, o_outt = [], [], []
        for i in range(2):
            a, b, o = sba(f"fin{i}", D, F32)
            fin_.append(a); B_fin.append(b); o_fin.append(o)
            a, b, o = sba(f"s2in{i}", D, F32)
            s2in_.append(a); B_s2in.append(b); o_s2in.append(o)
            a, b, o = sba(f"outt{i}", D, F32)
            outt_.append(a); B_outt.append(b); o_outt.append(o)
        z4_, B_z4s, o_z4s = [], [], []
        for i in range(2):
            a, b, o = sba(f"z4_{i}", D, F32)
            z4_.append(a); B_z4s.append(b); o_z4s.append(o)
        B_out_dram = Buf("out_dram")

        def fcload(tc):
            s = tc % 2
            p.dma("sp", fin_[s], ftok_dram[tc * 128:(tc + 1) * 128, :],
                  reads=[B_ftok], writes=[B_fin[s]], sem=None)
            p.dma("sp", s2in_[s], s2_dram[tc * 128:(tc + 1) * 128, :],
                  reads=[B_s2_dram], writes=[B_s2in[s]], sem=None)
        fcload(0)
        for tc in range(16):
            s = tc % 2
            if tc + 1 < 16:
                fcload(tc + 1)
            ln_token_major(B_fin[s], fin_[s], z4_[s], B_z4s[s], LN_EPS, sl=s)
            p.emit("ve", lambda s=s: nc.vector.tensor_add(
                out=outt_[s], in0=z4_[s], in1=s2in_[s]),
                reads=[B_z4s[s], B_s2in[s]], writes=[B_outt[s]])
            p.dma("sp", out_d[tc * 128:(tc + 1) * 128, :], outt_[s],
                  reads=[B_outt[s]], writes=[B_out_dram], sem=None)

        p.eng["sp"].wait_ge(p.sem[B_out_dram.dsem], p.cnt[B_out_dram.dsem])
        nc.all_engine_barrier()

    return nc


_PROG_CACHE = {}


def kernel(L0_attn_x, L0_attn_Wq, L0_attn_bq, L0_attn_Wl, L0_attn_bl, L0_attn_Wk,
           L0_attn_bk, L0_attn_Wv, L0_attn_bv, L0_attn_Wq2kv, L0_attn_Wkv2h,
           L0_attn_bkv2h, L0_attn_Wo, L0_attn_bo, L0_ffn_w_gate_up, L0_ffn_w_proj):
    x = np.ascontiguousarray(np.asarray(L0_attn_x).astype(np.float32, copy=False))
    w = {
        "Wq": np.ascontiguousarray(np.asarray(L0_attn_Wq).astype(np.float32, copy=False)),
        "bq": np.ascontiguousarray(np.asarray(L0_attn_bq).astype(np.float32, copy=False)),
        "Wl": np.ascontiguousarray(np.asarray(L0_attn_Wl).astype(np.float32, copy=False)),
        "bl": np.ascontiguousarray(np.asarray(L0_attn_bl).astype(np.float32, copy=False)),
        "Wk": np.ascontiguousarray(np.asarray(L0_attn_Wk).astype(np.float32, copy=False)),
        "bk": np.ascontiguousarray(np.asarray(L0_attn_bk).astype(np.float32, copy=False)),
        "Wv": np.ascontiguousarray(np.asarray(L0_attn_Wv).astype(np.float32, copy=False)),
        "bv": np.ascontiguousarray(np.asarray(L0_attn_bv).astype(np.float32, copy=False)),
        "Wq2kv": np.ascontiguousarray(np.asarray(L0_attn_Wq2kv).astype(np.float32, copy=False)),
        "Wkv2h": np.ascontiguousarray(np.asarray(L0_attn_Wkv2h).astype(np.float32, copy=False)),
        "bkv2h": np.ascontiguousarray(np.asarray(L0_attn_bkv2h).astype(np.float32, copy=False)),
        "Wo": np.ascontiguousarray(np.asarray(L0_attn_Wo).astype(np.float32, copy=False)),
        "bo": np.ascontiguousarray(np.asarray(L0_attn_bo).astype(np.float32, copy=False)),
    }
    wgu = np.asarray(L0_ffn_w_gate_up).astype(np.float32, copy=False)
    wp = np.asarray(L0_ffn_w_proj).astype(np.float32, copy=False)
    if "nc" not in _PROG_CACHE:
        _PROG_CACHE["nc"] = build_program(debug=False)
    nc = _PROG_CACHE["nc"]
    gsh = 2 * HF // NCORES
    psh = HF // NCORES
    in_maps = [dict(w, x=np.ascontiguousarray(x[c]),
                    Wgu_sh=np.ascontiguousarray(wgu[c * gsh:(c + 1) * gsh]),
                    Wp_sh=np.ascontiguousarray(wp[:, c * psh:(c + 1) * psh]))
               for c in range(NCORES)]
    res = run_bass_kernel_spmd(nc, in_maps, core_ids=list(range(NCORES)))
    out = np.stack([res.results[c]["out"] for c in range(NCORES)], axis=0)
    return out.astype(np.float32)
